# version 02
# baseline (speedup 1.0000x reference)
import numpy as np
from contextlib import ExitStack

import concourse.bass as bass
import concourse.mybir as mybir
from concourse.bass_utils import run_bass_kernel_spmd

F32 = mybir.dt.float32
BF16 = mybir.dt.bfloat16
AF = mybir.ActivationFunctionType
ALU = mybir.AluOpType
AX = mybir.AxisListType

T = 4096
D = 1024
NT = T // 128
NDC = D // 128
NKT = T // 512


class Ctx:

    def __init__(self, nc, es):
        self.nc = nc
        self.es = es
        self._sems = {}
        self.n = {}

    def uid(self):
        self._uid = getattr(self, "_uid", 0) + 1
        return self._uid

    def sem(self, name):
        if name not in self._sems:
            self._sems[name] = self.es.enter_context(self.nc.semaphore(name))
            self.n[name] = 0
        return self._sems[name]

    def inc(self, instr, name, step=1):
        instr.then_inc(self.sem(name), step)
        self.n[name] += step
        return self.n[name]

    def dinc(self, instr, name):
        return self.inc(instr, name, 16)

    def w(self, eng, name, val=None):
        v = self.n.setdefault(name, 0) if val is None else val
        self.sem(name)
        if v > 0:
            eng.wait_ge(self._sems[name], v)

    def barrier(self):
        nc = self.nc
        with nc.Block() as blk:
            @blk.gpsimd
            def _(gp):
                snap = dict(self.n)
                for eng in [nc.tensor, nc.vector, nc.scalar, nc.gpsimd, nc.sync]:
                    for name, val in snap.items():
                        if val > 0:
                            eng.wait_ge(self._sems[name], val)



def emit_ln_tile(ctx, xs_ap, xln_ap, junk_ap, stat):
    nc = ctx.nc
    v, s = nc.vector, nc.scalar
    i_sq = ctx.inc(s.activation(junk_ap, xs_ap, AF.Square, accum_out=stat["Q"]), "se")
    ctx.inc(v.reduce_sum(stat["S"], xs_ap, axis=AX.X), "ve")
    ctx.w(v, "ve")
    ctx.inc(v.tensor_scalar(stat["negmean"], stat["S"], -1.0 / D, None, ALU.mult), "ve")
    ctx.w(v, "ve")
    ctx.w(v, "se", i_sq)
    ctx.inc(v.tensor_scalar(stat["varr"], stat["Q"], 1.0 / D, 1e-5, ALU.mult, ALU.add), "ve")
    ctx.w(v, "ve")
    ctx.inc(v.tensor_tensor(stat["mm"], stat["negmean"], stat["negmean"], ALU.mult), "ve")
    ctx.w(v, "ve")
    ctx.inc(v.tensor_tensor(stat["varr"], stat["varr"], stat["mm"], ALU.subtract), "ve")
    i_var = ctx.n["ve"]
    ctx.w(s, "ve", i_var)
    i_sqrt = ctx.inc(s.activation(stat["rstd"], stat["varr"], AF.Sqrt), "se")
    ctx.w(v, "se", i_sqrt)
    ctx.inc(v.reciprocal(stat["rstd"], stat["rstd"]), "ve")
    ctx.w(v, "ve")
    ctx.inc(v.tensor_scalar(xln_ap, xs_ap, stat["negmean"], stat["rstd"],
                            ALU.add, ALU.mult), "ve")
    return ctx.n["ve"]


def alloc_stats(nc, es, prefix, nslots=2):
    out = []
    for sl in range(nslots):
        d = {}
        for nm in ["S", "Q", "negmean", "varr", "mm", "rstd"]:
            t = es.enter_context(nc.sbuf_tensor(f"{prefix}_{nm}{sl}", [128, 1], F32))
            d[nm] = t[:]
        out.append(d)
    return out



def emit_l0a(ctx, xin, xout, W):
    nc = ctx.nc
    te, v, s, g, sy = nc.tensor, nc.vector, nc.scalar, nc.gpsimd, nc.sync
    pfx = f"a{ctx.uid()}_"
    with ExitStack() as es:
        def sb(name, shape, dt):
            return es.enter_context(nc.sbuf_tensor(pfx + name, shape, dt))

        qtw = sb("qtw", [128, NDC, 64], BF16)
        ktw = sb("ktw", [128, NDC, 64], BF16)
        vow = sb("vow", [128, NDC, D], BF16)
        xqT = sb("xqT", [64, T], BF16)
        xkT = sb("xkT", [64, T], BF16)
        xv = sb("xv", [128, NT, D], BF16)
        xs = sb("xs", [128, 2, D], F32)
        xln = sb("xln", [128, 2, D], BF16)
        xlnT = sb("xlnT", [128, NDC, 512], BF16)
        junk = sb("junk", [128, D], BF16)
        stats = alloc_stats(nc, es, pfx, 2)
        s_sb = sb("s_sb", [128, T], F32)
        p_sb = sb("p_sb", [128, T], BF16)
        pt_sb = sb("pt", [128, NT, 128], BF16)
        mcol = sb("mcol", [128, 1], F32)
        negm = sb("negm", [128, 1], F32)
        lcol = sb("lcol", [128, 1], F32)
        rlcol = sb("rlcol", [128, 1], F32)
        xold = sb("xold", [128, 2, D], F32)
        onew = sb("onew", [128, 2, D], F32)
        ps_q = es.enter_context(nc.psum_tensor(pfx + "psq", [64, 512], F32))
        ps_k = es.enter_context(nc.psum_tensor(pfx + "psk", [64, 512], F32))
        ps_v = [es.enter_context(nc.psum_tensor(f"{pfx}psv{i}", [128, 512], F32))
                for i in range(2)]
        ps_s = [es.enter_context(nc.psum_tensor(f"{pfx}pss{i}", [128, 512], F32))
                for i in range(2)]
        ps_o = es.enter_context(nc.psum_tensor(pfx + "pso", [128, 1024], F32))

        with nc.Block() as blk:
            @blk.gpsimd
            def _(gp):
                ctx.dinc(g.dma_start(qtw[:], W["l0a_QT"].rearrange("(c p) r -> p c r", p=128)), "dg")
                ctx.dinc(g.dma_start(ktw[:], W["l0a_KT"].rearrange("(c p) r -> p c r", p=128)), "dg")
                ctx.dinc(g.dma_start(vow[:], W["l0a_VO"].rearrange("(c p) n -> p c n", p=128)), "dg")
                w_ready = ctx.n["dg"]

                ev_norm = [0] * NT
                ev_tp = [0] * NT
                ev_gproj = [0] * (NT // 4)
                ev_vevac = [0] * NT
                ev_vbank = [0, 0]
                ev_qkevac = 0
                vuse = 0

                for gi in range(NT // 4):
                    for ii in range(4):
                        i = gi * 4 + ii
                        sl = i % 2
                        if i >= 2:
                            ctx.w(sy, "ve", ev_norm[i - 2])
                        ctx.dinc(sy.dma_start(xs[:, sl, :],
                                              xin[i * 128:(i + 1) * 128, :]), f"sx{sl}")
                        ctx.w(v, f"sx{sl}")
                        ctx.w(s, f"sx{sl}")
                        if i >= 2:
                            ctx.w(v, f"tp{sl}", ev_tp[i - 2])
                        ev_norm[i] = emit_ln_tile(ctx, xs[:, sl, :], xln[:, sl, :],
                                                  junk[:], stats[sl])
                        ctx.w(sy, "ve", ev_norm[i])
                        if gi >= 1 and ii == 0:
                            ctx.w(sy, "pe", ev_gproj[gi - 1])
                        ctx.dinc(sy.dma_start(xlnT[:, :, ii * 128:(ii + 1) * 128],
                                              xln[:, sl, :], transpose=True), f"tp{sl}")
                        ev_tp[i] = ctx.n[f"tp{sl}"]

                    ctx.w(te, "tp0", ev_tp[gi * 4 + 2])
                    ctx.w(te, "tp1", ev_tp[gi * 4 + 3])
                    if gi == 0:
                        ctx.w(te, "dg", w_ready)
                    if gi >= 1:
                        ctx.w(te, "se", ev_qkevac)
                    for dc in range(NDC):
                        mm = te.matmul(ps_q[:], qtw[:, dc, :], xlnT[:, dc, :],
                                       start=(dc == 0), stop=(dc == NDC - 1))
                    i_q = ctx.inc(mm, "pe")
                    for dc in range(NDC):
                        mm = te.matmul(ps_k[:], ktw[:, dc, :], xlnT[:, dc, :],
                                       start=(dc == 0), stop=(dc == NDC - 1))
                    i_k = ctx.inc(mm, "pe")
                    for ii in range(4):
                        i = gi * 4 + ii
                        for nh in range(2):
                            bank = vuse % 2
                            vuse += 1
                            if ev_vbank[bank]:
                                ctx.w(te, "ve", ev_vbank[bank])
                            for dc in range(NDC):
                                mm = te.matmul(ps_v[bank][:],
                                               xlnT[:, dc, ii * 128:(ii + 1) * 128],
                                               vow[:, dc, nh * 512:(nh + 1) * 512],
                                               start=(dc == 0), stop=(dc == NDC - 1))
                            i_v = ctx.inc(mm, "pe")
                            ctx.w(v, "pe", i_v)
                            ctx.inc(v.tensor_copy(xv[:, i, nh * 512:(nh + 1) * 512],
                                                  ps_v[bank][:]), "ve")
                            ev_vbank[bank] = ctx.n["ve"]
                            ev_vevac[i] = ctx.n["ve"]
                    ev_gproj[gi] = ctx.n["pe"]
                    ctx.w(s, "pe", i_q)
                    ctx.inc(s.copy(xqT[:, gi * 512:(gi + 1) * 512], ps_q[:]), "se")
                    ctx.w(s, "pe", i_k)
                    ctx.inc(s.copy(xkT[:, gi * 512:(gi + 1) * 512], ps_k[:]), "se")
                    ev_qkevac = ctx.n["se"]

                ev_rmax = 0
                ev_exp = 0
                ev_pt = 0
                ev_stt = 0
                ev_recip = 0
                ev_pv = 0
                ev_ssbank = [0, 0]
                ev_stt_slot = [0, 0]
                ev_so_slot = [0, 0]
                for qt in range(NT):
                    sl = qt % 2
                    for kt in range(NKT):
                        bank = kt % 2
                        if ev_ssbank[bank]:
                            ctx.w(te, "se", ev_ssbank[bank])
                        if qt == 0 and kt == 0:
                            ctx.w(te, "se", ev_qkevac)
                        mm = te.matmul(ps_s[bank][:], xqT[:, qt * 128:(qt + 1) * 128],
                                       xkT[:, kt * 512:(kt + 1) * 512],
                                       start=True, stop=True)
                        i_s = ctx.inc(mm, "pe")
                        ctx.w(s, "pe", i_s)
                        if qt >= 1 and kt == 0:
                            ctx.w(s, "ve", ev_rmax)
                        ctx.inc(s.activation(s_sb[:, kt * 512:(kt + 1) * 512],
                                             ps_s[bank][:], AF.Copy, scale=1.0 / 32.0),
                                "se")
                        ev_ssbank[bank] = ctx.n["se"]
                    ev_sevac = ctx.n["se"]
                    ctx.w(v, "se", ev_sevac)
                    ctx.inc(v.reduce_max(mcol[:], s_sb[:], axis=AX.X), "ve")
                    ctx.w(v, "ve")
                    if qt >= 1:
                        ctx.w(v, "se", ev_exp)
                    ctx.inc(v.tensor_scalar(negm[:], mcol[:], -1.0, None, ALU.mult), "ve")
                    ev_rmax = ctx.n["ve"]
                    ctx.w(s, "ve", ev_rmax)
                    if qt >= 1:
                        ctx.w(s, "pt0", ev_pt)
                        ctx.w(s, "ve", ev_recip)
                    ev_exp = ctx.inc(s.activation(p_sb[:], s_sb[:], AF.Exp, bias=negm[:],
                                                  accum_out=lcol[:]), "se")
                    ctx.w(v, "se", ev_exp)
                    ctx.inc(v.reciprocal(rlcol[:], lcol[:]), "ve")
                    ev_recip = ctx.n["ve"]
                    ctx.w(sy, "se", ev_exp)
                    if qt >= 1:
                        ctx.w(sy, "pe", ev_pv)
                    ctx.dinc(sy.dma_start(pt_sb[:], p_sb[:], transpose=True), "pt0")
                    ev_pt = ctx.n["pt0"]
                    ctx.w(te, "pt0", ev_pt)
                    if qt >= 1:
                        ctx.w(te, "ve", ev_stt)
                    if qt == 0:
                        ctx.w(te, "ve", ev_vevac[NT - 1])
                    for nh in range(2):
                        for kc in range(NT):
                            mm = te.matmul(ps_o[:, nh * 512:(nh + 1) * 512],
                                           pt_sb[:, kc, :],
                                           xv[:, kc, nh * 512:(nh + 1) * 512],
                                           start=(kc == 0), stop=(kc == NT - 1))
                        ctx.inc(mm, "pe")
                    ev_pv = ctx.n["pe"]
                    if qt >= 2:
                        ctx.w(sy, "ve", ev_stt_slot[sl])
                    ctx.dinc(sy.dma_start(xold[:, sl, :],
                                          xin[qt * 128:(qt + 1) * 128, :]), f"xo{sl}")
                    ctx.w(v, "pe", ev_pv)
                    ctx.w(v, f"xo{sl}")
                    ctx.w(v, "ve", ev_recip)
                    if qt >= 2:
                        ctx.w(v, f"so{sl}", ev_so_slot[sl])
                    for nh in range(2):
                        ins = v.scalar_tensor_tensor(
                            onew[:, sl, nh * 512:(nh + 1) * 512],
                            ps_o[:, nh * 512:(nh + 1) * 512], rlcol[:],
                            xold[:, sl, nh * 512:(nh + 1) * 512], ALU.mult, ALU.add)
                    ctx.inc(ins, "ve")
                    ev_stt = ctx.n["ve"]
                    ev_stt_slot[sl] = ev_stt
                    ctx.w(sy, "ve", ev_stt)
                    ctx.dinc(sy.dma_start(xout[qt * 128:(qt + 1) * 128, :],
                                          onew[:, sl, :]), f"so{sl}")
                    ev_so_slot[sl] = ctx.n[f"so{sl}"]
    ctx.barrier()




def emit_consts(ctx):
    nc = ctx.nc
    es = ctx.es
    C = {}
    C["ones16"] = es.enter_context(nc.sbuf_tensor("c_ones16", [1, 128], BF16))
    C["ones32"] = es.enter_context(nc.sbuf_tensor("c_ones32", [1, 128], F32))
    C["ident32"] = es.enter_context(nc.sbuf_tensor("c_id32", [128, 128], F32))
    C["ident16"] = es.enter_context(nc.sbuf_tensor("c_id16", [128, 128], BF16))
    C["w8"] = es.enter_context(nc.sbuf_tensor("c_w8", [128, 8], F32))
    ctx.consts = C
    g = nc.gpsimd
    with nc.Block() as blk:
        @blk.gpsimd
        def _(gp):
            ctx.inc(g.memset(C["ones16"][:], 1.0), "po")
            ctx.inc(g.memset(C["ones32"][:], 1.0), "po")
            ctx.inc(g.memset(C["ident32"][:], 0.0), "po")
            ctx.w(g, "po")
            ctx.inc(g.affine_select(C["ident32"][:], C["ident32"][:], [[-1, 128]],
                                    ALU.not_equal, 1.0, base=0, channel_multiplier=1), "po")
            ctx.inc(g.memset(C["ident16"][:], 0.0), "po")
            ctx.w(g, "po")
            ctx.inc(g.affine_select(C["ident16"][:], C["ident16"][:], [[-1, 128]],
                                    ALU.not_equal, 1.0, base=0, channel_multiplier=1), "po")
            ctx.inc(g.iota(C["w8"][:], [[-1, 8]], base=8, channel_multiplier=0,
                           allow_small_or_imprecise_dtypes=True), "po")
    ctx.barrier()


def alloc_topk(nc, es, prefix):
    B = {}
    for nm, shape in [("m", [128, 1]), ("negm", [128, 1]), ("l", [128, 1]),
                      ("rl", [128, 1]), ("s8", [128, 1]), ("m1", [128, 1]),
                      ("m2", [128, 1]), ("e8", [128, 8]), ("pr", [128, 8]),
                      ("eq", [128, 8]), ("t8", [128, 8]), ("raw", [128, 8]),
                      ("lg2", [128, 8])]:
        t = es.enter_context(nc.sbuf_tensor(f"{prefix}_tk_{nm}", shape, F32))
        B[nm] = t[:]
    return B


def emit_probs(ctx, lg_ap, B):
    nc = ctx.nc
    v, s = nc.vector, nc.scalar
    ctx.w(v, "se")
    ctx.w(s, "ve")
    ctx.inc(v.reduce_max(B["m"], lg_ap, axis=AX.X), "ve")
    ctx.w(v, "ve")
    ctx.inc(v.tensor_scalar(B["negm"], B["m"], -1.0, None, ALU.mult), "ve")
    ctx.w(s, "ve")
    ctx.inc(s.activation(B["e8"], lg_ap, AF.Exp, bias=B["negm"], accum_out=B["l"]), "se")
    ctx.w(v, "se")
    ctx.inc(v.reciprocal(B["rl"], B["l"]), "ve")
    ctx.w(v, "ve")
    ctx.inc(v.tensor_scalar(B["pr"], B["e8"], B["rl"], None, ALU.mult), "ve")
    ctx.w(v, "ve")
    return ctx.n["ve"]


def emit_argmax_mask(ctx, src_ap, mask_ap, B, save_max=None):
    nc = ctx.nc
    v, s = nc.vector, nc.scalar
    w8 = ctx.consts["w8"][:]
    ctx.inc(v.reduce_max(B["m"], src_ap, axis=AX.X), "ve")
    ctx.w(v, "ve")
    if save_max is not None:
        ctx.inc(v.tensor_copy(save_max, B["m"]), "ve")
    ctx.inc(v.tensor_scalar(B["negm"], B["m"], -1.0, None, ALU.mult), "ve")
    ctx.w(s, "ve")
    ctx.inc(s.activation(B["eq"], src_ap, AF.Sign, bias=B["negm"]), "se")
    ctx.w(v, "se")
    ctx.inc(v.tensor_scalar(B["eq"], B["eq"], 1.0, None, ALU.add), "ve")
    ctx.w(v, "ve")
    ctx.inc(v.tensor_tensor(B["t8"], B["eq"], w8, ALU.mult), "ve")
    ctx.w(v, "ve")
    ctx.inc(v.reduce_max(B["s8"], B["t8"], axis=AX.X), "ve")
    ctx.w(v, "ve")
    ctx.inc(v.tensor_scalar(B["s8"], B["s8"], -1.0, None, ALU.mult), "ve")
    ctx.w(s, "ve")
    ctx.inc(s.activation(B["raw"], B["t8"], AF.Sign, bias=B["s8"]), "se")
    ctx.w(v, "se")
    ctx.inc(v.tensor_scalar(mask_ap, B["raw"], 1.0, None, ALU.add), "ve")
    ctx.w(v, "ve")
    return ctx.n["ve"]



GRP = 256


def emit_l0ffn(ctx, xin, xout, W, dbg=None):
    nc = ctx.nc
    te, v, s, g, sy = nc.tensor, nc.vector, nc.scalar, nc.gpsimd, nc.sync
    pfx = f"f{ctx.uid()}_"
    NG = T // GRP
    TPG = GRP // 128
    with ExitStack() as es:
        def sb(name, shape, dt):
            return es.enter_context(nc.sbuf_tensor(pfx + name, shape, dt))

        wiw = sb("wiw", [128, NDC, 4096], BF16)
        wow = sb("wow", [128, 32, 1024], BF16)
        gtw = sb("gtw", [128, NDC, 8], F32)
        gbrow = sb("gbrow", [1, 8], F32)
        atw = sb("atw", [128, NDC, 32], BF16)
        btw = sb("btw", [32, 4096], BF16)
        wib = sb("wib", [128, 32], F32)
        wob = sb("wob", [1, 1024], BF16)
        xs = sb("xs", [128, 2, D], F32)
        xln32 = sb("xln32", [128, 2, D], F32)
        xln16 = sb("xln16", [128, 2, D], BF16)
        xlnT = sb("xlnT", [128, NDC, GRP], BF16)
        xlnT32 = sb("xlnT32", [128, NDC, 128], F32)
        junk = sb("junk", [128, D], BF16)
        stats = alloc_stats(nc, es, pfx, 2)
        tkB = alloc_topk(nc, es, pfx)
        lg = sb("lg", [128, 8], F32)
        mask = sb("mask", [128, 2, 8], F32)
        u_m = sb("u_m", [128, 2, 32], BF16)
        utT = sb("utT", [32, GRP], BF16)
        h_sb = sb("h", [128, 32, GRP], BF16)
        xold = sb("xold", [128, 1, D], F32)
        onew = sb("onew", [128, 2, D], F32)
        ps_g = es.enter_context(nc.psum_tensor(pfx + "psg", [128, 8], F32))
        ps_u = es.enter_context(nc.psum_tensor(pfx + "psu", [128, 32], F32))
        ps_t = [es.enter_context(nc.psum_tensor(pfx + "pst0", [128, 128], F32))]
        ps_ut = es.enter_context(nc.psum_tensor(pfx + "psut", [32, 128], BF16))
        ps_h = [es.enter_context(nc.psum_tensor(f"{pfx}psh{i}", [128, GRP], F32))
                for i in range(2)]
        ps_o = es.enter_context(nc.psum_tensor(pfx + "pso", [128, 1024], F32))

        ones16 = ctx.consts["ones16"][:]
        ones32 = ctx.consts["ones32"][:]
        id32 = ctx.consts["ident32"][:]
        id16 = ctx.consts["ident16"][:]

        with nc.Block() as blk:
            @blk.gpsimd
            def _(gp):
                ctx.dinc(g.dma_start(wiw[:], W["l0f_wiT"].rearrange("(c p) f -> p c f", p=128)), "dg")
                ctx.dinc(g.dma_start(wow[:], W["l0f_woT"].rearrange("(c p) f -> p c f", p=128)), "dg")
                ctx.dinc(g.dma_start(gtw[:], W["l0f_gateT"].rearrange("(c p) e -> p c e", p=128)), "dg")
                ctx.dinc(g.dma_start(gbrow[:], W["l0f_gate_b"]), "dg")
                ctx.dinc(g.dma_start(atw[:], W["l0f_AallT"].rearrange("(c p) r -> p c r", p=128)), "dg")
                ctx.dinc(g.dma_start(btw[:], W["l0f_BallT"]), "dg")
                ctx.dinc(g.dma_start(wib[:], W["l0f_wib_cm"]), "dg")
                ctx.dinc(g.dma_start(wob[:], W["l0f_wo_b"]), "dg")
                w_ready = ctx.n["dg"]

                ev_norm = [0] * NT
                ev_tp = [0] * NT
                ev_t32 = [0] * NT
                ev_cast = [0, 0]
                ev_gmm = 0
                ev_lg_free = 0
                ev_wo = 0
                ev_gbase = 0
                ev_g32 = 0
                ev_hbank = [0, 0]
                ev_tbank = [0, 0]
                ev_stt_slot = [0, 0]
                ev_so_slot = [0, 0]
                ev_stt = 0
                ev_g_evac = 0
                ev_um = [0, 0]
                ev_ut = 0
                ev_mask = [0, 0]
                first = True

                for gi in range(NG):
                    for ii in range(TPG):
                        i = gi * TPG + ii
                        sl = i % 2
                        if i >= 2:
                            ctx.w(sy, "ve", ev_norm[i - 2])
                        ctx.dinc(sy.dma_start(xs[:, sl, :],
                                              xin[i * 128:(i + 1) * 128, :]), f"sx{sl}")
                        ctx.w(v, f"sx{sl}")
                        ctx.w(s, f"sx{sl}")
                        if i >= 2:
                            ctx.w(v, "se", ev_cast[sl])
                            ctx.w(v, "ve", ev_t32[i - 2])
                            ctx.w(v, f"tp{sl}", ev_tp[i - 2])
                        ev_norm[i] = emit_ln_tile(ctx, xs[:, sl, :], xln32[:, sl, :],
                                                  junk[:], stats[sl])
                        ctx.w(s, "ve", ev_norm[i])
                        ev_cast[sl] = ctx.inc(s.copy(xln16[:, sl, :], xln32[:, sl, :]), "se")
                        ctx.w(sy, "se", ev_cast[sl])
                        if gi >= 1 and ii == 0:
                            ctx.w(sy, "pe", ev_gbase)
                        ctx.dinc(sy.dma_start(xlnT[:, :, ii * 128:(ii + 1) * 128],
                                              xln16[:, sl, :], transpose=True), f"tp{sl}")
                        ev_tp[i] = ctx.n[f"tp{sl}"]
                        if first:
                            ctx.w(te, "dg", w_ready)
                            ctx.w(te, "po", ctx.n["po"])
                            first = False
                        ctx.w(te, "ve", ev_norm[i])
                        for dc in range(NDC):
                            bank = 0
                            if ev_tbank[bank]:
                                ctx.w(te, "ve", ev_tbank[bank])
                            i_t = ctx.inc(te.transpose(ps_t[bank][:],
                                                       xln32[:, sl, dc * 128:(dc + 1) * 128],
                                                       id32), "pe")
                            ctx.w(v, "pe", i_t)
                            if ev_gmm:
                                ctx.w(v, "pe", ev_gmm)
                            ctx.inc(v.tensor_copy(xlnT32[:, dc, :],
                                                  ps_t[bank][:]), "ve")
                            ev_tbank[bank] = ctx.n["ve"]
                        ev_t32[i] = ctx.n["ve"]
                        if dbg is not None and len(dbg) > 3:
                            ctx.w(sy, "ve", ev_t32[i])
                            ctx.dinc(sy.dma_start(dbg[3][i * 128:(i + 1) * 128, :],
                                                  xlnT32[:, 0, :]), "sdbg")
                            ctx.w(sy, "pe", ctx.n["pe"])

                        ctx.w(te, "ve", ev_t32[i])
                        if ev_g_evac:
                            ctx.w(te, "ve", ev_g_evac)
                        mm = te.matmul(ps_g[:], ones32, gbrow[:], start=True, stop=False)
                        for dc in range(NDC):
                            mm = te.matmul(ps_g[:], xlnT32[:, dc, :],
                                           gtw[:, dc, :], start=False, stop=(dc == NDC - 1))
                        i_g = ctx.inc(mm, "pe")
                        ev_gmm = i_g
                        ctx.w(v, "pe", i_g)
                        if ev_lg_free:
                            ctx.w(v, "se", ev_lg_free)
                        ctx.inc(v.tensor_copy(lg[:], ps_g[:]), "ve")
                        ev_g_evac = ctx.n["ve"]
                        ctx.w(v, "ve")
                        emit_probs(ctx, lg[:], tkB)
                        ev_lg_free = ctx.n["se"]
                        if dbg is not None:
                            ctx.w(sy, "ve", ctx.n["ve"])
                            ctx.dinc(sy.dma_start(dbg[0][i * 128:(i + 1) * 128, :], lg[:]), "sdbg")
                            ctx.dinc(sy.dma_start(dbg[1][i * 128:(i + 1) * 128, :], tkB["pr"]), "sdbg")
                        ev_mask[sl] = emit_argmax_mask(ctx, tkB["pr"], mask[:, sl, :], tkB)
                        if dbg is not None:
                            ctx.w(sy, "ve", ev_mask[sl])
                            ctx.dinc(sy.dma_start(dbg[2][i * 128:(i + 1) * 128, :], mask[:, sl, :]), "sdbg")

                        ctx.w(te, f"tp{sl}", ev_tp[i])
                        if max(ev_um):
                            ctx.w(te, "ve", max(ev_um))
                        for dc in range(NDC):
                            mm = te.matmul(ps_u[:], xlnT[:, dc, ii * 128:(ii + 1) * 128],
                                           atw[:, dc, :], start=(dc == 0), stop=(dc == NDC - 1))
                        i_u = ctx.inc(mm, "pe")
                        ctx.w(v, "pe", i_u)
                        ctx.w(v, "ve", ev_mask[sl])
                        mask_b4 = mask[:, sl, :].rearrange("p (e one) -> p e one", one=1)                                                .broadcast_to([128, 8, 4])
                        ctx.inc(v.tensor_tensor(u_m[:, sl, :].rearrange("p (e r) -> p e r", e=8),
                                                ps_u[:].rearrange("p (e r) -> p e r", e=8),
                                                mask_b4, ALU.mult), "ve")
                        ev_um[sl] = ctx.n["ve"]
                        ctx.w(te, "ve", ev_um[sl])
                        if ev_ut:
                            ctx.w(te, "ve", ev_ut)
                        i_ut = ctx.inc(te.transpose(ps_ut[:], u_m[:, sl, :], id16), "pe")
                        ctx.w(v, "pe", i_ut)
                        ctx.inc(v.tensor_copy(utT[:, ii * 128:(ii + 1) * 128], ps_ut[:]), "ve")
                        ev_ut = ctx.n["ve"]

                    ctx.w(te, f"tp0", ev_tp[gi * TPG + (TPG - 2)])
                    ctx.w(te, f"tp1", ev_tp[gi * TPG + (TPG - 1)])
                    ctx.w(te, "ve", ev_ut)
                    for fc in range(32):
                        bank = fc % 2
                        if ev_hbank[bank]:
                            ctx.w(te, "se", ev_hbank[bank])
                        if fc == 0 and ev_wo:
                            ctx.w(s, "pe", ev_wo)
                        for dc in range(NDC):
                            mm = te.matmul(ps_h[bank][:], wiw[:, dc, fc * 128:(fc + 1) * 128],
                                           xlnT[:, dc, :], start=(dc == 0), stop=False)
                        mm = te.matmul(ps_h[bank][:], btw[:, fc * 128:(fc + 1) * 128],
                                       utT[:], start=False, stop=True)
                        i_h = ctx.inc(mm, "pe")
                        ctx.w(s, "pe", i_h)
                        ctx.inc(s.activation(h_sb[:, fc, :], ps_h[bank][:], AF.Relu,
                                             bias=wib[:, fc:fc + 1]), "se")
                        ev_hbank[bank] = ctx.n["se"]
                    ev_gbase = ctx.n["pe"]
                    ev_h_all = ctx.n["se"]
                    ev_g32 = ctx.n["pe"]

                    for tt in range(TPG):
                        i = gi * TPG + tt
                        sl = i % 2
                        ctx.w(te, "se", ev_h_all)
                        if ev_stt:
                            ctx.w(te, "ve", ev_stt)
                        for nh in range(2):
                            mm = te.matmul(ps_o[:, nh * 512:(nh + 1) * 512], ones16,
                                           wob[0:1, nh * 512:(nh + 1) * 512],
                                           start=True, stop=False)
                            for fc in range(32):
                                mm = te.matmul(ps_o[:, nh * 512:(nh + 1) * 512],
                                               h_sb[:, fc, tt * 128:(tt + 1) * 128],
                                               wow[:, fc, nh * 512:(nh + 1) * 512],
                                               start=False, stop=(fc == 31))
                        i_o = ctx.inc(mm, "pe")
                        if ev_stt:
                            ctx.w(sy, "ve", ev_stt)
                        ctx.dinc(sy.dma_start(xold[:, 0, :],
                                              xin[i * 128:(i + 1) * 128, :]), "xo0")
                        ctx.w(v, "pe", i_o)
                        ctx.w(v, "xo0")
                        if ev_so_slot[sl]:
                            ctx.w(v, f"so{sl}", ev_so_slot[sl])
                        ins = v.scalar_tensor_tensor(onew[:, sl, :], ps_o[:], 1.0,
                                                     xold[:, 0, :], ALU.mult, ALU.add)
                        ctx.inc(ins, "ve")
                        ev_stt = ctx.n["ve"]
                        ev_stt_slot[sl] = ev_stt
                        ctx.w(sy, "ve", ev_stt)
                        ctx.dinc(sy.dma_start(xout[i * 128:(i + 1) * 128, :],
                                              onew[:, sl, :]), f"so{sl}")
                        ev_so_slot[sl] = ctx.n[f"so{sl}"]
                    ev_wo = ctx.n["pe"]
    ctx.barrier()





def emit_l0b(ctx, xin, xout, W):
    nc = ctx.nc
    te, v, s, g, sy = nc.tensor, nc.vector, nc.scalar, nc.gpsimd, nc.sync
    uid = ctx.uid()
    pfx = f"b{uid}_"
    qsp = nc.dram_tensor(f"qsp{uid}", [D, T], BF16, kind="Internal").ap()
    ksp = nc.dram_tensor(f"ksp{uid}", [D, T], BF16, kind="Internal").ap()
    vsp = nc.dram_tensor(f"vsp{uid}", [T, D], BF16, kind="Internal").ap()
    XX = (2.0 ** 8) ** (1.0 / 16)
    slopes = [1.0 / XX ** (i + 1) for i in range(16)]
    with ExitStack() as eso:
        def sbo(name, shape, dt):
            return es.enter_context(nc.sbuf_tensor(pfx + name, shape, dt))
        es = eso
        o_full = sbo("ofull", [128, NT, D], BF16)
        rel0 = sbo("rel0", [128, T], F32)
        wo_w = sbo("wo", [128, NDC, D], BF16)
        borow = sbo("bo", [1, D], BF16)
        ones16 = ctx.consts["ones16"][:]

        with nc.Block() as blk:
            @blk.gpsimd
            def _(gp):
                ctx.dinc(g.dma_start(wo_w[:], W["l0b_wo"].rearrange("(c p) f -> p c f", p=128)), "dg")
                ctx.dinc(g.dma_start(borow[:], W["l0b_bo"]), "dg")
                ctx.inc(g.iota(rel0[:], [[1, T]], base=0, channel_multiplier=-1,
                               allow_small_or_imprecise_dtypes=True), "po")

        with ExitStack() as esA:
            es = esA
            qkw = sbo("qkw", [128, NDC, 2048], BF16)
            wv = sbo("wv", [128, NDC, D], BF16)
            bqk = sbo("bqk", [128, 16], F32)
            bvrow = sbo("bv", [1, D], BF16)
            xs = sbo("xs", [128, 2, D], F32)
            xln = sbo("xln", [128, 2, D], BF16)
            xlnT = sbo("xlnT", [128, NDC, 512], BF16)
            junk = sbo("junk", [128, D], BF16)
            stats = alloc_stats(nc, es, pfx + "s", 2)
            stqk = sbo("stqk", [128, 16, 512], BF16)
            stv = sbo("stv", [128, 4, D], BF16)
            ps_p = [es.enter_context(nc.psum_tensor(f"{pfx}psp{i}", [128, 512], F32))
                    for i in range(2)]
            ps_v = [es.enter_context(nc.psum_tensor(f"{pfx}psv{i}", [128, 512], F32))
                    for i in range(2)]
            with nc.Block() as blk:
                @blk.gpsimd
                def _(gp):
                    ctx.dinc(g.dma_start(qkw[:], W["l0b_qkW"].rearrange("(c p) f -> p c f", p=128)), "dg")
                    ctx.dinc(g.dma_start(wv[:], W["l0b_wv"].rearrange("(c p) f -> p c f", p=128)), "dg")
                    ctx.dinc(g.dma_start(bqk[:], W["l0b_bqk_cm"]), "dg")
                    ctx.dinc(g.dma_start(bvrow[:], W["l0b_bv"]), "dg")
                    w_ready = ctx.n["dg"]
                    ctx.w(v, "dg", w_ready)
                    ctx.inc(v.tensor_scalar(bqk[:, 0:8], bqk[:, 0:8], 0.125, None,
                                            ALU.mult), "ve")
                    ctx.w(v, "ve")
                    bqk_ready = ctx.n["ve"]

                    ev_norm = [0] * NT
                    ev_tp = [0] * NT
                    ev_pbank = [0, 0]
                    ev_vbank = [0, 0]
                    ev_gdone = 0
                    ev_st = {}
                    first = True
                    for gi in range(NKT):
                        for ii in range(4):
                            i = gi * 4 + ii
                            sl = i % 2
                            if i >= 2:
                                ctx.w(sy, "ve", ev_norm[i - 2])
                            ctx.dinc(sy.dma_start(xs[:, sl, :],
                                                  xin[i * 128:(i + 1) * 128, :]), f"sx{sl}")
                            ctx.w(v, f"sx{sl}")
                            ctx.w(s, f"sx{sl}")
                            if i >= 2:
                                ctx.w(v, f"tp{sl}", ev_tp[i - 2])
                            ev_norm[i] = emit_ln_tile(ctx, xs[:, sl, :], xln[:, sl, :],
                                                      junk[:], stats[sl])
                            ctx.w(sy, "ve", ev_norm[i])
                            if gi >= 1 and ii == 0:
                                ctx.w(sy, "pe", ev_gdone)
                            ctx.dinc(sy.dma_start(xlnT[:, :, ii * 128:(ii + 1) * 128],
                                                  xln[:, sl, :], transpose=True), f"tp{sl}")
                            ev_tp[i] = ctx.n[f"tp{sl}"]
                        ctx.w(te, "tp0", ev_tp[gi * 4 + 2])
                        ctx.w(te, "tp1", ev_tp[gi * 4 + 3])
                        if first:
                            ctx.w(te, "dg", w_ready)
                            first = False
                        for fc in range(16):
                            bank = fc % 2
                            if ev_pbank[bank]:
                                ctx.w(te, "se", ev_pbank[bank])
                            for dc in range(NDC):
                                mm = te.matmul(ps_p[bank][:], qkw[:, dc, fc * 128:(fc + 1) * 128],
                                               xlnT[:, dc, :], start=(dc == 0),
                                               stop=(dc == NDC - 1))
                            i_p = ctx.inc(mm, "pe")
                            ctx.w(s, "pe", i_p)
                            if fc == 0:
                                ctx.w(s, "ve", bqk_ready)
                                if gi >= 1:
                                    ctx.w(s, f"sq", ev_st["q"])
                            ctx.inc(s.activation(stqk[:, fc, :], ps_p[bank][:], AF.Identity,
                                                 scale=(0.125 if fc < 8 else 1.0),
                                                 bias=bqk[:, fc:fc + 1]), "se")
                            ev_pbank[bank] = ctx.n["se"]
                        ev_qk_st = ctx.n["se"]
                        for ii in range(4):
                            for nh in range(2):
                                bank = (ii * 2 + nh) % 2
                                if ev_vbank[bank]:
                                    ctx.w(te, "ve", ev_vbank[bank])
                                if ii == 0 and nh == 0 and gi >= 1:
                                    ctx.w(te, f"sv", ev_st["v"])
                                mm = te.matmul(ps_v[bank][:], ones16,
                                               bvrow[0:1, nh * 512:(nh + 1) * 512],
                                               start=True, stop=False)
                                for dc in range(NDC):
                                    mm = te.matmul(ps_v[bank][:],
                                                   xlnT[:, dc, ii * 128:(ii + 1) * 128],
                                                   wv[:, dc, nh * 512:(nh + 1) * 512],
                                                   start=False, stop=(dc == NDC - 1))
                                i_v = ctx.inc(mm, "pe")
                                ctx.w(v, "pe", i_v)
                                ctx.inc(v.tensor_copy(stv[:, ii, nh * 512:(nh + 1) * 512],
                                                      ps_v[bank][:]), "ve")
                                ev_vbank[bank] = ctx.n["ve"]
                        ev_v_st = ctx.n["ve"]
                        ev_gdone = ctx.n["pe"]
                        ctx.w(sy, "se", ev_qk_st)
                        ctx.dinc(sy.dma_start(
                            qsp[:, gi * 512:(gi + 1) * 512].rearrange("(c p) t -> p c t", p=128),
                            stqk[:, 0:8, :]), "sq")
                        ctx.dinc(sy.dma_start(
                            ksp[:, gi * 512:(gi + 1) * 512].rearrange("(c p) t -> p c t", p=128),
                            stqk[:, 8:16, :]), "sq")
                        ev_st["q"] = ctx.n["sq"]
                        ctx.w(sy, "ve", ev_v_st)
                        ctx.dinc(sy.dma_start(
                            vsp[gi * 512:(gi + 1) * 512, :].rearrange("(i p) d -> p i d", p=128),
                            stv[:]), "sv")
                        ev_st["v"] = ctx.n["sv"]

        with ExitStack() as esB:
            es = esB
            s_sb = sbo("s_sb", [128, 2, T], F32)
            p_sb = sbo("p_sb", [128, 2, T], BF16)
            pt_sb = sbo("pt", [128, 2, NT, 128], BF16)
            k_h = sbo("k_h", [64, 2, T], BF16)
            v_h = sbo("v_h", [128, 2, NT, 64], BF16)
            q_st = sbo("q_st", [64, 2, 128], BF16)
            mcol = sbo("mcol", [128, 2], F32)
            negm = sbo("negm", [128, 2], F32)
            lcol = sbo("lcol", [128, 2], F32)
            rlcol = sbo("rlcol", [128, 2], F32)
            ps_s = [es.enter_context(nc.psum_tensor(f"{pfx}pss{i}", [128, 512], F32))
                    for i in range(2)]
            ps_pv = [es.enter_context(nc.psum_tensor(f"{pfx}pspv{i}", [128, 64], F32))
                     for i in range(2)]
            with nc.Block() as blk:
                @blk.gpsimd
                def _(gp):
                    ctx.w(te, "se")
                    ctx.w(te, "ve")
                    ctx.w(sy, "sq")
                    ctx.w(sy, "sv")
                    ev_ssbank = [0, 0]
                    ev_exp_slot = [0, 0]
                    ev_recip_slot = [0, 0]
                    ev_pt_slot = [0, 0]
                    ev_pv_slot = [0, 0]
                    ev_pvbank = [0, 0]
                    ev_kv = [0, 0]
                    ev_scores_q = [0, 0]
                    ev_exp_last = 0

                    def emit_front(h, qt, hsl):
                        qsl = qt % 2
                        if ev_scores_q[qsl]:
                            ctx.w(sy, "pe", ev_scores_q[qsl])
                        ctx.dinc(sy.dma_start(q_st[:, qsl, :],
                                              qsp[h * 64:(h + 1) * 64,
                                                  qt * 128:(qt + 1) * 128]), f"qh{qsl}")
                        ctx.w(te, f"qh{qsl}")
                        for kt in range(NKT):
                            bank = kt % 2
                            if ev_ssbank[bank]:
                                ctx.w(te, "ve", ev_ssbank[bank])
                            mm = te.matmul(ps_s[bank][:], q_st[:, qsl, :],
                                           k_h[:, hsl, kt * 512:(kt + 1) * 512],
                                           start=True, stop=True)
                            i_s = ctx.inc(mm, "pe")
                            ctx.w(v, "pe", i_s)
                            if kt == 0 and ev_exp_slot[qsl]:
                                ctx.w(v, "se", ev_exp_slot[qsl])
                            ctx.inc(v.scalar_tensor_tensor(
                                s_sb[:, qsl, kt * 512:(kt + 1) * 512],
                                rel0[:, kt * 512:(kt + 1) * 512], slopes[h],
                                ps_s[bank][:], ALU.mult, ALU.add), "ve")
                            ev_ssbank[bank] = ctx.n["ve"]
                        ev_scores_q[qsl] = ctx.n["pe"]
                        ctx.w(v, "ve")
                        ctx.inc(v.reduce_max(mcol[:, qsl:qsl + 1], s_sb[:, qsl, :],
                                             axis=AX.X), "ve")
                        ctx.w(v, "ve")
                        if ev_exp_slot[qsl]:
                            ctx.w(v, "se", ev_exp_slot[qsl])
                        ctx.inc(v.tensor_scalar(negm[:, qsl:qsl + 1], mcol[:, qsl:qsl + 1],
                                                -1.0, None, ALU.mult), "ve")
                        ev_rmax = ctx.n["ve"]
                        ctx.w(s, "ve", ev_rmax)
                        if ev_pt_slot[qsl]:
                            ctx.w(s, f"pt{qsl}", ev_pt_slot[qsl])
                        if ev_recip_slot[qsl]:
                            ctx.w(s, "ve", ev_recip_slot[qsl])
                        ev_exp_slot[qsl] = ctx.inc(
                            s.activation(p_sb[:, qsl, :], s_sb[:, qsl, :], AF.Exp,
                                         bias=negm[:, qsl:qsl + 1],
                                         accum_out=lcol[:, qsl:qsl + 1]), "se")
                        ctx.w(v, "se", ev_exp_slot[qsl])
                        ctx.inc(v.reciprocal(rlcol[:, qsl:qsl + 1], lcol[:, qsl:qsl + 1]),
                                "ve")
                        ev_recip_slot[qsl] = ctx.n["ve"]
                        ctx.w(sy, "se", ev_exp_slot[qsl])
                        if ev_pv_slot[qsl]:
                            ctx.w(sy, "pe", ev_pv_slot[qsl])
                        ctx.dinc(sy.dma_start(pt_sb[:, qsl, :, :], p_sb[:, qsl, :],
                                              transpose=True), f"pt{qsl}")
                        ev_pt_slot[qsl] = ctx.n[f"pt{qsl}"]

                    def emit_back(h, qt, hsl):
                        qsl = qt % 2
                        ctx.w(te, f"pt{qsl}", ev_pt_slot[qsl])
                        bank = qsl
                        if ev_pvbank[bank]:
                            ctx.w(te, "ve", ev_pvbank[bank])
                        for kc in range(NT):
                            mm = te.matmul(ps_pv[bank][:], pt_sb[:, qsl, kc, :],
                                           v_h[:, hsl, kc, :],
                                           start=(kc == 0), stop=(kc == NT - 1))
                        i_pv = ctx.inc(mm, "pe")
                        ev_pv_slot[qsl] = ctx.n["pe"]
                        ctx.w(v, "pe", i_pv)
                        ctx.w(v, "ve", ev_recip_slot[qsl])
                        ctx.inc(v.tensor_scalar(o_full[:, qt, h * 64:(h + 1) * 64],
                                                ps_pv[bank][:], rlcol[:, qsl:qsl + 1],
                                                None, ALU.mult), "ve")
                        ev_pvbank[bank] = ctx.n["ve"]

                    for h in range(16):
                        hsl = h % 2
                        if h >= 2:
                            ctx.w(sy, "pe", ev_kv[hsl])
                        ctx.dinc(sy.dma_start(k_h[:, hsl, :],
                                              ksp[h * 64:(h + 1) * 64, :]), f"kh{hsl}")
                        ctx.dinc(sy.dma_start(
                            v_h[:, hsl, :, :],
                            vsp[:, h * 64:(h + 1) * 64].rearrange("(i p) d -> p i d", p=128)),
                            f"kh{hsl}")
                        if h == 0:
                            ctx.w(v, "po", ctx.n["po"])
                        ctx.w(te, f"kh{hsl}")
                        for qt in range(NT):
                            emit_front(h, qt, hsl)
                            if qt >= 1:
                                emit_back(h, qt - 1, hsl)
                        emit_back(h, NT - 1, hsl)
                        ev_kv[hsl] = ctx.n["pe"]

        with ExitStack() as esC:
            es = esC
            oT = sbo("oT", [128, NDC, 128], BF16)
            xold = sbo("xold", [128, 1, D], F32)
            onew = sbo("onew", [128, 2, D], F32)
            ps_c = es.enter_context(nc.psum_tensor(pfx + "psc", [128, 1024], F32))
            with nc.Block() as blk:
                @blk.gpsimd
                def _(gp):
                    ctx.w(te, "ve")
                    ctx.w(sy, "ve")
                    ev_stt = 0
                    ev_so_slot = [0, 0]
                    ev_cmm = 0
                    for qt in range(NT):
                        sl = qt % 2
                        if ev_cmm:
                            ctx.w(sy, "pe", ev_cmm)
                        ctx.dinc(sy.dma_start(oT[:], o_full[:, qt, :], transpose=True),
                                 "pt0")
                        ctx.w(te, "pt0")
                        if ev_stt:
                            ctx.w(te, "ve", ev_stt)
                        for nh in range(2):
                            mm = te.matmul(ps_c[:, nh * 512:(nh + 1) * 512], ones16,
                                           borow[0:1, nh * 512:(nh + 1) * 512],
                                           start=True, stop=False)
                            for dc in range(NDC):
                                mm = te.matmul(ps_c[:, nh * 512:(nh + 1) * 512],
                                               oT[:, dc, :],
                                               wo_w[:, dc, nh * 512:(nh + 1) * 512],
                                               start=False, stop=(dc == NDC - 1))
                        i_c = ctx.inc(mm, "pe")
                        ev_cmm = ctx.n["pe"]
                        if ev_stt:
                            ctx.w(sy, "ve", ev_stt)
                        ctx.dinc(sy.dma_start(xold[:, 0, :],
                                              xin[qt * 128:(qt + 1) * 128, :]), "xo0")
                        ctx.w(v, "pe", i_c)
                        ctx.w(v, "xo0")
                        if ev_so_slot[sl]:
                            ctx.w(v, f"so{sl}", ev_so_slot[sl])
                        ins = v.scalar_tensor_tensor(onew[:, sl, :], ps_c[:], 1.0,
                                                     xold[:, 0, :], ALU.mult, ALU.add)
                        ctx.inc(ins, "ve")
                        ev_stt = ctx.n["ve"]
                        ctx.w(sy, "ve", ev_stt)
                        ctx.dinc(sy.dma_start(xout[qt * 128:(qt + 1) * 128, :],
                                              onew[:, sl, :]), f"so{sl}")
                        ev_so_slot[sl] = ctx.n[f"so{sl}"]
    ctx.barrier()





def emit_l1a(ctx, xin, xout, W):
    nc = ctx.nc
    te, v, s, g, sy = nc.tensor, nc.vector, nc.scalar, nc.gpsimd, nc.sync
    uid = ctx.uid()
    pfx = f"ga{uid}_"
    qsp = nc.dram_tensor(f"gqsp{uid}", [D, T], BF16, kind="Internal").ap()
    with ExitStack() as eso:
        def sbo(name, shape, dt):
            return es.enter_context(nc.sbuf_tensor(pfx + name, shape, dt))
        es = eso
        kT = sbo("kT", [128, NDC, T], BF16)
        v_tm = sbo("v", [128, NT, D], BF16)
        c1col = sbo("c1c", [128, 1], F32)
        c2col = sbo("c2c", [128, 1], F32)
        ones32 = ctx.consts["ones32"][:]
        ones16 = ctx.consts["ones16"][:]

        with ExitStack() as esS:
            es = esS
            rotw = sbo("rotw", [1, 3, D], F32)
            scb = sbo("scb", [1, 2], F32)
            tA = sbo("tA", [1, 3, 512], F32)
            tB = sbo("tB", [1, 3, 512], F32)
            crow = sbo("crow", [1, 3, 512], F32)
            srow = sbo("srow", [1, 3, 512], F32)
            c12row = sbo("c12", [1, 2], F32)
            cb = sbo("cb", [128, 3, 512], F32)
            sb_ = sbo("sb", [128, 3, 512], F32)
            ps_b = es.enter_context(nc.psum_tensor(pfx + "psb", [128, 512], F32))
            with nc.Block() as blk:
                @blk.gpsimd
                def _(gp):
                    ctx.dinc(g.dma_start(rotw[:], W["l1a_rot"].rearrange("(one r) d -> one r d", one=1)), "dg")
                    ctx.dinc(g.dma_start(scb[:], W["l1a_scb"]), "dg")
                    ctx.w(v, "dg")
                    re = rotw[:, :, :].rearrange("o r (i two) -> o r i two", two=2)
                    ctx.inc(v.tensor_tensor(tA[:], re[:, :, :, 0], re[:, :, :, 0], ALU.mult), "ve")
                    ctx.inc(v.tensor_tensor(tB[:], re[:, :, :, 1], re[:, :, :, 1], ALU.mult), "ve")
                    ctx.w(v, "ve")
                    ctx.inc(v.tensor_tensor(tA[:], tA[:], tB[:], ALU.add), "ve")
                    ctx.w(s, "ve")
                    ctx.inc(s.activation(tA[:], tA[:], AF.Sqrt), "se")
                    ctx.w(v, "se")
                    ctx.inc(v.reciprocal(tA[:], tA[:]), "ve")
                    ctx.w(v, "ve")
                    ctx.inc(v.tensor_tensor(crow[:], re[:, :, :, 0], tA[:], ALU.mult), "ve")
                    ctx.inc(v.tensor_tensor(srow[:], re[:, :, :, 1], tA[:], ALU.mult), "ve")
                    ctx.w(v, "ve")
                    ctx.inc(v.reciprocal(c12row[:, 0:1], scb[:, 0:1]), "ve")
                    ctx.w(v, "ve")
                    ctx.inc(v.tensor_scalar(c12row[:, 0:1], c12row[:, 0:1], 2.0, None, ALU.mult), "ve")
                    ctx.w(v, "ve")
                    ctx.inc(v.tensor_tensor(c12row[:, 1:2], c12row[:, 0:1], scb[:, 1:2], ALU.add), "ve")
                    ctx.w(v, "ve")
                    st_rows = ctx.n["ve"]
                    ctx.w(te, "ve", st_rows)
                    for r in range(3):
                        i_m = ctx.inc(te.matmul(ps_b[:], ones32, crow[:, r, :],
                                                start=True, stop=True), "pe")
                        ctx.w(v, "pe", i_m)
                        ctx.inc(v.tensor_copy(cb[:, r, :], ps_b[:]), "ve")
                        ctx.w(te, "ve")
                        i_m = ctx.inc(te.matmul(ps_b[:], ones32, srow[:, r, :],
                                                start=True, stop=True), "pe")
                        ctx.w(v, "pe", i_m)
                        ctx.inc(v.tensor_copy(sb_[:, r, :], ps_b[:]), "ve")
                        ctx.w(te, "ve")
                    i_m = ctx.inc(te.matmul(ps_b[:, 0:2], ones32, c12row[:],
                                            start=True, stop=True), "pe")
                    ctx.w(v, "pe", i_m)
                    ctx.inc(v.tensor_copy(c1col[:], ps_b[:, 0:1]), "ve")
                    ctx.inc(v.tensor_copy(c2col[:], ps_b[:, 1:2]), "ve")
                    ctx.w(v, "ve")

            with ExitStack() as esA:
                es = esA
                xs = sbo("xs", [128, 2, D], F32)
                xln = sbo("xln", [128, 2, D], BF16)
                junk = sbo("junk", [128, D], BF16)
                stats = alloc_stats(nc, es, pfx, 2)
                t1 = sbo("t1", [128, 512], F32)
                t2 = sbo("t2", [128, 512], F32)
                rq = sbo("rq", [128, 2, D], BF16)
                rk = sbo("rk", [128, 2, D], BF16)
                stq = sbo("stq", [128, NDC, 128], BF16)
                with nc.Block() as blk:
                    @blk.gpsimd
                    def _(gp):
                        ev_norm = [0] * NT
                        ev_rot = [0, 0]
                        ev_tpq = [0, 0]
                        ev_tpk = [0, 0]
                        ev_stq = 0
                        for i in range(NT):
                            sl = i % 2
                            if i >= 2:
                                ctx.w(sy, "ve", ev_norm[i - 2])
                            ctx.dinc(sy.dma_start(xs[:, sl, :],
                                                  xin[i * 128:(i + 1) * 128, :]), f"sx{sl}")
                            ctx.w(v, f"sx{sl}")
                            ctx.w(s, f"sx{sl}")
                            if i >= 2:
                                ctx.w(v, f"tpq{sl}", ev_tpq[sl])
                                ctx.w(v, f"tpk{sl}", ev_tpk[sl])
                            ev_norm[i] = emit_ln_tile(ctx, xs[:, sl, :], xln[:, sl, :],
                                                      junk[:], stats[sl])
                            ctx.w(v, "ve", ev_norm[i])
                            xe = xln[:, sl, :].rearrange("p (i two) -> p i two", two=2)
                            for r, dst in ((0, rq[:, sl, :]), (1, rk[:, sl, :]),
                                           (2, v_tm[:, i, :])):
                                de = dst.rearrange("p (i two) -> p i two", two=2)
                                ctx.inc(v.tensor_tensor(t1[:], xe[:, :, 0], cb[:, r, :], ALU.mult), "ve")
                                ctx.inc(v.tensor_tensor(t2[:], xe[:, :, 1], sb_[:, r, :], ALU.mult), "ve")
                                ctx.w(v, "ve")
                                ctx.inc(v.tensor_tensor(de[:, :, 0], t1[:], t2[:], ALU.subtract), "ve")
                                ctx.w(v, "ve")
                                ctx.inc(v.tensor_tensor(t1[:], xe[:, :, 0], sb_[:, r, :], ALU.mult), "ve")
                                ctx.inc(v.tensor_tensor(t2[:], xe[:, :, 1], cb[:, r, :], ALU.mult), "ve")
                                ctx.w(v, "ve")
                                ctx.inc(v.tensor_tensor(de[:, :, 1], t1[:], t2[:], ALU.add), "ve")
                                ctx.w(v, "ve")
                            ev_rot[sl] = ctx.n["ve"]
                            ctx.w(sy, "ve", ev_rot[sl])
                            if ev_stq:
                                ctx.w(sy, f"sq", ev_stq)
                            ctx.dinc(sy.dma_start(stq[:], rq[:, sl, :], transpose=True), f"tpq{sl}")
                            ev_tpq[sl] = ctx.n[f"tpq{sl}"]
                            ctx.w(sy, f"tpq{sl}", ev_tpq[sl])
                            ctx.dinc(sy.dma_start(
                                qsp[:, i * 128:(i + 1) * 128].rearrange("(c p) t -> p c t", p=128),
                                stq[:]), "sq")
                            ev_stq = ctx.n["sq"]
                            ctx.dinc(sy.dma_start(kT[:, :, i * 128:(i + 1) * 128],
                                                  rk[:, sl, :], transpose=True), f"tpk{sl}")
                            ev_tpk[sl] = ctx.n[f"tpk{sl}"]


        with ExitStack() as esB:
            es = esB
            s_sb = sbo("s_sb", [128, T], F32)
            p_sb = sbo("p_sb", [128, T], BF16)
            pt_sb = sbo("pt", [128, NT, 128], BF16)
            q_st = sbo("q_st", [128, 2, NDC, 128], BF16)
            junkB = sbo("junkB", [128, 512], BF16)
            mcol = sbo("mcol", [128, 1], F32)
            negm = sbo("negm", [128, 1], F32)
            lcol = sbo("lcol", [128, 1], F32)
            rlcol = sbo("rlcol", [128, 1], F32)
            i1 = sbo("i1", [128, 1], F32)
            i2 = sbo("i2", [128, 1], F32)
            rdcol = sbo("rdcol", [128, 1], F32)
            xold = sbo("xold", [128, 1, D], F32)
            onew = sbo("onew", [128, 2, D], F32)
            ps_s = [es.enter_context(nc.psum_tensor(f"{pfx}pss{i}", [128, 512], F32))
                    for i in range(2)]
            ps_o = es.enter_context(nc.psum_tensor(pfx + "pso", [128, 1024], F32))
            with nc.Block() as blk:
                @blk.gpsimd
                def _(gp):
                    ctx.w(te, "ve")
                    ctx.w(te, "se")
                    ctx.w(te, "tpk0")
                    ctx.w(te, "tpk1")
                    ctx.w(sy, "sq")
                    ev_ssbank = [0, 0]
                    ev_rmax = 0
                    ev_exp = 0
                    ev_recip = 0
                    ev_pt = 0
                    ev_pv = 0
                    ev_stt = 0
                    ev_sq = [0, 0]
                    ev_so_slot = [0, 0]
                    ev_den = 0
                    for qt in range(NT):
                        sl = qt % 2
                        if ev_sq[sl]:
                            ctx.w(sy, "pe", ev_sq[sl])
                        ctx.dinc(sy.dma_start(
                            q_st[:, sl, :, :],
                            qsp[:, qt * 128:(qt + 1) * 128].rearrange("(c p) t -> p c t", p=128)),
                            f"qh{sl}")
                        ctx.w(te, f"qh{sl}")
                        for kt in range(NKT):
                            bank = kt % 2
                            if ev_ssbank[bank]:
                                ctx.w(te, "ve", ev_ssbank[bank])
                            for dc in range(NDC):
                                mm = te.matmul(ps_s[bank][:], q_st[:, sl, dc, :],
                                               kT[:, dc, kt * 512:(kt + 1) * 512],
                                               start=(dc == 0), stop=(dc == NDC - 1))
                            i_s = ctx.inc(mm, "pe")
                            ctx.w(v, "pe", i_s)
                            if qt >= 1 and kt == 0:
                                ctx.w(v, "se", ev_exp)
                            ctx.inc(v.tensor_scalar(s_sb[:, kt * 512:(kt + 1) * 512],
                                                    ps_s[bank][:], c1col[:], c2col[:],
                                                    ALU.mult, ALU.add), "ve")
                            ev_ssbank[bank] = ctx.n["ve"]
                        ev_sq[sl] = ctx.n["pe"]
                        ctx.w(v, "ve")
                        ctx.inc(v.reduce_max(mcol[:], s_sb[:], axis=AX.X), "ve")
                        ctx.w(v, "ve")
                        if ev_exp:
                            ctx.w(v, "se", ev_exp)
                        ctx.inc(v.tensor_scalar(negm[:], mcol[:], -1.0, None, ALU.mult), "ve")
                        ev_rmax = ctx.n["ve"]
                        ctx.w(s, "ve", ev_rmax)
                        if ev_pt:
                            ctx.w(s, "pt0", ev_pt)
                        if ev_recip:
                            ctx.w(s, "ve", ev_recip)
                        ev_exp = ctx.inc(s.activation(p_sb[:], s_sb[:], AF.Exp,
                                                      bias=negm[:], accum_out=lcol[:]), "se")
                        ctx.w(v, "se", ev_exp)
                        ctx.inc(v.reciprocal(rlcol[:], lcol[:]), "ve")
                        ev_recip = ctx.n["ve"]
                        ctx.w(sy, "se", ev_exp)
                        if ev_pv:
                            ctx.w(sy, "pe", ev_pv)
                        ctx.dinc(sy.dma_start(pt_sb[:], p_sb[:], transpose=True), "pt0")
                        ev_pt = ctx.n["pt0"]
                        ctx.w(te, "pt0", ev_pt)
                        if ev_stt:
                            ctx.w(te, "ve", ev_stt)
                            ctx.w(te, "se", ev_den)
                        for nh in range(2):
                            for kc in range(NT):
                                mm = te.matmul(ps_o[:, nh * 512:(nh + 1) * 512],
                                               pt_sb[:, kc, :],
                                               v_tm[:, kc, nh * 512:(nh + 1) * 512],
                                               start=(kc == 0), stop=(kc == NT - 1))
                            ctx.inc(mm, "pe")
                        ev_pv = ctx.n["pe"]
                        ctx.w(s, "pe", ev_pv)
                        ctx.inc(s.activation(junkB[:], ps_o[:, 0:512], AF.Square,
                                             accum_out=i1[:]), "se")
                        ctx.w(s, "se")
                        ctx.inc(s.activation(junkB[:], ps_o[:, 512:1024], AF.Square,
                                             accum_out=i2[:]), "se")
                        ev_den = ctx.n["se"]
                        ctx.w(v, "se", ev_den)
                        ctx.w(v, "ve", ev_recip)
                        ctx.inc(v.tensor_tensor(i1[:], i1[:], i2[:], ALU.add), "ve")
                        ctx.w(v, "ve")
                        ctx.inc(v.tensor_tensor(i1[:], i1[:], rlcol[:], ALU.mult), "ve")
                        ctx.w(v, "ve")
                        ctx.inc(v.tensor_tensor(i1[:], i1[:], rlcol[:], ALU.mult), "ve")
                        ctx.w(v, "ve")
                        i_rl2 = ctx.n["ve"]
                        ctx.w(s, "ve", i_rl2)
                        i_abs = ctx.inc(s.activation(i1[:], i1[:], AF.Abs), "se")
                        ctx.w(v, "se", i_abs)
                        ctx.inc(v.tensor_scalar(i1[:], i1[:], 1e-8, None, ALU.max), "ve")
                        i_cl = ctx.n["ve"]
                        ctx.w(s, "ve", i_cl)
                        i_sq2 = ctx.inc(s.activation(rdcol[:], i1[:], AF.Sqrt), "se")
                        ctx.w(v, "se", i_sq2)
                        ctx.inc(v.reciprocal(rdcol[:], rdcol[:]), "ve")
                        ctx.w(v, "ve")
                        ctx.inc(v.tensor_tensor(rdcol[:], rdcol[:], rlcol[:], ALU.mult), "ve")
                        ctx.w(v, "ve")
                        if ev_stt:
                            ctx.w(sy, "ve", ev_stt)
                        ctx.dinc(sy.dma_start(xold[:, 0, :],
                                              xin[qt * 128:(qt + 1) * 128, :]), "xo0")
                        ctx.w(v, "xo0")
                        if ev_so_slot[sl]:
                            ctx.w(v, f"so{sl}", ev_so_slot[sl])
                        for nh in range(2):
                            ins = v.scalar_tensor_tensor(
                                onew[:, sl, nh * 512:(nh + 1) * 512],
                                ps_o[:, nh * 512:(nh + 1) * 512], rdcol[:],
                                xold[:, 0, nh * 512:(nh + 1) * 512], ALU.mult, ALU.add)
                        ctx.inc(ins, "ve")
                        ev_stt = ctx.n["ve"]
                        ctx.w(sy, "ve", ev_stt)
                        ctx.dinc(sy.dma_start(xout[qt * 128:(qt + 1) * 128, :],
                                              onew[:, sl, :]), f"so{sl}")
                        ev_so_slot[sl] = ctx.n[f"so{sl}"]
    ctx.barrier()





def emit_precast_l1w(ctx, W):
    if getattr(ctx, "l1w", None) is not None:
        return ctx.l1w
    nc = ctx.nc
    g, sy = nc.gpsimd, nc.sync
    w1b = nc.dram_tensor("w1bsp", [8, 128, NDC, 4096], BF16, kind="Internal").ap()
    w2b = nc.dram_tensor("w2bsp", [8, 128, 32, 1024], BF16, kind="Internal").ap()
    with ExitStack() as es:
        st1 = [es.enter_context(nc.sbuf_tensor(f"pc_st1_{i}", [128, NDC, 2048], BF16))
               for i in range(2)]
        with nc.Block() as blk:
            @blk.gpsimd
            def _(gp):
                ev_store = [0, 0]
                for e in range(8):
                    for hh in range(2):
                        sl = (e * 2 + hh) % 2
                        if ev_store[sl]:
                            ctx.w(g, f"pcs{sl}", ev_store[sl])
                        ctx.dinc(g.dma_start(
                            st1[sl][:],
                            W["l1f_w1"][e, :, hh * 2048:(hh + 1) * 2048]
                            .rearrange("(c p) f -> p c f", p=128)), f"pcl{sl}")
                        ctx.w(sy, f"pcl{sl}")
                        ctx.dinc(sy.dma_start(w1b[e, :, :, hh * 2048:(hh + 1) * 2048],
                                              st1[sl][:]), f"pcs{sl}")
                        ev_store[sl] = ctx.n[f"pcs{sl}"]
                for e in range(8):
                    for hh in range(2):
                        sl = (e * 2 + hh) % 2
                        ctx.w(g, f"pcs{sl}", ev_store[sl])
                        ctx.dinc(g.dma_start(
                            st1[sl][:].rearrange("p c f -> p (c f)")[:, 0:16384]
                            .rearrange("p (c d) -> p c d", c=16),
                            W["l1f_w2"][e, hh * 2048:(hh + 1) * 2048, :]
                            .rearrange("(c p) d -> p c d", p=128)), f"pcl{sl}")
                        ctx.w(sy, f"pcl{sl}")
                        ctx.dinc(sy.dma_start(w2b[e, :, hh * 16:(hh + 1) * 16, :],
                                              st1[sl][:].rearrange("p c f -> p (c f)")[:, 0:16384]
                                              .rearrange("p (c d) -> p c d", c=16)), f"pcs{sl}")
                        ev_store[sl] = ctx.n[f"pcs{sl}"]
    ctx.barrier()
    ctx.l1w = (w1b, w2b)
    return ctx.l1w


def emit_l1ffn(ctx, xin, xout, W):
    nc = ctx.nc
    te, v, s, g, sy = nc.tensor, nc.vector, nc.scalar, nc.gpsimd, nc.sync
    w1b, w2b = emit_precast_l1w(ctx, W)
    uid = ctx.uid()
    pfx = f"m{uid}_"
    xm = nc.dram_tensor(f"xmid{uid}", [T, D], F32, kind="Internal").ap()
    TS = 512
    NSL = T // TS
    TPS = TS // 128
    with ExitStack() as eso:
        def sbo(name, shape, dt):
            return es.enter_context(nc.sbuf_tensor(pfx + name, shape, dt))
        es = eso
        pool = es.enter_context(nc.psum_tensor(pfx + "pool", [128, 8, 512], F32))
        ps_t = pool[:, 5, 0:128]
        ones16 = ctx.consts["ones16"][:]
        ones32 = ctx.consts["ones32"][:]
        id32 = ctx.consts["ident32"][:]

        def bcast_row(row_ap, dst, n_done):
            for nh in range(2):
                ctx.w(te, "ve", n_done[0])
                i_m = ctx.inc(te.matmul(pool[:, 7, :],
                                        ones32, row_ap[0:1, nh * 512:(nh + 1) * 512],
                                        start=True, stop=True), "pe")
                ctx.w(v, "pe", i_m)
                ctx.inc(v.tensor_copy(dst[:, nh * 512:(nh + 1) * 512], pool[:, 7, :]), "ve")
                n_done[0] = ctx.n["ve"]

        with ExitStack() as esM:
            es = esM
            g1b = sbo("g1b", [128, D], F32)
            be1b = sbo("be1b", [128, D], F32)
            rows = sbo("rows", [1, 2, D], F32)
            mixw = sbo("mixw", [128, NDC, D], BF16)
            mixb = sbo("mixb", [1, D], BF16)
            xs = sbo("xs", [128, 2, D], F32)
            z32 = sbo("z32", [128, 2, D], F32)
            t132 = sbo("t132", [128, 2, D], F32)
            nbuf = sbo("nbuf", [128, 2, D], F32)
            stile = sbo("stile", [128, 2, D], F32)
            n16 = sbo("n16", [128, 2, D], BF16)
            nT = sbo("nT", [128, NDC, 128], BF16)
            junk = sbo("junk", [128, D], BF16)
            stats = alloc_stats(nc, es, pfx, 2)
            onew = sbo("onew", [128, 2, D], F32)
            with nc.Block() as blk:
                @blk.gpsimd
                def _(gp):
                    ctx.dinc(g.dma_start(rows[:, 0, :], W["l1f_g1r"]), "dg")
                    ctx.dinc(g.dma_start(rows[:, 1, :], W["l1f_be1r"]), "dg")
                    ctx.dinc(g.dma_start(mixw[:], W["l1f_mixw"].rearrange("(c p) f -> p c f", p=128)), "dg")
                    ctx.dinc(g.dma_start(mixb[:], W["l1f_mixb"]), "dg")
                    w_ready = ctx.n["dg"]
                    ctx.w(te, "dg", w_ready)
                    ctx.w(te, "po", ctx.n["po"])
                    nd = [0]
                    bcast_row(rows[:, 0, :], g1b[:], nd)
                    bcast_row(rows[:, 1, :], be1b[:], nd)
                    ev_bc = ctx.n["ve"]
                    ev_norm = [0] * NT
                    ev_tp = 0
                    ev_mm = 0
                    ev_stt = 0
                    ev_stile = [0, 0]
                    ev_tp_slot = [0, 0]
                    ev_stt_slot = [0, 0]
                    ev_so_slot = [0, 0]
                    for i in range(NT):
                        sl = i % 2
                        if ev_stile[sl]:
                            ctx.w(sy, "ve", ev_stile[sl])
                        ctx.dinc(sy.dma_start(xs[:, sl, :],
                                              xin[i * 128:(i + 1) * 128, :]), f"sx{sl}")
                        ctx.w(v, f"sx{sl}")
                        ctx.w(s, f"sx{sl}")
                        if i == 0:
                            ctx.w(v, "ve", ev_bc)
                        ev_z = emit_ln_tile(ctx, xs[:, sl, :], z32[:, sl, :],
                                            junk[:], stats[sl])
                        ctx.w(v, "ve", ev_z)
                        ctx.w(s, "ve", ev_z)
                        ev_norm[i] = emit_ln_tile(ctx, z32[:, sl, :], t132[:, sl, :],
                                                  junk[:], stats[sl])
                        ctx.w(v, "ve", ev_norm[i])
                        ctx.inc(v.tensor_tensor(nbuf[:, sl, :], t132[:, sl, :], g1b[:],
                                                ALU.mult), "ve")
                        ctx.w(v, "ve")
                        ctx.inc(v.tensor_tensor(nbuf[:, sl, :], nbuf[:, sl, :], be1b[:],
                                                ALU.add), "ve")
                        ctx.w(v, "ve")
                        ctx.inc(v.tensor_tensor(stile[:, sl, :], z32[:, sl, :],
                                                nbuf[:, sl, :], ALU.add), "ve")
                        ev_stile[sl] = ctx.n["ve"]
                        ctx.w(s, "ve", ev_stile[sl])
                        if ev_tp_slot[sl]:
                            ctx.w(s, "tpm", ev_tp_slot[sl])
                        i_c = ctx.inc(s.copy(n16[:, sl, :], nbuf[:, sl, :]), "se")
                        ctx.w(sy, "se", i_c)
                        if ev_mm:
                            ctx.w(sy, "pe", ev_mm)
                        ctx.dinc(sy.dma_start(nT[:], n16[:, sl, :], transpose=True), "tpm")
                        ev_tp = ctx.n["tpm"]
                        ev_tp_slot[sl] = ev_tp
                        ctx.w(te, "tpm", ev_tp)
                        if ev_stt:
                            ctx.w(te, "ve", ev_stt)
                        for nh in range(2):
                            mm = te.matmul(pool[:, nh, :], ones16,
                                           mixb[0:1, nh * 512:(nh + 1) * 512],
                                           start=True, stop=False)
                            for dc in range(NDC):
                                mm = te.matmul(pool[:, nh, :], nT[:, dc, :],
                                               mixw[:, dc, nh * 512:(nh + 1) * 512],
                                               start=False, stop=(dc == NDC - 1))
                        i_mm = ctx.inc(mm, "pe")
                        ev_mm = ctx.n["pe"]
                        ctx.w(v, "pe", i_mm)
                        if ev_so_slot[sl]:
                            ctx.w(v, f"so{sl}", ev_so_slot[sl])
                        ins = v.scalar_tensor_tensor(
                            onew[:, sl, :],
                            pool[:, 0:2, :].rearrange("p a b -> p (a b)"),
                            1.0, stile[:, sl, :], ALU.mult, ALU.add)
                        ctx.inc(ins, "ve")
                        ev_stt = ctx.n["ve"]
                        ev_stt_slot[sl] = ev_stt
                        ctx.w(sy, "ve", ev_stt)
                        ctx.dinc(sy.dma_start(xm[i * 128:(i + 1) * 128, :],
                                              onew[:, sl, :]), f"so{sl}")
                        ev_so_slot[sl] = ctx.n[f"so{sl}"]

        with ExitStack() as esE:
            es = esE
            g2b = sbo("g2b", [128, D], F32)
            be2b = sbo("be2b", [128, D], F32)
            rows2 = sbo("rows2", [1, 2, D], F32)
            routw = sbo("routw", [128, NDC, 8], F32)
            w1h = [sbo(f"w1h{i}", [128, NDC, 2048], BF16) for i in range(2)]
            w2ch = sbo("w2ch", [128, 2, D], BF16)
            b1cm = sbo("b1cm", [128, 32], F32)
            b2row = sbo("b2row", [1, D], BF16)
            h_sb = sbo("h", [128, 32, TS], BF16)
            n2T = sbo("n2T", [128, NDC, TS], BF16)
            n2T32 = sbo("n2T32", [128, NDC, 128], F32)
            acc = sbo("acc", [128, TPS, D], F32)
            xs2 = sbo("xs2", [128, 2, D], F32)
            xr2 = sbo("xr2", [128, 2, D], F32)
            t232 = sbo("t232", [128, 2, D], F32)
            n2buf = sbo("n2buf", [128, 2, D], F32)
            n216 = sbo("n216", [128, 2, D], BF16)
            junk2 = sbo("junk2", [128, D], BF16)
            stats2 = alloc_stats(nc, es, pfx + "2", 2)
            tkB = alloc_topk(nc, es, pfx)
            lg = sbo("lg", [128, 8], F32)
            m1c = sbo("m1c", [128, 1], F32)
            m2c = sbo("m2c", [128, 1], F32)
            mask1 = sbo("mask1", [128, 8], F32)
            mask2 = sbo("mask2", [128, 8], F32)
            comb = sbo("comb", [128, TPS, 8], F32)
            with nc.Block() as blk:
                @blk.gpsimd
                def _(gp):
                    ctx.w(sy, "so0")
                    ctx.w(sy, "so1")
                    ctx.w(g, "so0")
                    ctx.w(g, "so1")
                    ctx.w(g, "ve")
                    ctx.w(g, "se")
                    ctx.dinc(g.dma_start(rows2[:, 0, :], W["l1f_g2r"]), "dg")
                    ctx.dinc(g.dma_start(rows2[:, 1, :], W["l1f_be2r"]), "dg")
                    ctx.dinc(g.dma_start(routw[:], W["l1f_routw"].rearrange("(c p) e -> p c e", p=128)), "dg")
                    w_ready = ctx.n["dg"]
                    ctx.w(te, "dg", w_ready)
                    ctx.w(te, "ve")
                    nd = [0]
                    bcast_row(rows2[:, 0, :], g2b[:], nd)
                    bcast_row(rows2[:, 1, :], be2b[:], nd)
                    ev_bc = ctx.n["ve"]
                    ev_gmm = 0
                    ev_tbank = 0
                    ev_lg_free = 0
                    ev_hband = [0, 0]
                    ev_w2evac = 0
                    ev_hmm = 0
                    ev_acc_store = 0
                    ev_w1ld = [0, 0]
                    ev_w2sl = [0, 0]
                    ev_norm2 = {}
                    ev_tp2 = 0
                    ev_cast2 = [0, 0]
                    ev_hmm = 0
                    ev_w1mm = [0, 0]
                    ev_w2mmsl = [0, 0]
                    ev_probs_se = 0
                    ev_h_all_last = 0
                    ev_b2_used = 0
                    ev_tp2_slot = [0, 0]
                    for sli in range(NSL):
                        for tt in range(TPS):
                            i = sli * TPS + tt
                            sl = i % 2
                            if i >= 2:
                                ctx.w(sy, "ve", ev_norm2.get(i - 2, 0))
                            ctx.dinc(sy.dma_start(xs2[:, sl, :],
                                                  xm[i * 128:(i + 1) * 128, :]), f"sm{sl}")
                            ctx.dinc(sy.dma_start(xr2[:, sl, :],
                                                  xin[i * 128:(i + 1) * 128, :]), f"sr{sl}")
                            ctx.w(v, f"sm{sl}")
                            ctx.w(v, f"sr{sl}")
                            ctx.w(s, f"sm{sl}")
                            if sli == 0 and tt == 0:
                                ctx.w(v, "ve", ev_bc)
                            if i >= 2:
                                ctx.w(v, "se", ev_cast2[sl])
                                if ev_gmm:
                                    ctx.w(v, "pe", ev_gmm)
                            env = emit_ln_tile(ctx, xs2[:, sl, :], t232[:, sl, :],
                                               junk2[:], stats2[sl])
                            ctx.w(v, "ve", env)
                            ctx.inc(v.tensor_tensor(n2buf[:, sl, :], t232[:, sl, :],
                                                    g2b[:], ALU.mult), "ve")
                            ctx.w(v, "ve")
                            ctx.inc(v.tensor_tensor(n2buf[:, sl, :], n2buf[:, sl, :],
                                                    be2b[:], ALU.add), "ve")
                            ctx.w(v, "ve")
                            ctx.w(v, "so_acc", ev_acc_store)
                            ctx.inc(v.tensor_tensor(acc[:, tt, :], xr2[:, sl, :],
                                                    xs2[:, sl, :], ALU.add), "ve")
                            ev_norm2[i] = ctx.n["ve"]
                            ctx.w(s, "ve", ev_norm2[i])
                            if ev_tp2_slot[sl]:
                                ctx.w(s, f"tpm{sl}", ev_tp2_slot[sl])
                            ev_cast2[sl] = ctx.inc(s.copy(n216[:, sl, :], n2buf[:, sl, :]), "se")
                            ctx.w(sy, "se", ev_cast2[sl])
                            if sli >= 1 and tt == 0:
                                ctx.w(sy, "pe", ev_hmm)
                            ctx.dinc(sy.dma_start(n2T[:, :, tt * 128:(tt + 1) * 128],
                                                  n216[:, sl, :], transpose=True), f"tpm{sl}")
                            ev_tp2_slot[sl] = ctx.n[f"tpm{sl}"]
                            ctx.w(te, "ve", ev_norm2[i])
                            for dc in range(NDC):
                                if ev_tbank:
                                    ctx.w(te, "ve", ev_tbank)
                                i_t = ctx.inc(te.transpose(ps_t, n2buf[:, sl, dc * 128:(dc + 1) * 128],
                                                           id32), "pe")
                                ctx.w(v, "pe", i_t)
                                if ev_gmm:
                                    ctx.w(v, "pe", ev_gmm)
                                ctx.inc(v.tensor_copy(n2T32[:, dc, :], ps_t), "ve")
                                ev_tbank = ctx.n["ve"]
                            i_t32 = ctx.n["ve"]
                            ctx.w(te, "ve", i_t32)
                            if ev_lg_free:
                                ctx.w(te, "ve", ev_lg_free)
                            for dc in range(NDC):
                                mm = te.matmul(pool[:, 6, 0:8], n2T32[:, dc, :],
                                               routw[:, dc, :], start=(dc == 0),
                                               stop=(dc == NDC - 1))
                            i_g = ctx.inc(mm, "pe")
                            ev_gmm = i_g
                            ctx.w(v, "pe", i_g)
                            if ev_lg_free:
                                ctx.w(v, "se", ev_probs_se)
                            ctx.inc(v.tensor_copy(lg[:], pool[:, 6, 0:8]), "ve")
                            ev_lg_free = ctx.n["ve"]
                            ctx.w(v, "ve")
                            emit_probs(ctx, lg[:], tkB)
                            ev_probs_se = ctx.n["se"]
                            emit_argmax_mask(ctx, tkB["pr"], mask1[:], tkB, save_max=m1c[:])
                            ctx.w(v, "ve")
                            ctx.inc(v.scalar_tensor_tensor(tkB["lg2"], mask1[:], -1e30,
                                                           tkB["pr"], ALU.mult, ALU.add), "ve")
                            ctx.w(v, "ve")
                            emit_argmax_mask(ctx, tkB["lg2"], mask2[:], tkB, save_max=m2c[:])
                            ctx.w(v, "ve")
                            ctx.inc(v.tensor_scalar(mask1[:], mask1[:], m1c[:], None,
                                                    ALU.mult), "ve")
                            ctx.w(v, "ve")
                            ctx.inc(v.tensor_scalar(mask2[:], mask2[:], m2c[:], None,
                                                    ALU.mult), "ve")
                            ctx.w(v, "ve")
                            ctx.inc(v.tensor_tensor(comb[:, tt, :], mask1[:], mask2[:],
                                                    ALU.add), "ve")
                            ctx.w(v, "ve")
                        ev_prep_done = ctx.n["ve"]
                        for e in range(8):
                            for hh in range(2):
                                whsl = hh
                                if ev_w1mm[whsl]:
                                    ctx.w(sy, "pe", ev_w1mm[whsl])
                                ctx.dinc(sy.dma_start(w1h[whsl][:],
                                                      w1b[e, :, :, hh * 2048:(hh + 1) * 2048]),
                                         f"wl{whsl}")
                                ev_w1ld[whsl] = ctx.n[f"wl{whsl}"]
                            if ev_h_all_last:
                                ctx.w(g, "se", ev_h_all_last)
                                ctx.w(g, "pe", ev_b2_used)
                            ctx.dinc(g.dma_start(b1cm[:], W["l1f_b1cm"][e]), "dgb")
                            ctx.dinc(g.dma_start(b2row[:], W["l1f_b2"][e:e + 1, :]), "dgb")
                            ev_b = ctx.n["dgb"]
                            ctx.w(te, "tpm0", ev_tp2_slot[0])
                            ctx.w(te, "tpm1", ev_tp2_slot[1])
                            ctx.w(te, "ve", ev_prep_done)
                            if ev_w2evac:
                                ctx.w(te, "ve", ev_w2evac)
                            for fc in range(32):
                                hh = fc // 16
                                bank = fc % 2
                                ctx.w(te, f"wl{hh}", ev_w1ld[hh])
                                if ev_hband[bank]:
                                    ctx.w(te, "se", ev_hband[bank])
                                for dc in range(NDC):
                                    mm = te.matmul(pool[:, bank, :],
                                                   w1h[hh][:, dc, (fc % 16) * 128:(fc % 16 + 1) * 128],
                                                   n2T[:, dc, :], start=(dc == 0),
                                                   stop=(dc == NDC - 1))
                                i_h = ctx.inc(mm, "pe")
                                if fc == 15:
                                    ev_w1mm_half0 = ctx.n["pe"]
                                ctx.w(s, "pe", i_h)
                                if fc == 0:
                                    ctx.w(s, "dgb", ev_b)
                                ctx.inc(s.activation(h_sb[:, fc, :], pool[:, bank, :],
                                                     AF.Silu, bias=b1cm[:, fc:fc + 1]), "se")
                                ev_hband[bank] = ctx.n["se"]
                            ev_w1mm = [ev_w1mm_half0, ctx.n["pe"]]
                            ev_h_all = ctx.n["se"]
                            ev_h_all_last = ev_h_all
                            for fc in range(32):
                                w2sl = fc % 2
                                if ev_w2mmsl[w2sl]:
                                    ctx.w(sy, "pe", ev_w2mmsl[w2sl])
                                ctx.dinc(sy.dma_start(w2ch[:, w2sl, :], w2b[e, :, fc, :]),
                                         f"w2l{w2sl}")
                                ev_w2sl[w2sl] = ctx.n[f"w2l{w2sl}"]
                                if fc == 0:
                                    ctx.w(te, "se", ev_h_all)
                                ctx.w(te, f"w2l{w2sl}", ev_w2sl[w2sl])
                                for tt in range(TPS):
                                    for nh in range(2):
                                        pb = pool[:, 2 + tt * 2 + nh - 2, :] if False else pool[:, tt * 2 + nh, :]
                                        if fc == 0:
                                            mm = te.matmul(pb, ones16,
                                                           b2row[0:1, nh * 512:(nh + 1) * 512],
                                                           start=True, stop=False)
                                        mm = te.matmul(pb, h_sb[:, fc, tt * 128:(tt + 1) * 128],
                                                       w2ch[:, w2sl, nh * 512:(nh + 1) * 512],
                                                       start=False, stop=(fc == 31))
                                i_w2 = ctx.inc(mm, "pe")
                                ev_w2mmsl[w2sl] = ctx.n["pe"]
                            i_w2_all = ctx.n["pe"]
                            ev_b2_used = i_w2_all
                            ctx.w(v, "pe", i_w2_all)
                            ctx.w(v, "ve", ev_prep_done)
                            for tt in range(TPS):
                                ins = v.scalar_tensor_tensor(
                                    acc[:, tt, :],
                                    pool[:, tt * 2:(tt + 1) * 2, :].rearrange("p a b -> p (a b)"),
                                    comb[:, tt, e:e + 1], acc[:, tt, :],
                                    ALU.mult, ALU.add)
                                ctx.inc(ins, "ve")
                                ctx.w(v, "ve")
                            ev_w2evac = ctx.n["ve"]
                        ev_hmm = ctx.n["pe"]
                        for tt in range(TPS):
                            i = sli * TPS + tt
                            ctx.w(sy, "ve", ev_w2evac)
                            ctx.dinc(sy.dma_start(xout[i * 128:(i + 1) * 128, :],
                                                  acc[:, tt, :]), "so_acc")
                        ev_acc_store = ctx.n["so_acc"]
    ctx.barrier()





def emit_l1b(ctx, xin, xout, W):
    nc = ctx.nc
    te, v, s, g, sy = nc.tensor, nc.vector, nc.scalar, nc.gpsimd, nc.sync
    uid = ctx.uid()
    pfx = f"rb{uid}_"
    qsp = nc.dram_tensor(f"rqsp{uid}", [D, T], BF16, kind="Internal").ap()
    ksp = nc.dram_tensor(f"rksp{uid}", [D, T], BF16, kind="Internal").ap()
    vsp = nc.dram_tensor(f"rvsp{uid}", [T, D], BF16, kind="Internal").ap()
    with ExitStack() as eso:
        def sbo(name, shape, dt):
            return es.enter_context(nc.sbuf_tensor(pfx + name, shape, dt))
        es = eso
        o_full = sbo("ofull", [128, NT, D], BF16)
        wo_w = sbo("wo", [128, NDC, D], BF16)
        borow = sbo("bo", [1, D], BF16)
        masks = sbo("masks", [128, 4, 512], F32)
        ones16 = ctx.consts["ones16"][:]

        with nc.Block() as blk:
            @blk.gpsimd
            def _(gp):
                ctx.dinc(g.dma_start(wo_w[:], W["l1b_woT"].rearrange("(c p) f -> p c f", p=128)), "dg")
                ctx.dinc(g.dma_start(borow[:], W["l1b_ob"]), "dg")
                for m in range(4):
                    ctx.inc(g.memset(masks[:, m, :], 0.0), "po")
                    ctx.w(g, "po")
                    ctx.inc(g.affine_select(masks[:, m, :], masks[:, m, :], [[-1, 512]],
                                            ALU.is_ge, -1e30, base=128 * m,
                                            channel_multiplier=1), "po")
                    ctx.w(g, "po")

        with ExitStack() as esA:
            es = esA
            qkvw = sbo("qkvw", [128, NDC, 3072], BF16)
            qkvb = sbo("qkvb", [1, 3072], BF16)
            cosT = sbo("cosT", [128, NT, 32], F32)
            sinT = sbo("sinT", [128, NT, 32], F32)
            xs = sbo("xs", [128, 2, D], F32)
            xln = sbo("xln", [128, 2, D], BF16)
            xlnT = sbo("xlnT", [128, NDC, 128], BF16)
            junk = sbo("junk", [128, D], BF16)
            stats = alloc_stats(nc, es, pfx, 2)
            qf = sbo("qf", [128, D], F32)
            kf = sbo("kf", [128, D], F32)
            t1 = sbo("t1", [128, 512], F32)
            t2 = sbo("t2", [128, 512], F32)
            rq = sbo("rq", [128, 2, D], BF16)
            rk = sbo("rk", [128, 2, D], BF16)
            vstage = sbo("vst", [128, 2, D], BF16)
            stq = sbo("stq", [128, NDC, 128], BF16)
            ps_p = [es.enter_context(nc.psum_tensor(f"{pfx}psp{i}", [128, 512], F32))
                    for i in range(2)]
            with nc.Block() as blk:
                @blk.gpsimd
                def _(gp):
                    ctx.dinc(g.dma_start(qkvw[:], W["l1b_qkvT"].rearrange("(c p) f -> p c f", p=128)), "dg")
                    ctx.dinc(g.dma_start(qkvb[:], W["l1b_qkvb"]), "dg")
                    ctx.dinc(g.dma_start(cosT[:], W["l1b_cos"][0:T, :].rearrange("(t p) i -> p t i", p=128)), "dg")
                    ctx.dinc(g.dma_start(sinT[:], W["l1b_sin"][0:T, :].rearrange("(t p) i -> p t i", p=128)), "dg")
                    w_ready = ctx.n["dg"]
                    ev_norm = [0] * NT
                    ev_tp = [0, 0]
                    ev_pbank = [0, 0]
                    ev_mm_tile = 0
                    ev_rot = [0, 0]
                    ev_rot_last = 0
                    ev_tpq = [0, 0]
                    ev_tpk = [0, 0]
                    ev_stq = 0
                    ev_sv = [0, 0]
                    ev_evac_q = 0
                    ev_evac_v = [0, 0]
                    first = True
                    for i in range(NT):
                        sl = i % 2
                        if i >= 2:
                            ctx.w(sy, "ve", ev_norm[i - 2])
                        ctx.dinc(sy.dma_start(xs[:, sl, :],
                                              xin[i * 128:(i + 1) * 128, :]), f"sx{sl}")
                        ctx.w(v, f"sx{sl}")
                        ctx.w(s, f"sx{sl}")
                        if i >= 2:
                            ctx.w(v, f"tp{sl}", ev_tp[sl])
                        ev_norm[i] = emit_ln_tile(ctx, xs[:, sl, :], xln[:, sl, :],
                                                  junk[:], stats[sl])
                        ctx.w(sy, "ve", ev_norm[i])
                        if ev_mm_tile:
                            ctx.w(sy, "pe", ev_mm_tile)
                        ctx.dinc(sy.dma_start(xlnT[:], xln[:, sl, :], transpose=True),
                                 f"tp{sl}")
                        ev_tp[sl] = ctx.n[f"tp{sl}"]
                        ctx.w(te, f"tp{sl}", ev_tp[sl])
                        if first:
                            ctx.w(te, "dg", w_ready)
                            first = False
                        for fc6 in range(6):
                            bank = fc6 % 2
                            if ev_pbank[bank]:
                                ctx.w(te, "se" if fc6 < 4 else "se", ev_pbank[bank])
                            mm = te.matmul(ps_p[bank][:], ones16,
                                           qkvb[0:1, fc6 * 512:(fc6 + 1) * 512],
                                           start=True, stop=False)
                            for dc in range(NDC):
                                mm = te.matmul(ps_p[bank][:], xlnT[:, dc, :],
                                               qkvw[:, dc, fc6 * 512:(fc6 + 1) * 512],
                                               start=False, stop=(dc == NDC - 1))
                            i_p = ctx.inc(mm, "pe")
                            ctx.w(s, "pe", i_p)
                            if fc6 < 2:
                                if fc6 == 0 and ev_rot_last:
                                    ctx.w(s, "ve", ev_rot_last)
                                ctx.inc(s.activation(qf[:, fc6 * 512:(fc6 + 1) * 512],
                                                     ps_p[bank][:], AF.Copy, scale=0.125),
                                        "se")
                            elif fc6 < 4:
                                ctx.inc(s.activation(kf[:, (fc6 - 2) * 512:(fc6 - 1) * 512],
                                                     ps_p[bank][:], AF.Copy), "se")
                            else:
                                if ev_sv[sl] and fc6 == 4:
                                    ctx.w(s, f"sv{sl}", ev_sv[sl])
                                ctx.inc(s.copy(vstage[:, sl, (fc6 - 4) * 512:(fc6 - 3) * 512],
                                               ps_p[bank][:]), "se")
                            ev_pbank[bank] = ctx.n["se"]
                        ev_mm_tile = ctx.n["pe"]
                        ev_evac_q = ctx.n["se"]
                        ctx.w(sy, "se", ev_evac_q)
                        ctx.dinc(sy.dma_start(
                            vsp[i * 128:(i + 1) * 128, :], vstage[:, sl, :]), f"sv{sl}")
                        ev_sv[sl] = ctx.n[f"sv{sl}"]
                        ctx.w(v, "se", ev_evac_q)
                        if i >= 2:
                            ctx.w(v, f"tpq{sl}", ev_tpq[sl])
                            ctx.w(v, f"tpk{sl}", ev_tpk[sl])
                        cosb = cosT[:, i, :].rearrange("p (o i) -> p o i", o=1)\
                                            .broadcast_to([128, 16, 32])
                        sinb = sinT[:, i, :].rearrange("p (o i) -> p o i", o=1)\
                                            .broadcast_to([128, 16, 32])
                        for src_f, dst in ((qf, rq), (kf, rk)):
                            sv = src_f[:].rearrange("p (h j) -> p h j", h=16)
                            dv = dst[:, sl, :].rearrange("p (h j) -> p h j", h=16)
                            ctx.inc(v.tensor_tensor(t1[:].rearrange("p (h j) -> p h j", h=16),
                                                    sv[:, :, 0:32], cosb, ALU.mult), "ve")
                            ctx.inc(v.tensor_tensor(t2[:].rearrange("p (h j) -> p h j", h=16),
                                                    sv[:, :, 32:64], sinb, ALU.mult), "ve")
                            ctx.w(v, "ve")
                            ctx.inc(v.tensor_tensor(dv[:, :, 0:32],
                                                    t1[:].rearrange("p (h j) -> p h j", h=16),
                                                    t2[:].rearrange("p (h j) -> p h j", h=16),
                                                    ALU.subtract), "ve")
                            ctx.w(v, "ve")
                            ctx.inc(v.tensor_tensor(t1[:].rearrange("p (h j) -> p h j", h=16),
                                                    sv[:, :, 0:32], sinb, ALU.mult), "ve")
                            ctx.inc(v.tensor_tensor(t2[:].rearrange("p (h j) -> p h j", h=16),
                                                    sv[:, :, 32:64], cosb, ALU.mult), "ve")
                            ctx.w(v, "ve")
                            ctx.inc(v.tensor_tensor(dv[:, :, 32:64],
                                                    t1[:].rearrange("p (h j) -> p h j", h=16),
                                                    t2[:].rearrange("p (h j) -> p h j", h=16),
                                                    ALU.add), "ve")
                            ctx.w(v, "ve")
                        ev_rot[sl] = ctx.n["ve"]
                        ev_rot_last = ev_rot[sl]
                        ctx.w(sy, "ve", ev_rot[sl])
                        if ev_stq:
                            ctx.w(sy, "sq", ev_stq)
                        ctx.dinc(sy.dma_start(stq[:], rq[:, sl, :], transpose=True), f"tpq{sl}")
                        ev_tpq[sl] = ctx.n[f"tpq{sl}"]
                        ctx.w(sy, f"tpq{sl}", ev_tpq[sl])
                        ctx.dinc(sy.dma_start(
                            qsp[:, i * 128:(i + 1) * 128].rearrange("(c p) t -> p c t", p=128),
                            stq[:]), "sq")
                        ev_stq = ctx.n["sq"]
                        ctx.w(sy, "sq", ev_stq)
                        ctx.dinc(sy.dma_start(stq[:], rk[:, sl, :], transpose=True), f"tpk{sl}")
                        ev_tpk[sl] = ctx.n[f"tpk{sl}"]
                        ctx.w(sy, f"tpk{sl}", ev_tpk[sl])
                        ctx.dinc(sy.dma_start(
                            ksp[:, i * 128:(i + 1) * 128].rearrange("(c p) t -> p c t", p=128),
                            stq[:]), "sq")
                        ev_stq = ctx.n["sq"]

        with ExitStack() as esB:
            es = esB
            s_sb = sbo("s_sb", [128, 2, T], F32)
            p_sb = sbo("p_sb", [128, 2, T], BF16)
            pt_sb = sbo("pt", [128, 2, NT, 128], BF16)
            k_h = sbo("k_h", [64, 2, T], BF16)
            v_h = sbo("v_h", [128, 2, NT, 64], BF16)
            q_st = sbo("q_st", [64, 2, 128], BF16)
            mcol = sbo("mcol", [128, 2], F32)
            negm = sbo("negm", [128, 2], F32)
            lcol = sbo("lcol", [128, 2], F32)
            rlcol = sbo("rlcol", [128, 2], F32)
            ps_s = [es.enter_context(nc.psum_tensor(f"{pfx}pss{i}", [128, 512], F32))
                    for i in range(2)]
            ps_pv = [es.enter_context(nc.psum_tensor(f"{pfx}pspv{i}", [128, 64], F32))
                     for i in range(2)]
            with nc.Block() as blk:
                @blk.gpsimd
                def _(gp):
                    ctx.w(te, "se")
                    ctx.w(te, "ve")
                    ctx.w(sy, "sq")
                    ctx.w(sy, "sv0")
                    ctx.w(sy, "sv1")
                    ev_ssbank = [0, 0]
                    ev_exp_slot = [0, 0]
                    ev_recip_slot = [0, 0]
                    ev_pt_slot = [0, 0]
                    ev_pv_slot = [0, 0]
                    ev_pvbank = [0, 0]
                    ev_kv = [0, 0]
                    ev_scores_q = [0, 0]
                    ev_mask_slot = [0, 0]

                    def emit_front(h, qt, hsl):
                        qsl = qt % 2
                        nv = qt // 4 + 1
                        if ev_scores_q[qsl]:
                            ctx.w(sy, "pe", ev_scores_q[qsl])
                        ctx.dinc(sy.dma_start(q_st[:, qsl, :],
                                              qsp[h * 64:(h + 1) * 64,
                                                  qt * 128:(qt + 1) * 128]), f"qh{qsl}")
                        ctx.w(te, f"qh{qsl}")
                        for kt in range(nv):
                            bank = kt % 2
                            if ev_ssbank[bank]:
                                ctx.w(te, "se", ev_ssbank[bank])
                            mm = te.matmul(ps_s[bank][:], q_st[:, qsl, :],
                                           k_h[:, hsl, kt * 512:(kt + 1) * 512],
                                           start=True, stop=True)
                            i_s = ctx.inc(mm, "pe")
                            ctx.w(s, "pe", i_s)
                            if kt == 0:
                                if ev_exp_slot[qsl]:
                                    ctx.w(s, "se", ev_exp_slot[qsl])
                                if ev_mask_slot[qsl]:
                                    ctx.w(s, "ve", ev_mask_slot[qsl])
                            ctx.inc(s.activation(s_sb[:, qsl, kt * 512:(kt + 1) * 512],
                                                 ps_s[bank][:], AF.Copy), "se")
                            ev_ssbank[bank] = ctx.n["se"]
                        ev_scores_q[qsl] = ctx.n["pe"]
                        ev_sevac = ctx.n["se"]
                        ctx.w(v, "se", ev_sevac)
                        ctx.inc(v.tensor_tensor(
                            s_sb[:, qsl, (nv - 1) * 512:nv * 512],
                            s_sb[:, qsl, (nv - 1) * 512:nv * 512],
                            masks[:, qt % 4, :], ALU.add), "ve")
                        ev_mask_slot[qsl] = ctx.n["ve"]
                        ctx.w(v, "ve")
                        ctx.inc(v.reduce_max(mcol[:, qsl:qsl + 1],
                                             s_sb[:, qsl, 0:nv * 512], axis=AX.X), "ve")
                        ctx.w(v, "ve")
                        if ev_exp_slot[qsl]:
                            ctx.w(v, "se", ev_exp_slot[qsl])
                        ctx.inc(v.tensor_scalar(negm[:, qsl:qsl + 1], mcol[:, qsl:qsl + 1],
                                                -1.0, None, ALU.mult), "ve")
                        ev_rmax = ctx.n["ve"]
                        ctx.w(s, "ve", ev_rmax)
                        if ev_pt_slot[qsl]:
                            ctx.w(s, f"pt{qsl}", ev_pt_slot[qsl])
                        if ev_recip_slot[qsl]:
                            ctx.w(s, "ve", ev_recip_slot[qsl])
                        ev_exp_slot[qsl] = ctx.inc(
                            s.activation(p_sb[:, qsl, 0:nv * 512],
                                         s_sb[:, qsl, 0:nv * 512], AF.Exp,
                                         bias=negm[:, qsl:qsl + 1],
                                         accum_out=lcol[:, qsl:qsl + 1]), "se")
                        ctx.w(v, "se", ev_exp_slot[qsl])
                        ctx.inc(v.reciprocal(rlcol[:, qsl:qsl + 1], lcol[:, qsl:qsl + 1]),
                                "ve")
                        ev_recip_slot[qsl] = ctx.n["ve"]
                        ctx.w(sy, "se", ev_exp_slot[qsl])
                        if ev_pv_slot[qsl]:
                            ctx.w(sy, "pe", ev_pv_slot[qsl])
                        ctx.dinc(sy.dma_start(pt_sb[:, qsl, 0:nv * 4, :],
                                              p_sb[:, qsl, 0:nv * 512], transpose=True),
                                 f"pt{qsl}")
                        ev_pt_slot[qsl] = ctx.n[f"pt{qsl}"]

                    def emit_back(h, qt, hsl):
                        qsl = qt % 2
                        nkc = qt + 1
                        ctx.w(te, f"pt{qsl}", ev_pt_slot[qsl])
                        bank = qsl
                        if ev_pvbank[bank]:
                            ctx.w(te, "ve", ev_pvbank[bank])
                        for kc in range(nkc):
                            mm = te.matmul(ps_pv[bank][:], pt_sb[:, qsl, kc, :],
                                           v_h[:, hsl, kc, :],
                                           start=(kc == 0), stop=(kc == nkc - 1))
                        i_pv = ctx.inc(mm, "pe")
                        ev_pv_slot[qsl] = ctx.n["pe"]
                        ctx.w(v, "pe", i_pv)
                        ctx.w(v, "ve", ev_recip_slot[qsl])
                        ctx.inc(v.tensor_scalar(o_full[:, qt, h * 64:(h + 1) * 64],
                                                ps_pv[bank][:], rlcol[:, qsl:qsl + 1],
                                                None, ALU.mult), "ve")
                        ev_pvbank[bank] = ctx.n["ve"]

                    for h in range(16):
                        hsl = h % 2
                        if h >= 2:
                            ctx.w(sy, "pe", ev_kv[hsl])
                        ctx.dinc(sy.dma_start(k_h[:, hsl, :],
                                              ksp[h * 64:(h + 1) * 64, :]), f"kh{hsl}")
                        ctx.dinc(sy.dma_start(
                            v_h[:, hsl, :, :],
                            vsp[:, h * 64:(h + 1) * 64].rearrange("(i p) d -> p i d", p=128)),
                            f"kh{hsl}")
                        ctx.w(te, f"kh{hsl}")
                        for qt in range(NT):
                            emit_front(h, qt, hsl)
                            if qt >= 1:
                                emit_back(h, qt - 1, hsl)
                        emit_back(h, NT - 1, hsl)
                        ev_kv[hsl] = ctx.n["pe"]

        with ExitStack() as esC:
            es = esC
            oT = sbo("oT", [128, NDC, 128], BF16)
            xold = sbo("xold", [128, 1, D], F32)
            onew = sbo("onew", [128, 2, D], F32)
            ps_c = es.enter_context(nc.psum_tensor(pfx + "psc", [128, 1024], F32))
            with nc.Block() as blk:
                @blk.gpsimd
                def _(gp):
                    ctx.w(te, "ve")
                    ctx.w(sy, "ve")
                    ev_stt = 0
                    ev_so_slot = [0, 0]
                    ev_cmm = 0
                    for qt in range(NT):
                        sl = qt % 2
                        if ev_cmm:
                            ctx.w(sy, "pe", ev_cmm)
                        ctx.dinc(sy.dma_start(oT[:], o_full[:, qt, :], transpose=True),
                                 "pt0")
                        ctx.w(te, "pt0")
                        if ev_stt:
                            ctx.w(te, "ve", ev_stt)
                        for nh in range(2):
                            mm = te.matmul(ps_c[:, nh * 512:(nh + 1) * 512], ones16,
                                           borow[0:1, nh * 512:(nh + 1) * 512],
                                           start=True, stop=False)
                            for dc in range(NDC):
                                mm = te.matmul(ps_c[:, nh * 512:(nh + 1) * 512],
                                               oT[:, dc, :],
                                               wo_w[:, dc, nh * 512:(nh + 1) * 512],
                                               start=False, stop=(dc == NDC - 1))
                        i_c = ctx.inc(mm, "pe")
                        ev_cmm = ctx.n["pe"]
                        if ev_stt:
                            ctx.w(sy, "ve", ev_stt)
                        ctx.dinc(sy.dma_start(xold[:, 0, :],
                                              xin[qt * 128:(qt + 1) * 128, :]), "xo0")
                        ctx.w(v, "pe", i_c)
                        ctx.w(v, "xo0")
                        if ev_so_slot[sl]:
                            ctx.w(v, f"so{sl}", ev_so_slot[sl])
                        ins = v.scalar_tensor_tensor(onew[:, sl, :], ps_c[:], 1.0,
                                                     xold[:, 0, :], ALU.mult, ALU.add)
                        ctx.inc(ins, "ve")
                        ev_stt = ctx.n["ve"]
                        ctx.w(sy, "ve", ev_stt)
                        ctx.dinc(sy.dma_start(xout[qt * 128:(qt + 1) * 128, :],
                                              onew[:, sl, :]), f"so{sl}")
                        ev_so_slot[sl] = ctx.n[f"so{sl}"]
    ctx.barrier()



def _prep_weights(inp):
    f32 = lambda a: np.ascontiguousarray(np.asarray(a), dtype=np.float32)
    W = {}
    W["l0a_QT"] = f32(inp["L0_attn_a_Q"].T)
    W["l0a_KT"] = f32(inp["L0_attn_a_K"].T)
    W["l0a_VO"] = f32(inp["L0_attn_a_VO"])
    W["l0f_wiT"] = f32(inp["L0_ffn_wi_W"].T)
    W["l0f_woT"] = f32(inp["L0_ffn_wo_W"].T)
    W["l0f_gateT"] = f32(inp["L0_ffn_gate_W"].T)
    W["l0f_gate_b"] = f32(inp["L0_ffn_gate_b"]).reshape(1, 8)
    A = np.asarray(inp["L0_ffn_lora_A"])
    W["l0f_AallT"] = f32(A.transpose(2, 0, 1).reshape(1024, 32))
    Bm = np.asarray(inp["L0_ffn_lora_B"])
    W["l0f_BallT"] = f32(Bm.transpose(0, 2, 1).reshape(32, 4096))
    W["l0f_wib_cm"] = f32(np.asarray(inp["L0_ffn_wi_b"]).reshape(32, 128).T)
    W["l0f_wo_b"] = f32(inp["L0_ffn_wo_b"]).reshape(1, 1024)
    W["l0b_qkW"] = f32(np.concatenate([np.asarray(inp["L0_attn_b_Wq"]),
                                       np.asarray(inp["L0_attn_b_Wk"])], axis=1))
    W["l0b_wv"] = f32(inp["L0_attn_b_Wv"])
    W["l0b_wo"] = f32(inp["L0_attn_b_Wo"])
    bqk = np.concatenate([np.asarray(inp["L0_attn_b_bq"]),
                          np.asarray(inp["L0_attn_b_bk"])])
    W["l0b_bqk_cm"] = f32(bqk.reshape(16, 128).T)
    W["l0b_bv"] = f32(inp["L0_attn_b_bv"]).reshape(1, 1024)
    W["l0b_bo"] = f32(inp["L0_attn_b_bo"]).reshape(1, 1024)
    W["l1a_rot"] = f32(inp["L1_attn_a_rot_w"])
    W["l1a_scb"] = f32(np.stack([np.asarray(inp["L1_attn_a_scale"]).reshape(()),
                                 np.asarray(inp["L1_attn_a_bias"]).reshape(())])).reshape(1, 2)
    W["l1f_g1r"] = f32(inp["L1_ffn_g1"]).reshape(1, 1024)
    W["l1f_be1r"] = f32(inp["L1_ffn_be1"]).reshape(1, 1024)
    W["l1f_mixw"] = f32(inp["L1_ffn_mix_w"])
    W["l1f_mixb"] = f32(inp["L1_ffn_mix_b"]).reshape(1, 1024)
    W["l1f_g2r"] = f32(inp["L1_ffn_g2"]).reshape(1, 1024)
    W["l1f_be2r"] = f32(inp["L1_ffn_be2"]).reshape(1, 1024)
    W["l1f_routw"] = f32(inp["L1_ffn_router_w"])
    W["l1f_w1"] = f32(inp["L1_ffn_w1"])
    W["l1f_w2"] = f32(inp["L1_ffn_w2"])
    b1 = np.asarray(inp["L1_ffn_b1"])
    W["l1f_b1cm"] = f32(np.stack([b1[e].reshape(32, 128).T for e in range(8)]))
    W["l1f_b2"] = f32(inp["L1_ffn_b2"])
    W["l1b_qkvT"] = f32(np.asarray(inp["L1_attn_b_qkv_w"]).T)
    W["l1b_qkvb"] = f32(inp["L1_attn_b_qkv_b"]).reshape(1, 3072)
    W["l1b_woT"] = f32(np.asarray(inp["L1_attn_b_out_w"]).T)
    W["l1b_ob"] = f32(inp["L1_attn_b_out_b"]).reshape(1, 1024)
    half = 32
    inv_freq = 1.0 / 10000.0 ** (np.arange(half, dtype=np.float64) / half)
    pos = np.arange(4096, dtype=np.float64)
    ang = np.outer(pos, inv_freq)
    W["l1b_cos"] = np.cos(ang).astype(np.float32)
    W["l1b_sin"] = np.sin(ang).astype(np.float32)
    return W


SUBLAYERS = [emit_l0a, emit_l0ffn, emit_l0b, emit_l0ffn, emit_l1a, emit_l1ffn, emit_l1b, emit_l1ffn]


def build_program(W_shapes, sublayers=None):
    sublayers = sublayers or SUBLAYERS
    nc = bass.Bass("TRN2", target_bir_lowering=False)
    xin = nc.dram_tensor("x_in", [T, D], F32, kind="ExternalInput").ap()
    xout = nc.dram_tensor("x_out", [T, D], F32, kind="ExternalOutput").ap()
    W = {name: nc.dram_tensor(name, list(shape), F32, kind="ExternalInput").ap()
         for name, shape in W_shapes.items()}
    with ExitStack() as es:
        ctx = Ctx(nc, es)
        streams = [xin]
        for i in range(len(sublayers) - 1):
            streams.append(nc.dram_tensor(f"x_mid{i}", [T, D], F32, kind="Internal").ap())
        streams.append(xout)
        emit_consts(ctx)
        for i, emit in enumerate(sublayers):
            emit(ctx, streams[i], streams[i + 1], W)
    return nc


def kernel(**inputs):
    x = np.ascontiguousarray(inputs["L0_attn_a_x"], dtype=np.float32)
    W = _prep_weights(inputs)
    nc = build_program({k: v.shape for k, v in W.items()})
    in_maps = []
    for c in range(8):
        m = {"x_in": np.ascontiguousarray(x[c])}
        m.update(W)
        in_maps.append(m)
    res = run_bass_kernel_spmd(nc, in_maps, core_ids=list(range(8)))
    return np.stack([r["x_out"] for r in res.results], axis=0)


# version 03 (final)
# speedup vs baseline: 1.0068x; 1.0068x over previous
import numpy as np
from contextlib import ExitStack

import concourse.bass as bass
import concourse.mybir as mybir
from concourse.bass_utils import run_bass_kernel_spmd

F32 = mybir.dt.float32
BF16 = mybir.dt.bfloat16
AF = mybir.ActivationFunctionType
ALU = mybir.AluOpType
AX = mybir.AxisListType

T = 4096
D = 1024
NT = T // 128
NDC = D // 128
NKT = T // 512


class Ctx:

    def __init__(self, nc, es):
        self.nc = nc
        self.es = es
        self._sems = {}
        self.n = {}

    def uid(self):
        self._uid = getattr(self, "_uid", 0) + 1
        return self._uid

    def sem(self, name):
        if name not in self._sems:
            self._sems[name] = self.es.enter_context(self.nc.semaphore(name))
            self.n[name] = 0
        return self._sems[name]

    def inc(self, instr, name, step=1):
        instr.then_inc(self.sem(name), step)
        self.n[name] += step
        return self.n[name]

    def dinc(self, instr, name):
        return self.inc(instr, name, 16)

    def w(self, eng, name, val=None):
        v = self.n.setdefault(name, 0) if val is None else val
        self.sem(name)
        if v > 0:
            eng.wait_ge(self._sems[name], v)

    def barrier(self):
        nc = self.nc
        with nc.Block() as blk:
            @blk.gpsimd
            def _(gp):
                snap = dict(self.n)
                for eng in [nc.tensor, nc.vector, nc.scalar, nc.gpsimd, nc.sync]:
                    for name, val in snap.items():
                        if val > 0:
                            eng.wait_ge(self._sems[name], val)



def emit_ln_tile(ctx, xs_ap, xln_ap, junk_ap, stat):
    nc = ctx.nc
    v, s = nc.vector, nc.scalar
    i_sq = ctx.inc(s.activation(junk_ap, xs_ap, AF.Square, accum_out=stat["Q"]), "se")
    ctx.inc(v.reduce_sum(stat["S"], xs_ap, axis=AX.X), "ve")
    ctx.w(v, "ve")
    ctx.inc(v.tensor_scalar(stat["negmean"], stat["S"], -1.0 / D, None, ALU.mult), "ve")
    ctx.w(v, "ve")
    ctx.w(v, "se", i_sq)
    ctx.inc(v.tensor_scalar(stat["varr"], stat["Q"], 1.0 / D, 1e-5, ALU.mult, ALU.add), "ve")
    ctx.w(v, "ve")
    ctx.inc(v.tensor_tensor(stat["mm"], stat["negmean"], stat["negmean"], ALU.mult), "ve")
    ctx.w(v, "ve")
    ctx.inc(v.tensor_tensor(stat["varr"], stat["varr"], stat["mm"], ALU.subtract), "ve")
    i_var = ctx.n["ve"]
    ctx.w(s, "ve", i_var)
    i_sqrt = ctx.inc(s.activation(stat["rstd"], stat["varr"], AF.Sqrt), "se")
    ctx.w(v, "se", i_sqrt)
    ctx.inc(v.reciprocal(stat["rstd"], stat["rstd"]), "ve")
    ctx.w(v, "ve")
    ctx.inc(v.tensor_scalar(xln_ap, xs_ap, stat["negmean"], stat["rstd"],
                            ALU.add, ALU.mult), "ve")
    return ctx.n["ve"]


def alloc_stats(nc, es, prefix, nslots=2):
    out = []
    for sl in range(nslots):
        d = {}
        for nm in ["S", "Q", "negmean", "varr", "mm", "rstd"]:
            t = es.enter_context(nc.sbuf_tensor(f"{prefix}_{nm}{sl}", [128, 1], F32))
            d[nm] = t[:]
        out.append(d)
    return out



def emit_l0a(ctx, xin, xout, W):
    nc = ctx.nc
    te, v, s, g, sy = nc.tensor, nc.vector, nc.scalar, nc.gpsimd, nc.sync
    pfx = f"a{ctx.uid()}_"
    with ExitStack() as es:
        def sb(name, shape, dt):
            return es.enter_context(nc.sbuf_tensor(pfx + name, shape, dt))

        qtw = sb("qtw", [128, NDC, 64], BF16)
        ktw = sb("ktw", [128, NDC, 64], BF16)
        vow = sb("vow", [128, NDC, D], BF16)
        xqT = sb("xqT", [64, T], BF16)
        xkT = sb("xkT", [64, T], BF16)
        xv = sb("xv", [128, NT, D], BF16)
        xs = sb("xs", [128, 2, D], F32)
        xln = sb("xln", [128, 2, D], BF16)
        xlnT = sb("xlnT", [128, NDC, 512], BF16)
        junk = sb("junk", [128, D], BF16)
        stats = alloc_stats(nc, es, pfx, 2)
        s_sb = sb("s_sb", [128, T], F32)
        p_sb = sb("p_sb", [128, T], BF16)
        pt_sb = sb("pt", [128, NT, 128], BF16)
        mcol = sb("mcol", [128, 1], F32)
        negm = sb("negm", [128, 1], F32)
        lcol = sb("lcol", [128, 1], F32)
        rlcol = sb("rlcol", [128, 1], F32)
        xold = sb("xold", [128, 2, D], F32)
        onew = sb("onew", [128, 2, D], F32)
        ps_q = es.enter_context(nc.psum_tensor(pfx + "psq", [64, 512], F32))
        ps_k = es.enter_context(nc.psum_tensor(pfx + "psk", [64, 512], F32))
        ps_v = [es.enter_context(nc.psum_tensor(f"{pfx}psv{i}", [128, 512], F32))
                for i in range(2)]
        ps_s = [es.enter_context(nc.psum_tensor(f"{pfx}pss{i}", [128, 512], F32))
                for i in range(2)]
        ps_o = es.enter_context(nc.psum_tensor(pfx + "pso", [128, 1024], F32))

        with nc.Block() as blk:
            @blk.gpsimd
            def _(gp):
                ctx.dinc(g.dma_start(qtw[:], W["l0a_QT"].rearrange("(c p) r -> p c r", p=128)), "dg")
                ctx.dinc(g.dma_start(ktw[:], W["l0a_KT"].rearrange("(c p) r -> p c r", p=128)), "dg")
                ctx.dinc(g.dma_start(vow[:], W["l0a_VO"].rearrange("(c p) n -> p c n", p=128)), "dg")
                w_ready = ctx.n["dg"]

                ev_norm = [0] * NT
                ev_tp = [0] * NT
                ev_gproj = [0] * (NT // 4)
                ev_vevac = [0] * NT
                ev_vbank = [0, 0]
                ev_qkevac = 0
                vuse = 0

                for gi in range(NT // 4):
                    for ii in range(4):
                        i = gi * 4 + ii
                        sl = i % 2
                        if i >= 2:
                            ctx.w(sy, "ve", ev_norm[i - 2])
                        ctx.dinc(sy.dma_start(xs[:, sl, :],
                                              xin[i * 128:(i + 1) * 128, :]), f"sx{sl}")
                        ctx.w(v, f"sx{sl}")
                        ctx.w(s, f"sx{sl}")
                        if i >= 2:
                            ctx.w(v, f"tp{sl}", ev_tp[i - 2])
                        ev_norm[i] = emit_ln_tile(ctx, xs[:, sl, :], xln[:, sl, :],
                                                  junk[:], stats[sl])
                        ctx.w(sy, "ve", ev_norm[i])
                        if gi >= 1 and ii == 0:
                            ctx.w(sy, "pe", ev_gproj[gi - 1])
                        ctx.dinc(sy.dma_start(xlnT[:, :, ii * 128:(ii + 1) * 128],
                                              xln[:, sl, :], transpose=True), f"tp{sl}")
                        ev_tp[i] = ctx.n[f"tp{sl}"]

                    ctx.w(te, "tp0", ev_tp[gi * 4 + 2])
                    ctx.w(te, "tp1", ev_tp[gi * 4 + 3])
                    if gi == 0:
                        ctx.w(te, "dg", w_ready)
                    if gi >= 1:
                        ctx.w(te, "se", ev_qkevac)
                    for dc in range(NDC):
                        mm = te.matmul(ps_q[:], qtw[:, dc, :], xlnT[:, dc, :],
                                       start=(dc == 0), stop=(dc == NDC - 1))
                    i_q = ctx.inc(mm, "pe")
                    for dc in range(NDC):
                        mm = te.matmul(ps_k[:], ktw[:, dc, :], xlnT[:, dc, :],
                                       start=(dc == 0), stop=(dc == NDC - 1))
                    i_k = ctx.inc(mm, "pe")
                    for ii in range(4):
                        i = gi * 4 + ii
                        for nh in range(2):
                            bank = vuse % 2
                            vuse += 1
                            if ev_vbank[bank]:
                                ctx.w(te, "ve", ev_vbank[bank])
                            for dc in range(NDC):
                                mm = te.matmul(ps_v[bank][:],
                                               xlnT[:, dc, ii * 128:(ii + 1) * 128],
                                               vow[:, dc, nh * 512:(nh + 1) * 512],
                                               start=(dc == 0), stop=(dc == NDC - 1))
                            i_v = ctx.inc(mm, "pe")
                            ctx.w(v, "pe", i_v)
                            ctx.inc(v.tensor_copy(xv[:, i, nh * 512:(nh + 1) * 512],
                                                  ps_v[bank][:]), "ve")
                            ev_vbank[bank] = ctx.n["ve"]
                            ev_vevac[i] = ctx.n["ve"]
                    ev_gproj[gi] = ctx.n["pe"]
                    ctx.w(s, "pe", i_q)
                    ctx.inc(s.copy(xqT[:, gi * 512:(gi + 1) * 512], ps_q[:]), "se")
                    ctx.w(s, "pe", i_k)
                    ctx.inc(s.copy(xkT[:, gi * 512:(gi + 1) * 512], ps_k[:]), "se")
                    ev_qkevac = ctx.n["se"]

                ev_rmax = 0
                ev_exp = 0
                ev_pt = 0
                ev_stt = 0
                ev_recip = 0
                ev_pv = 0
                ev_ssbank = [0, 0]
                ev_stt_slot = [0, 0]
                ev_so_slot = [0, 0]
                for qt in range(NT):
                    sl = qt % 2
                    for kt in range(NKT):
                        bank = kt % 2
                        if ev_ssbank[bank]:
                            ctx.w(te, "se", ev_ssbank[bank])
                        if qt == 0 and kt == 0:
                            ctx.w(te, "se", ev_qkevac)
                        mm = te.matmul(ps_s[bank][:], xqT[:, qt * 128:(qt + 1) * 128],
                                       xkT[:, kt * 512:(kt + 1) * 512],
                                       start=True, stop=True)
                        i_s = ctx.inc(mm, "pe")
                        ctx.w(s, "pe", i_s)
                        if qt >= 1 and kt == 0:
                            ctx.w(s, "ve", ev_rmax)
                        ctx.inc(s.activation(s_sb[:, kt * 512:(kt + 1) * 512],
                                             ps_s[bank][:], AF.Copy, scale=1.0 / 32.0),
                                "se")
                        ev_ssbank[bank] = ctx.n["se"]
                    ev_sevac = ctx.n["se"]
                    ctx.w(v, "se", ev_sevac)
                    ctx.inc(v.reduce_max(mcol[:], s_sb[:], axis=AX.X), "ve")
                    ctx.w(v, "ve")
                    if qt >= 1:
                        ctx.w(v, "se", ev_exp)
                    ctx.inc(v.tensor_scalar(negm[:], mcol[:], -1.0, None, ALU.mult), "ve")
                    ev_rmax = ctx.n["ve"]
                    ctx.w(s, "ve", ev_rmax)
                    if qt >= 1:
                        ctx.w(s, "pt0", ev_pt)
                        ctx.w(s, "ve", ev_recip)
                    ev_exp = ctx.inc(s.activation(p_sb[:], s_sb[:], AF.Exp, bias=negm[:],
                                                  accum_out=lcol[:]), "se")
                    ctx.w(v, "se", ev_exp)
                    ctx.inc(v.reciprocal(rlcol[:], lcol[:]), "ve")
                    ev_recip = ctx.n["ve"]
                    ctx.w(sy, "se", ev_exp)
                    if qt >= 1:
                        ctx.w(sy, "pe", ev_pv)
                    ctx.dinc(sy.dma_start(pt_sb[:], p_sb[:], transpose=True), "pt0")
                    ev_pt = ctx.n["pt0"]
                    ctx.w(te, "pt0", ev_pt)
                    if qt >= 1:
                        ctx.w(te, "ve", ev_stt)
                    if qt == 0:
                        ctx.w(te, "ve", ev_vevac[NT - 1])
                    for nh in range(2):
                        for kc in range(NT):
                            mm = te.matmul(ps_o[:, nh * 512:(nh + 1) * 512],
                                           pt_sb[:, kc, :],
                                           xv[:, kc, nh * 512:(nh + 1) * 512],
                                           start=(kc == 0), stop=(kc == NT - 1))
                        ctx.inc(mm, "pe")
                    ev_pv = ctx.n["pe"]
                    if qt >= 2:
                        ctx.w(sy, "ve", ev_stt_slot[sl])
                    ctx.dinc(sy.dma_start(xold[:, sl, :],
                                          xin[qt * 128:(qt + 1) * 128, :]), f"xo{sl}")
                    ctx.w(v, "pe", ev_pv)
                    ctx.w(v, f"xo{sl}")
                    ctx.w(v, "ve", ev_recip)
                    if qt >= 2:
                        ctx.w(v, f"so{sl}", ev_so_slot[sl])
                    for nh in range(2):
                        ins = v.scalar_tensor_tensor(
                            onew[:, sl, nh * 512:(nh + 1) * 512],
                            ps_o[:, nh * 512:(nh + 1) * 512], rlcol[:],
                            xold[:, sl, nh * 512:(nh + 1) * 512], ALU.mult, ALU.add)
                    ctx.inc(ins, "ve")
                    ev_stt = ctx.n["ve"]
                    ev_stt_slot[sl] = ev_stt
                    ctx.w(sy, "ve", ev_stt)
                    ctx.dinc(sy.dma_start(xout[qt * 128:(qt + 1) * 128, :],
                                          onew[:, sl, :]), f"so{sl}")
                    ev_so_slot[sl] = ctx.n[f"so{sl}"]
    ctx.barrier()




def emit_consts(ctx):
    nc = ctx.nc
    es = ctx.es
    C = {}
    C["ones16"] = es.enter_context(nc.sbuf_tensor("c_ones16", [1, 128], BF16))
    C["ones32"] = es.enter_context(nc.sbuf_tensor("c_ones32", [1, 128], F32))
    C["ident32"] = es.enter_context(nc.sbuf_tensor("c_id32", [128, 128], F32))
    C["ident16"] = es.enter_context(nc.sbuf_tensor("c_id16", [128, 128], BF16))
    C["w8"] = es.enter_context(nc.sbuf_tensor("c_w8", [128, 8], F32))
    ctx.consts = C
    g = nc.gpsimd
    with nc.Block() as blk:
        @blk.gpsimd
        def _(gp):
            ctx.inc(g.memset(C["ones16"][:], 1.0), "po")
            ctx.inc(g.memset(C["ones32"][:], 1.0), "po")
            ctx.inc(g.memset(C["ident32"][:], 0.0), "po")
            ctx.w(g, "po")
            ctx.inc(g.affine_select(C["ident32"][:], C["ident32"][:], [[-1, 128]],
                                    ALU.not_equal, 1.0, base=0, channel_multiplier=1), "po")
            ctx.inc(g.memset(C["ident16"][:], 0.0), "po")
            ctx.w(g, "po")
            ctx.inc(g.affine_select(C["ident16"][:], C["ident16"][:], [[-1, 128]],
                                    ALU.not_equal, 1.0, base=0, channel_multiplier=1), "po")
            ctx.inc(g.iota(C["w8"][:], [[-1, 8]], base=8, channel_multiplier=0,
                           allow_small_or_imprecise_dtypes=True), "po")
    ctx.barrier()


def alloc_topk(nc, es, prefix):
    B = {}
    for nm, shape in [("m", [128, 1]), ("negm", [128, 1]), ("l", [128, 1]),
                      ("rl", [128, 1]), ("s8", [128, 1]), ("m1", [128, 1]),
                      ("m2", [128, 1]), ("e8", [128, 8]), ("pr", [128, 8]),
                      ("eq", [128, 8]), ("t8", [128, 8]), ("raw", [128, 8]),
                      ("lg2", [128, 8])]:
        t = es.enter_context(nc.sbuf_tensor(f"{prefix}_tk_{nm}", shape, F32))
        B[nm] = t[:]
    return B


def emit_probs(ctx, lg_ap, B):
    nc = ctx.nc
    v, s = nc.vector, nc.scalar
    ctx.w(v, "se")
    ctx.w(s, "ve")
    ctx.inc(v.reduce_max(B["m"], lg_ap, axis=AX.X), "ve")
    ctx.w(v, "ve")
    ctx.inc(v.tensor_scalar(B["negm"], B["m"], -1.0, None, ALU.mult), "ve")
    ctx.w(s, "ve")
    ctx.inc(s.activation(B["e8"], lg_ap, AF.Exp, bias=B["negm"], accum_out=B["l"]), "se")
    ctx.w(v, "se")
    ctx.inc(v.reciprocal(B["rl"], B["l"]), "ve")
    ctx.w(v, "ve")
    ctx.inc(v.tensor_scalar(B["pr"], B["e8"], B["rl"], None, ALU.mult), "ve")
    ctx.w(v, "ve")
    return ctx.n["ve"]


def emit_argmax_mask(ctx, src_ap, mask_ap, B, save_max=None):
    nc = ctx.nc
    v, s = nc.vector, nc.scalar
    w8 = ctx.consts["w8"][:]
    ctx.inc(v.reduce_max(B["m"], src_ap, axis=AX.X), "ve")
    ctx.w(v, "ve")
    if save_max is not None:
        ctx.inc(v.tensor_copy(save_max, B["m"]), "ve")
    ctx.inc(v.tensor_scalar(B["negm"], B["m"], -1.0, None, ALU.mult), "ve")
    ctx.w(s, "ve")
    ctx.inc(s.activation(B["eq"], src_ap, AF.Sign, bias=B["negm"]), "se")
    ctx.w(v, "se")
    ctx.inc(v.tensor_scalar(B["eq"], B["eq"], 1.0, None, ALU.add), "ve")
    ctx.w(v, "ve")
    ctx.inc(v.tensor_tensor(B["t8"], B["eq"], w8, ALU.mult), "ve")
    ctx.w(v, "ve")
    ctx.inc(v.reduce_max(B["s8"], B["t8"], axis=AX.X), "ve")
    ctx.w(v, "ve")
    ctx.inc(v.tensor_scalar(B["s8"], B["s8"], -1.0, None, ALU.mult), "ve")
    ctx.w(s, "ve")
    ctx.inc(s.activation(B["raw"], B["t8"], AF.Sign, bias=B["s8"]), "se")
    ctx.w(v, "se")
    ctx.inc(v.tensor_scalar(mask_ap, B["raw"], 1.0, None, ALU.add), "ve")
    ctx.w(v, "ve")
    return ctx.n["ve"]



GRP = 256


def emit_l0ffn(ctx, xin, xout, W, dbg=None):
    nc = ctx.nc
    te, v, s, g, sy = nc.tensor, nc.vector, nc.scalar, nc.gpsimd, nc.sync
    pfx = f"f{ctx.uid()}_"
    NG = T // GRP
    TPG = GRP // 128
    with ExitStack() as es:
        def sb(name, shape, dt):
            return es.enter_context(nc.sbuf_tensor(pfx + name, shape, dt))

        wiw = sb("wiw", [128, NDC, 4096], BF16)
        wow = sb("wow", [128, 32, 1024], BF16)
        gtw = sb("gtw", [128, NDC, 8], F32)
        gbrow = sb("gbrow", [1, 8], F32)
        atw = sb("atw", [128, NDC, 32], BF16)
        btw = sb("btw", [32, 4096], BF16)
        wib = sb("wib", [128, 32], F32)
        wob = sb("wob", [1, 1024], BF16)
        xs = sb("xs", [128, 2, D], F32)
        xln32 = sb("xln32", [128, 2, D], F32)
        xln16 = sb("xln16", [128, 2, D], BF16)
        xlnT = sb("xlnT", [128, NDC, GRP], BF16)
        xlnT32 = sb("xlnT32", [128, NDC, 128], F32)
        junk = sb("junk", [128, D], BF16)
        stats = alloc_stats(nc, es, pfx, 2)
        tkB = alloc_topk(nc, es, pfx)
        lg = sb("lg", [128, 8], F32)
        mask = sb("mask", [128, 2, 8], F32)
        u_m = sb("u_m", [128, 2, 32], BF16)
        utT = sb("utT", [32, GRP], BF16)
        h_sb = sb("h", [128, 32, GRP], BF16)
        xold = sb("xold", [128, 1, D], F32)
        onew = sb("onew", [128, 2, D], F32)
        ps_g = es.enter_context(nc.psum_tensor(pfx + "psg", [128, 8], F32))
        ps_u = es.enter_context(nc.psum_tensor(pfx + "psu", [128, 32], F32))
        ps_t = [es.enter_context(nc.psum_tensor(pfx + "pst0", [128, 128], F32))]
        ps_ut = es.enter_context(nc.psum_tensor(pfx + "psut", [32, 128], BF16))
        ps_h = [es.enter_context(nc.psum_tensor(f"{pfx}psh{i}", [128, GRP], F32))
                for i in range(2)]
        ps_o = es.enter_context(nc.psum_tensor(pfx + "pso", [128, 1024], F32))

        ones16 = ctx.consts["ones16"][:]
        ones32 = ctx.consts["ones32"][:]
        id32 = ctx.consts["ident32"][:]
        id16 = ctx.consts["ident16"][:]

        with nc.Block() as blk:
            @blk.gpsimd
            def _(gp):
                ctx.dinc(g.dma_start(wiw[:], W["l0f_wiT"].rearrange("(c p) f -> p c f", p=128)), "dg")
                ctx.dinc(g.dma_start(wow[:], W["l0f_woT"].rearrange("(c p) f -> p c f", p=128)), "dg")
                ctx.dinc(g.dma_start(gtw[:], W["l0f_gateT"].rearrange("(c p) e -> p c e", p=128)), "dg")
                ctx.dinc(g.dma_start(gbrow[:], W["l0f_gate_b"]), "dg")
                ctx.dinc(g.dma_start(atw[:], W["l0f_AallT"].rearrange("(c p) r -> p c r", p=128)), "dg")
                ctx.dinc(g.dma_start(btw[:], W["l0f_BallT"]), "dg")
                ctx.dinc(g.dma_start(wib[:], W["l0f_wib_cm"]), "dg")
                ctx.dinc(g.dma_start(wob[:], W["l0f_wo_b"]), "dg")
                w_ready = ctx.n["dg"]

                ev_norm = [0] * NT
                ev_tp = [0] * NT
                ev_t32 = [0] * NT
                ev_cast = [0, 0]
                ev_gmm = 0
                ev_lg_free = 0
                ev_wo = 0
                ev_gbase = 0
                ev_g32 = 0
                ev_hbank = [0, 0]
                ev_tbank = [0, 0]
                ev_stt_slot = [0, 0]
                ev_so_slot = [0, 0]
                ev_stt = 0
                ev_g_evac = 0
                ev_um = [0, 0]
                ev_ut = 0
                ev_mask = [0, 0]
                first = True

                for gi in range(NG):
                    for ii in range(TPG):
                        i = gi * TPG + ii
                        sl = i % 2
                        if i >= 2:
                            ctx.w(sy, "ve", ev_norm[i - 2])
                        ctx.dinc(sy.dma_start(xs[:, sl, :],
                                              xin[i * 128:(i + 1) * 128, :]), f"sx{sl}")
                        ctx.w(v, f"sx{sl}")
                        ctx.w(s, f"sx{sl}")
                        if i >= 2:
                            ctx.w(v, "se", ev_cast[sl])
                            ctx.w(v, "ve", ev_t32[i - 2])
                            ctx.w(v, f"tp{sl}", ev_tp[i - 2])
                        ev_norm[i] = emit_ln_tile(ctx, xs[:, sl, :], xln32[:, sl, :],
                                                  junk[:], stats[sl])
                        ctx.w(s, "ve", ev_norm[i])
                        ev_cast[sl] = ctx.inc(s.copy(xln16[:, sl, :], xln32[:, sl, :]), "se")
                        ctx.w(sy, "se", ev_cast[sl])
                        if gi >= 1 and ii == 0:
                            ctx.w(sy, "pe", ev_gbase)
                        ctx.dinc(sy.dma_start(xlnT[:, :, ii * 128:(ii + 1) * 128],
                                              xln16[:, sl, :], transpose=True), f"tp{sl}")
                        ev_tp[i] = ctx.n[f"tp{sl}"]
                        if first:
                            ctx.w(te, "dg", w_ready)
                            ctx.w(te, "po", ctx.n["po"])
                            first = False
                        ctx.w(te, "ve", ev_norm[i])
                        for dc in range(NDC):
                            bank = 0
                            if ev_tbank[bank]:
                                ctx.w(te, "ve", ev_tbank[bank])
                            i_t = ctx.inc(te.transpose(ps_t[bank][:],
                                                       xln32[:, sl, dc * 128:(dc + 1) * 128],
                                                       id32), "pe")
                            ctx.w(v, "pe", i_t)
                            if ev_gmm:
                                ctx.w(v, "pe", ev_gmm)
                            ctx.inc(v.tensor_copy(xlnT32[:, dc, :],
                                                  ps_t[bank][:]), "ve")
                            ev_tbank[bank] = ctx.n["ve"]
                        ev_t32[i] = ctx.n["ve"]
                        if dbg is not None and len(dbg) > 3:
                            ctx.w(sy, "ve", ev_t32[i])
                            ctx.dinc(sy.dma_start(dbg[3][i * 128:(i + 1) * 128, :],
                                                  xlnT32[:, 0, :]), "sdbg")
                            ctx.w(sy, "pe", ctx.n["pe"])

                        ctx.w(te, "ve", ev_t32[i])
                        if ev_g_evac:
                            ctx.w(te, "ve", ev_g_evac)
                        mm = te.matmul(ps_g[:], ones32, gbrow[:], start=True, stop=False)
                        for dc in range(NDC):
                            mm = te.matmul(ps_g[:], xlnT32[:, dc, :],
                                           gtw[:, dc, :], start=False, stop=(dc == NDC - 1))
                        i_g = ctx.inc(mm, "pe")
                        ev_gmm = i_g
                        ctx.w(v, "pe", i_g)
                        if ev_lg_free:
                            ctx.w(v, "se", ev_lg_free)
                        ctx.inc(v.tensor_copy(lg[:], ps_g[:]), "ve")
                        ev_g_evac = ctx.n["ve"]
                        ctx.w(v, "ve")
                        emit_probs(ctx, lg[:], tkB)
                        ev_lg_free = ctx.n["se"]
                        if dbg is not None:
                            ctx.w(sy, "ve", ctx.n["ve"])
                            ctx.dinc(sy.dma_start(dbg[0][i * 128:(i + 1) * 128, :], lg[:]), "sdbg")
                            ctx.dinc(sy.dma_start(dbg[1][i * 128:(i + 1) * 128, :], tkB["pr"]), "sdbg")
                        ev_mask[sl] = emit_argmax_mask(ctx, tkB["pr"], mask[:, sl, :], tkB)
                        if dbg is not None:
                            ctx.w(sy, "ve", ev_mask[sl])
                            ctx.dinc(sy.dma_start(dbg[2][i * 128:(i + 1) * 128, :], mask[:, sl, :]), "sdbg")

                        ctx.w(te, f"tp{sl}", ev_tp[i])
                        if max(ev_um):
                            ctx.w(te, "ve", max(ev_um))
                        for dc in range(NDC):
                            mm = te.matmul(ps_u[:], xlnT[:, dc, ii * 128:(ii + 1) * 128],
                                           atw[:, dc, :], start=(dc == 0), stop=(dc == NDC - 1))
                        i_u = ctx.inc(mm, "pe")
                        ctx.w(v, "pe", i_u)
                        ctx.w(v, "ve", ev_mask[sl])
                        mask_b4 = mask[:, sl, :].rearrange("p (e one) -> p e one", one=1)                                                .broadcast_to([128, 8, 4])
                        ctx.inc(v.tensor_tensor(u_m[:, sl, :].rearrange("p (e r) -> p e r", e=8),
                                                ps_u[:].rearrange("p (e r) -> p e r", e=8),
                                                mask_b4, ALU.mult), "ve")
                        ev_um[sl] = ctx.n["ve"]
                        ctx.w(te, "ve", ev_um[sl])
                        if ev_ut:
                            ctx.w(te, "ve", ev_ut)
                        i_ut = ctx.inc(te.transpose(ps_ut[:], u_m[:, sl, :], id16), "pe")
                        ctx.w(v, "pe", i_ut)
                        ctx.inc(v.tensor_copy(utT[:, ii * 128:(ii + 1) * 128], ps_ut[:]), "ve")
                        ev_ut = ctx.n["ve"]

                    ctx.w(te, f"tp0", ev_tp[gi * TPG + (TPG - 2)])
                    ctx.w(te, f"tp1", ev_tp[gi * TPG + (TPG - 1)])
                    ctx.w(te, "ve", ev_ut)
                    for fc in range(32):
                        bank = fc % 2
                        if ev_hbank[bank]:
                            ctx.w(te, "se", ev_hbank[bank])
                        if fc == 0 and ev_wo:
                            ctx.w(s, "pe", ev_wo)
                        for dc in range(NDC):
                            mm = te.matmul(ps_h[bank][:], wiw[:, dc, fc * 128:(fc + 1) * 128],
                                           xlnT[:, dc, :], start=(dc == 0), stop=False)
                        mm = te.matmul(ps_h[bank][:], btw[:, fc * 128:(fc + 1) * 128],
                                       utT[:], start=False, stop=True)
                        i_h = ctx.inc(mm, "pe")
                        ctx.w(s, "pe", i_h)
                        ctx.inc(s.activation(h_sb[:, fc, :], ps_h[bank][:], AF.Relu,
                                             bias=wib[:, fc:fc + 1]), "se")
                        ev_hbank[bank] = ctx.n["se"]
                    ev_gbase = ctx.n["pe"]
                    ev_h_all = ctx.n["se"]
                    ev_g32 = ctx.n["pe"]

                    for tt in range(TPG):
                        i = gi * TPG + tt
                        sl = i % 2
                        ctx.w(te, "se", ev_h_all)
                        if ev_stt:
                            ctx.w(te, "ve", ev_stt)
                        for nh in range(2):
                            mm = te.matmul(ps_o[:, nh * 512:(nh + 1) * 512], ones16,
                                           wob[0:1, nh * 512:(nh + 1) * 512],
                                           start=True, stop=False)
                            for fc in range(32):
                                mm = te.matmul(ps_o[:, nh * 512:(nh + 1) * 512],
                                               h_sb[:, fc, tt * 128:(tt + 1) * 128],
                                               wow[:, fc, nh * 512:(nh + 1) * 512],
                                               start=False, stop=(fc == 31))
                        i_o = ctx.inc(mm, "pe")
                        if ev_stt:
                            ctx.w(sy, "ve", ev_stt)
                        ctx.dinc(sy.dma_start(xold[:, 0, :],
                                              xin[i * 128:(i + 1) * 128, :]), "xo0")
                        ctx.w(v, "pe", i_o)
                        ctx.w(v, "xo0")
                        if ev_so_slot[sl]:
                            ctx.w(v, f"so{sl}", ev_so_slot[sl])
                        ins = v.scalar_tensor_tensor(onew[:, sl, :], ps_o[:], 1.0,
                                                     xold[:, 0, :], ALU.mult, ALU.add)
                        ctx.inc(ins, "ve")
                        ev_stt = ctx.n["ve"]
                        ev_stt_slot[sl] = ev_stt
                        ctx.w(sy, "ve", ev_stt)
                        ctx.dinc(sy.dma_start(xout[i * 128:(i + 1) * 128, :],
                                              onew[:, sl, :]), f"so{sl}")
                        ev_so_slot[sl] = ctx.n[f"so{sl}"]
                    ev_wo = ctx.n["pe"]
    ctx.barrier()





def emit_l0b(ctx, xin, xout, W):
    nc = ctx.nc
    te, v, s, g, sy = nc.tensor, nc.vector, nc.scalar, nc.gpsimd, nc.sync
    uid = ctx.uid()
    pfx = f"b{uid}_"
    qsp = nc.dram_tensor(f"qsp{uid}", [D, T], BF16, kind="Internal").ap()
    ksp = nc.dram_tensor(f"ksp{uid}", [D, T], BF16, kind="Internal").ap()
    vsp = nc.dram_tensor(f"vsp{uid}", [T, D], BF16, kind="Internal").ap()
    XX = (2.0 ** 8) ** (1.0 / 16)
    slopes = [1.0 / XX ** (i + 1) for i in range(16)]
    with ExitStack() as eso:
        def sbo(name, shape, dt):
            return es.enter_context(nc.sbuf_tensor(pfx + name, shape, dt))
        es = eso
        o_full = sbo("ofull", [128, NT, D], BF16)
        rel0 = sbo("rel0", [128, T], F32)
        wo_w = sbo("wo", [128, NDC, D], BF16)
        borow = sbo("bo", [1, D], BF16)
        ones16 = ctx.consts["ones16"][:]

        with nc.Block() as blk:
            @blk.gpsimd
            def _(gp):
                ctx.dinc(g.dma_start(wo_w[:], W["l0b_wo"].rearrange("(c p) f -> p c f", p=128)), "dg")
                ctx.dinc(g.dma_start(borow[:], W["l0b_bo"]), "dg")
                ctx.inc(g.iota(rel0[:], [[1, T]], base=0, channel_multiplier=-1,
                               allow_small_or_imprecise_dtypes=True), "po")

        with ExitStack() as esA:
            es = esA
            qkw = sbo("qkw", [128, NDC, 2048], BF16)
            wv = sbo("wv", [128, NDC, D], BF16)
            bqk = sbo("bqk", [128, 16], F32)
            bvrow = sbo("bv", [1, D], BF16)
            xs = sbo("xs", [128, 2, D], F32)
            xln = sbo("xln", [128, 2, D], BF16)
            xlnT = sbo("xlnT", [128, NDC, 512], BF16)
            junk = sbo("junk", [128, D], BF16)
            stats = alloc_stats(nc, es, pfx + "s", 2)
            stqk = sbo("stqk", [128, 16, 512], BF16)
            stv = sbo("stv", [128, 4, D], BF16)
            ps_p = [es.enter_context(nc.psum_tensor(f"{pfx}psp{i}", [128, 512], F32))
                    for i in range(2)]
            ps_v = [es.enter_context(nc.psum_tensor(f"{pfx}psv{i}", [128, 512], F32))
                    for i in range(2)]
            with nc.Block() as blk:
                @blk.gpsimd
                def _(gp):
                    ctx.dinc(g.dma_start(qkw[:], W["l0b_qkW"].rearrange("(c p) f -> p c f", p=128)), "dg")
                    ctx.dinc(g.dma_start(wv[:], W["l0b_wv"].rearrange("(c p) f -> p c f", p=128)), "dg")
                    ctx.dinc(g.dma_start(bqk[:], W["l0b_bqk_cm"]), "dg")
                    ctx.dinc(g.dma_start(bvrow[:], W["l0b_bv"]), "dg")
                    w_ready = ctx.n["dg"]
                    ctx.w(v, "dg", w_ready)
                    ctx.inc(v.tensor_scalar(bqk[:, 0:8], bqk[:, 0:8], 0.125, None,
                                            ALU.mult), "ve")
                    ctx.w(v, "ve")
                    bqk_ready = ctx.n["ve"]

                    ev_norm = [0] * NT
                    ev_tp = [0] * NT
                    ev_pbank = [0, 0]
                    ev_vbank = [0, 0]
                    ev_gdone = 0
                    ev_st = {}
                    first = True
                    for gi in range(NKT):
                        for ii in range(4):
                            i = gi * 4 + ii
                            sl = i % 2
                            if i >= 2:
                                ctx.w(sy, "ve", ev_norm[i - 2])
                            ctx.dinc(sy.dma_start(xs[:, sl, :],
                                                  xin[i * 128:(i + 1) * 128, :]), f"sx{sl}")
                            ctx.w(v, f"sx{sl}")
                            ctx.w(s, f"sx{sl}")
                            if i >= 2:
                                ctx.w(v, f"tp{sl}", ev_tp[i - 2])
                            ev_norm[i] = emit_ln_tile(ctx, xs[:, sl, :], xln[:, sl, :],
                                                      junk[:], stats[sl])
                            ctx.w(sy, "ve", ev_norm[i])
                            if gi >= 1 and ii == 0:
                                ctx.w(sy, "pe", ev_gdone)
                            ctx.dinc(sy.dma_start(xlnT[:, :, ii * 128:(ii + 1) * 128],
                                                  xln[:, sl, :], transpose=True), f"tp{sl}")
                            ev_tp[i] = ctx.n[f"tp{sl}"]
                        ctx.w(te, "tp0", ev_tp[gi * 4 + 2])
                        ctx.w(te, "tp1", ev_tp[gi * 4 + 3])
                        if first:
                            ctx.w(te, "dg", w_ready)
                            first = False
                        for fc in range(16):
                            bank = fc % 2
                            if ev_pbank[bank]:
                                ctx.w(te, "se", ev_pbank[bank])
                            for dc in range(NDC):
                                mm = te.matmul(ps_p[bank][:], qkw[:, dc, fc * 128:(fc + 1) * 128],
                                               xlnT[:, dc, :], start=(dc == 0),
                                               stop=(dc == NDC - 1))
                            i_p = ctx.inc(mm, "pe")
                            ctx.w(s, "pe", i_p)
                            if fc == 0:
                                ctx.w(s, "ve", bqk_ready)
                                if gi >= 1:
                                    ctx.w(s, f"sq", ev_st["q"])
                            ctx.inc(s.activation(stqk[:, fc, :], ps_p[bank][:], AF.Identity,
                                                 scale=(0.125 if fc < 8 else 1.0),
                                                 bias=bqk[:, fc:fc + 1]), "se")
                            ev_pbank[bank] = ctx.n["se"]
                        ev_qk_st = ctx.n["se"]
                        for ii in range(4):
                            for nh in range(2):
                                bank = (ii * 2 + nh) % 2
                                if ev_vbank[bank]:
                                    ctx.w(te, "ve", ev_vbank[bank])
                                if ii == 0 and nh == 0 and gi >= 1:
                                    ctx.w(te, f"sv", ev_st["v"])
                                mm = te.matmul(ps_v[bank][:], ones16,
                                               bvrow[0:1, nh * 512:(nh + 1) * 512],
                                               start=True, stop=False)
                                for dc in range(NDC):
                                    mm = te.matmul(ps_v[bank][:],
                                                   xlnT[:, dc, ii * 128:(ii + 1) * 128],
                                                   wv[:, dc, nh * 512:(nh + 1) * 512],
                                                   start=False, stop=(dc == NDC - 1))
                                i_v = ctx.inc(mm, "pe")
                                ctx.w(v, "pe", i_v)
                                ctx.inc(v.tensor_copy(stv[:, ii, nh * 512:(nh + 1) * 512],
                                                      ps_v[bank][:]), "ve")
                                ev_vbank[bank] = ctx.n["ve"]
                        ev_v_st = ctx.n["ve"]
                        ev_gdone = ctx.n["pe"]
                        ctx.w(sy, "se", ev_qk_st)
                        ctx.dinc(sy.dma_start(
                            qsp[:, gi * 512:(gi + 1) * 512].rearrange("(c p) t -> p c t", p=128),
                            stqk[:, 0:8, :]), "sq")
                        ctx.dinc(sy.dma_start(
                            ksp[:, gi * 512:(gi + 1) * 512].rearrange("(c p) t -> p c t", p=128),
                            stqk[:, 8:16, :]), "sq")
                        ev_st["q"] = ctx.n["sq"]
                        ctx.w(sy, "ve", ev_v_st)
                        ctx.dinc(sy.dma_start(
                            vsp[gi * 512:(gi + 1) * 512, :].rearrange("(i p) d -> p i d", p=128),
                            stv[:]), "sv")
                        ev_st["v"] = ctx.n["sv"]

        with ExitStack() as esB:
            es = esB
            s_sb = sbo("s_sb", [128, 2, T], F32)
            p_sb = sbo("p_sb", [128, 2, T], BF16)
            pt_sb = sbo("pt", [128, 2, NT, 128], BF16)
            k_h = sbo("k_h", [64, 2, T], BF16)
            v_h = sbo("v_h", [128, 2, NT, 64], BF16)
            q_st = sbo("q_st", [64, 2, 128], BF16)
            mcol = sbo("mcol", [128, 2], F32)
            negm = sbo("negm", [128, 2], F32)
            lcol = sbo("lcol", [128, 2], F32)
            rlcol = sbo("rlcol", [128, 2], F32)
            ps_s = [es.enter_context(nc.psum_tensor(f"{pfx}pss{i}", [128, 512], F32))
                    for i in range(2)]
            ps_pv = [es.enter_context(nc.psum_tensor(f"{pfx}pspv{i}", [128, 64], F32))
                     for i in range(2)]
            with nc.Block() as blk:
                @blk.gpsimd
                def _(gp):
                    ctx.w(te, "se")
                    ctx.w(te, "ve")
                    ctx.w(sy, "sq")
                    ctx.w(sy, "sv")
                    ev_ssbank = [0, 0]
                    ev_exp_slot = [0, 0]
                    ev_recip_slot = [0, 0]
                    ev_pt_slot = [0, 0]
                    ev_pv_slot = [0, 0]
                    ev_pvbank = [0, 0]
                    ev_kv = [0, 0]
                    ev_scores_q = [0, 0]
                    ev_exp_last = 0

                    def emit_front(h, qt, hsl):
                        qsl = qt % 2
                        if ev_scores_q[qsl]:
                            ctx.w(sy, "pe", ev_scores_q[qsl])
                        ctx.dinc(sy.dma_start(q_st[:, qsl, :],
                                              qsp[h * 64:(h + 1) * 64,
                                                  qt * 128:(qt + 1) * 128]), f"qh{qsl}")
                        ctx.w(te, f"qh{qsl}")
                        for kt in range(NKT):
                            bank = kt % 2
                            if ev_ssbank[bank]:
                                ctx.w(te, "ve", ev_ssbank[bank])
                            mm = te.matmul(ps_s[bank][:], q_st[:, qsl, :],
                                           k_h[:, hsl, kt * 512:(kt + 1) * 512],
                                           start=True, stop=True)
                            i_s = ctx.inc(mm, "pe")
                            ctx.w(v, "pe", i_s)
                            if kt == 0 and ev_exp_slot[qsl]:
                                ctx.w(v, "se", ev_exp_slot[qsl])
                            ctx.inc(v.scalar_tensor_tensor(
                                s_sb[:, qsl, kt * 512:(kt + 1) * 512],
                                rel0[:, kt * 512:(kt + 1) * 512], slopes[h],
                                ps_s[bank][:], ALU.mult, ALU.add), "ve")
                            ev_ssbank[bank] = ctx.n["ve"]
                        ev_scores_q[qsl] = ctx.n["pe"]
                        ctx.w(v, "ve")
                        ctx.inc(v.reduce_max(mcol[:, qsl:qsl + 1], s_sb[:, qsl, :],
                                             axis=AX.X), "ve")
                        ctx.w(v, "ve")
                        if ev_exp_slot[qsl]:
                            ctx.w(v, "se", ev_exp_slot[qsl])
                        ctx.inc(v.tensor_scalar(negm[:, qsl:qsl + 1], mcol[:, qsl:qsl + 1],
                                                -1.0, None, ALU.mult), "ve")
                        ev_rmax = ctx.n["ve"]
                        ctx.w(s, "ve", ev_rmax)
                        if ev_pt_slot[qsl]:
                            ctx.w(s, f"pt{qsl}", ev_pt_slot[qsl])
                        if ev_recip_slot[qsl]:
                            ctx.w(s, "ve", ev_recip_slot[qsl])
                        ev_exp_slot[qsl] = ctx.inc(
                            s.activation(p_sb[:, qsl, :], s_sb[:, qsl, :], AF.Exp,
                                         bias=negm[:, qsl:qsl + 1],
                                         accum_out=lcol[:, qsl:qsl + 1]), "se")
                        ctx.w(v, "se", ev_exp_slot[qsl])
                        ctx.inc(v.reciprocal(rlcol[:, qsl:qsl + 1], lcol[:, qsl:qsl + 1]),
                                "ve")
                        ev_recip_slot[qsl] = ctx.n["ve"]
                        ctx.w(sy, "se", ev_exp_slot[qsl])
                        if ev_pv_slot[qsl]:
                            ctx.w(sy, "pe", ev_pv_slot[qsl])
                        ctx.dinc(sy.dma_start(pt_sb[:, qsl, :, :], p_sb[:, qsl, :],
                                              transpose=True), f"pt{qsl}")
                        ev_pt_slot[qsl] = ctx.n[f"pt{qsl}"]

                    def emit_back(h, qt, hsl):
                        qsl = qt % 2
                        ctx.w(te, f"pt{qsl}", ev_pt_slot[qsl])
                        bank = qsl
                        if ev_pvbank[bank]:
                            ctx.w(te, "ve", ev_pvbank[bank])
                        for kc in range(NT):
                            mm = te.matmul(ps_pv[bank][:], pt_sb[:, qsl, kc, :],
                                           v_h[:, hsl, kc, :],
                                           start=(kc == 0), stop=(kc == NT - 1))
                        i_pv = ctx.inc(mm, "pe")
                        ev_pv_slot[qsl] = ctx.n["pe"]
                        ctx.w(v, "pe", i_pv)
                        ctx.w(v, "ve", ev_recip_slot[qsl])
                        ctx.inc(v.tensor_scalar(o_full[:, qt, h * 64:(h + 1) * 64],
                                                ps_pv[bank][:], rlcol[:, qsl:qsl + 1],
                                                None, ALU.mult), "ve")
                        ev_pvbank[bank] = ctx.n["ve"]

                    for h in range(16):
                        hsl = h % 2
                        if h >= 2:
                            ctx.w(sy, "pe", ev_kv[hsl])
                        ctx.dinc(sy.dma_start(k_h[:, hsl, :],
                                              ksp[h * 64:(h + 1) * 64, :]), f"kh{hsl}")
                        ctx.dinc(sy.dma_start(
                            v_h[:, hsl, :, :],
                            vsp[:, h * 64:(h + 1) * 64].rearrange("(i p) d -> p i d", p=128)),
                            f"kh{hsl}")
                        if h == 0:
                            ctx.w(v, "po", ctx.n["po"])
                        ctx.w(te, f"kh{hsl}")
                        for qt in range(NT):
                            emit_front(h, qt, hsl)
                            if qt >= 1:
                                emit_back(h, qt - 1, hsl)
                        emit_back(h, NT - 1, hsl)
                        ev_kv[hsl] = ctx.n["pe"]

        with ExitStack() as esC:
            es = esC
            oT = sbo("oT", [128, NDC, 128], BF16)
            xold = sbo("xold", [128, 1, D], F32)
            onew = sbo("onew", [128, 2, D], F32)
            ps_c = es.enter_context(nc.psum_tensor(pfx + "psc", [128, 1024], F32))
            with nc.Block() as blk:
                @blk.gpsimd
                def _(gp):
                    ctx.w(te, "ve")
                    ctx.w(sy, "ve")
                    ev_stt = 0
                    ev_so_slot = [0, 0]
                    ev_cmm = 0
                    for qt in range(NT):
                        sl = qt % 2
                        if ev_cmm:
                            ctx.w(sy, "pe", ev_cmm)
                        ctx.dinc(sy.dma_start(oT[:], o_full[:, qt, :], transpose=True),
                                 "pt0")
                        ctx.w(te, "pt0")
                        if ev_stt:
                            ctx.w(te, "ve", ev_stt)
                        for nh in range(2):
                            mm = te.matmul(ps_c[:, nh * 512:(nh + 1) * 512], ones16,
                                           borow[0:1, nh * 512:(nh + 1) * 512],
                                           start=True, stop=False)
                            for dc in range(NDC):
                                mm = te.matmul(ps_c[:, nh * 512:(nh + 1) * 512],
                                               oT[:, dc, :],
                                               wo_w[:, dc, nh * 512:(nh + 1) * 512],
                                               start=False, stop=(dc == NDC - 1))
                        i_c = ctx.inc(mm, "pe")
                        ev_cmm = ctx.n["pe"]
                        if ev_stt:
                            ctx.w(sy, "ve", ev_stt)
                        ctx.dinc(sy.dma_start(xold[:, 0, :],
                                              xin[qt * 128:(qt + 1) * 128, :]), "xo0")
                        ctx.w(v, "pe", i_c)
                        ctx.w(v, "xo0")
                        if ev_so_slot[sl]:
                            ctx.w(v, f"so{sl}", ev_so_slot[sl])
                        ins = v.scalar_tensor_tensor(onew[:, sl, :], ps_c[:], 1.0,
                                                     xold[:, 0, :], ALU.mult, ALU.add)
                        ctx.inc(ins, "ve")
                        ev_stt = ctx.n["ve"]
                        ctx.w(sy, "ve", ev_stt)
                        ctx.dinc(sy.dma_start(xout[qt * 128:(qt + 1) * 128, :],
                                              onew[:, sl, :]), f"so{sl}")
                        ev_so_slot[sl] = ctx.n[f"so{sl}"]
    ctx.barrier()





def emit_l1a(ctx, xin, xout, W):
    nc = ctx.nc
    te, v, s, g, sy = nc.tensor, nc.vector, nc.scalar, nc.gpsimd, nc.sync
    uid = ctx.uid()
    pfx = f"ga{uid}_"
    qsp = nc.dram_tensor(f"gqsp{uid}", [D, T], BF16, kind="Internal").ap()
    with ExitStack() as eso:
        def sbo(name, shape, dt):
            return es.enter_context(nc.sbuf_tensor(pfx + name, shape, dt))
        es = eso
        kT = sbo("kT", [128, NDC, T], BF16)
        v_tm = sbo("v", [128, NT, D], BF16)
        c1col = sbo("c1c", [128, 1], F32)
        c2col = sbo("c2c", [128, 1], F32)
        ones32 = ctx.consts["ones32"][:]
        ones16 = ctx.consts["ones16"][:]

        with ExitStack() as esS:
            es = esS
            rotw = sbo("rotw", [1, 3, D], F32)
            scb = sbo("scb", [1, 2], F32)
            tA = sbo("tA", [1, 3, 512], F32)
            tB = sbo("tB", [1, 3, 512], F32)
            crow = sbo("crow", [1, 3, 512], F32)
            srow = sbo("srow", [1, 3, 512], F32)
            c12row = sbo("c12", [1, 2], F32)
            cb = sbo("cb", [128, 3, 512], F32)
            sb_ = sbo("sb", [128, 3, 512], F32)
            ps_b = es.enter_context(nc.psum_tensor(pfx + "psb", [128, 512], F32))
            with nc.Block() as blk:
                @blk.gpsimd
                def _(gp):
                    ctx.dinc(g.dma_start(rotw[:], W["l1a_rot"].rearrange("(one r) d -> one r d", one=1)), "dg")
                    ctx.dinc(g.dma_start(scb[:], W["l1a_scb"]), "dg")
                    ctx.w(v, "dg")
                    re = rotw[:, :, :].rearrange("o r (i two) -> o r i two", two=2)
                    ctx.inc(v.tensor_tensor(tA[:], re[:, :, :, 0], re[:, :, :, 0], ALU.mult), "ve")
                    ctx.inc(v.tensor_tensor(tB[:], re[:, :, :, 1], re[:, :, :, 1], ALU.mult), "ve")
                    ctx.w(v, "ve")
                    ctx.inc(v.tensor_tensor(tA[:], tA[:], tB[:], ALU.add), "ve")
                    ctx.w(s, "ve")
                    ctx.inc(s.activation(tA[:], tA[:], AF.Sqrt), "se")
                    ctx.w(v, "se")
                    ctx.inc(v.reciprocal(tA[:], tA[:]), "ve")
                    ctx.w(v, "ve")
                    ctx.inc(v.tensor_tensor(crow[:], re[:, :, :, 0], tA[:], ALU.mult), "ve")
                    ctx.inc(v.tensor_tensor(srow[:], re[:, :, :, 1], tA[:], ALU.mult), "ve")
                    ctx.w(v, "ve")
                    ctx.inc(v.reciprocal(c12row[:, 0:1], scb[:, 0:1]), "ve")
                    ctx.w(v, "ve")
                    ctx.inc(v.tensor_scalar(c12row[:, 0:1], c12row[:, 0:1], 2.0, None, ALU.mult), "ve")
                    ctx.w(v, "ve")
                    ctx.inc(v.tensor_tensor(c12row[:, 1:2], c12row[:, 0:1], scb[:, 1:2], ALU.add), "ve")
                    ctx.w(v, "ve")
                    st_rows = ctx.n["ve"]
                    ctx.w(te, "ve", st_rows)
                    for r in range(3):
                        i_m = ctx.inc(te.matmul(ps_b[:], ones32, crow[:, r, :],
                                                start=True, stop=True), "pe")
                        ctx.w(v, "pe", i_m)
                        ctx.inc(v.tensor_copy(cb[:, r, :], ps_b[:]), "ve")
                        ctx.w(te, "ve")
                        i_m = ctx.inc(te.matmul(ps_b[:], ones32, srow[:, r, :],
                                                start=True, stop=True), "pe")
                        ctx.w(v, "pe", i_m)
                        ctx.inc(v.tensor_copy(sb_[:, r, :], ps_b[:]), "ve")
                        ctx.w(te, "ve")
                    i_m = ctx.inc(te.matmul(ps_b[:, 0:2], ones32, c12row[:],
                                            start=True, stop=True), "pe")
                    ctx.w(v, "pe", i_m)
                    ctx.inc(v.tensor_copy(c1col[:], ps_b[:, 0:1]), "ve")
                    ctx.inc(v.tensor_copy(c2col[:], ps_b[:, 1:2]), "ve")
                    ctx.w(v, "ve")

            with ExitStack() as esA:
                es = esA
                xs = sbo("xs", [128, 2, D], F32)
                xln = sbo("xln", [128, 2, D], BF16)
                junk = sbo("junk", [128, D], BF16)
                stats = alloc_stats(nc, es, pfx, 2)
                t1 = sbo("t1", [128, 512], F32)
                t2 = sbo("t2", [128, 512], F32)
                rq = sbo("rq", [128, 2, D], BF16)
                rk = sbo("rk", [128, 2, D], BF16)
                stq = sbo("stq", [128, NDC, 128], BF16)
                with nc.Block() as blk:
                    @blk.gpsimd
                    def _(gp):
                        ev_norm = [0] * NT
                        ev_rot = [0, 0]
                        ev_tpq = [0, 0]
                        ev_tpk = [0, 0]
                        ev_stq = 0
                        for i in range(NT):
                            sl = i % 2
                            if i >= 2:
                                ctx.w(sy, "ve", ev_norm[i - 2])
                            ctx.dinc(sy.dma_start(xs[:, sl, :],
                                                  xin[i * 128:(i + 1) * 128, :]), f"sx{sl}")
                            ctx.w(v, f"sx{sl}")
                            ctx.w(s, f"sx{sl}")
                            if i >= 2:
                                ctx.w(v, f"tpq{sl}", ev_tpq[sl])
                                ctx.w(v, f"tpk{sl}", ev_tpk[sl])
                            ev_norm[i] = emit_ln_tile(ctx, xs[:, sl, :], xln[:, sl, :],
                                                      junk[:], stats[sl])
                            ctx.w(v, "ve", ev_norm[i])
                            xe = xln[:, sl, :].rearrange("p (i two) -> p i two", two=2)
                            for r, dst in ((0, rq[:, sl, :]), (1, rk[:, sl, :]),
                                           (2, v_tm[:, i, :])):
                                de = dst.rearrange("p (i two) -> p i two", two=2)
                                ctx.inc(v.tensor_tensor(t1[:], xe[:, :, 0], cb[:, r, :], ALU.mult), "ve")
                                ctx.inc(v.tensor_tensor(t2[:], xe[:, :, 1], sb_[:, r, :], ALU.mult), "ve")
                                ctx.w(v, "ve")
                                ctx.inc(v.tensor_tensor(de[:, :, 0], t1[:], t2[:], ALU.subtract), "ve")
                                ctx.w(v, "ve")
                                ctx.inc(v.tensor_tensor(t1[:], xe[:, :, 0], sb_[:, r, :], ALU.mult), "ve")
                                ctx.inc(v.tensor_tensor(t2[:], xe[:, :, 1], cb[:, r, :], ALU.mult), "ve")
                                ctx.w(v, "ve")
                                ctx.inc(v.tensor_tensor(de[:, :, 1], t1[:], t2[:], ALU.add), "ve")
                                ctx.w(v, "ve")
                            ev_rot[sl] = ctx.n["ve"]
                            ctx.w(sy, "ve", ev_rot[sl])
                            if ev_stq:
                                ctx.w(sy, f"sq", ev_stq)
                            ctx.dinc(sy.dma_start(stq[:], rq[:, sl, :], transpose=True), f"tpq{sl}")
                            ev_tpq[sl] = ctx.n[f"tpq{sl}"]
                            ctx.w(sy, f"tpq{sl}", ev_tpq[sl])
                            ctx.dinc(sy.dma_start(
                                qsp[:, i * 128:(i + 1) * 128].rearrange("(c p) t -> p c t", p=128),
                                stq[:]), "sq")
                            ev_stq = ctx.n["sq"]
                            ctx.dinc(sy.dma_start(kT[:, :, i * 128:(i + 1) * 128],
                                                  rk[:, sl, :], transpose=True), f"tpk{sl}")
                            ev_tpk[sl] = ctx.n[f"tpk{sl}"]


        with ExitStack() as esB:
            es = esB
            s_sb = sbo("s_sb", [128, T], F32)
            p_sb = sbo("p_sb", [128, T], BF16)
            pt_sb = sbo("pt", [128, NT, 128], BF16)
            q_st = sbo("q_st", [128, 2, NDC, 128], BF16)
            junkB = sbo("junkB", [128, 512], BF16)
            mcol = sbo("mcol", [128, 1], F32)
            negm = sbo("negm", [128, 1], F32)
            lcol = sbo("lcol", [128, 1], F32)
            rlcol = sbo("rlcol", [128, 1], F32)
            i1 = sbo("i1", [128, 1], F32)
            i2 = sbo("i2", [128, 1], F32)
            rdcol = sbo("rdcol", [128, 1], F32)
            xold = sbo("xold", [128, 1, D], F32)
            onew = sbo("onew", [128, 2, D], F32)
            ps_s = [es.enter_context(nc.psum_tensor(f"{pfx}pss{i}", [128, 512], F32))
                    for i in range(2)]
            ps_o = es.enter_context(nc.psum_tensor(pfx + "pso", [128, 1024], F32))
            with nc.Block() as blk:
                @blk.gpsimd
                def _(gp):
                    ctx.w(te, "ve")
                    ctx.w(te, "se")
                    ctx.w(te, "tpk0")
                    ctx.w(te, "tpk1")
                    ctx.w(sy, "sq")
                    ev_ssbank = [0, 0]
                    ev_rmax = 0
                    ev_exp = 0
                    ev_recip = 0
                    ev_pt = 0
                    ev_pv = 0
                    ev_stt = 0
                    ev_sq = [0, 0]
                    ev_so_slot = [0, 0]
                    ev_den = 0
                    for qt in range(NT):
                        sl = qt % 2
                        if ev_sq[sl]:
                            ctx.w(sy, "pe", ev_sq[sl])
                        ctx.dinc(sy.dma_start(
                            q_st[:, sl, :, :],
                            qsp[:, qt * 128:(qt + 1) * 128].rearrange("(c p) t -> p c t", p=128)),
                            f"qh{sl}")
                        ctx.w(te, f"qh{sl}")
                        for kt in range(NKT):
                            bank = kt % 2
                            if ev_ssbank[bank]:
                                ctx.w(te, "ve", ev_ssbank[bank])
                            for dc in range(NDC):
                                mm = te.matmul(ps_s[bank][:], q_st[:, sl, dc, :],
                                               kT[:, dc, kt * 512:(kt + 1) * 512],
                                               start=(dc == 0), stop=(dc == NDC - 1))
                            i_s = ctx.inc(mm, "pe")
                            ctx.w(v, "pe", i_s)
                            if qt >= 1 and kt == 0:
                                ctx.w(v, "se", ev_exp)
                            ctx.inc(v.tensor_scalar(s_sb[:, kt * 512:(kt + 1) * 512],
                                                    ps_s[bank][:], c1col[:], c2col[:],
                                                    ALU.mult, ALU.add), "ve")
                            ev_ssbank[bank] = ctx.n["ve"]
                        ev_sq[sl] = ctx.n["pe"]
                        ctx.w(v, "ve")
                        ctx.inc(v.reduce_max(mcol[:], s_sb[:], axis=AX.X), "ve")
                        ctx.w(v, "ve")
                        if ev_exp:
                            ctx.w(v, "se", ev_exp)
                        ctx.inc(v.tensor_scalar(negm[:], mcol[:], -1.0, None, ALU.mult), "ve")
                        ev_rmax = ctx.n["ve"]
                        ctx.w(s, "ve", ev_rmax)
                        if ev_pt:
                            ctx.w(s, "pt0", ev_pt)
                        if ev_recip:
                            ctx.w(s, "ve", ev_recip)
                        ev_exp = ctx.inc(s.activation(p_sb[:], s_sb[:], AF.Exp,
                                                      bias=negm[:], accum_out=lcol[:]), "se")
                        ctx.w(v, "se", ev_exp)
                        ctx.inc(v.reciprocal(rlcol[:], lcol[:]), "ve")
                        ev_recip = ctx.n["ve"]
                        ctx.w(sy, "se", ev_exp)
                        if ev_pv:
                            ctx.w(sy, "pe", ev_pv)
                        ctx.dinc(sy.dma_start(pt_sb[:], p_sb[:], transpose=True), "pt0")
                        ev_pt = ctx.n["pt0"]
                        ctx.w(te, "pt0", ev_pt)
                        if ev_stt:
                            ctx.w(te, "ve", ev_stt)
                            ctx.w(te, "se", ev_den)
                        for nh in range(2):
                            for kc in range(NT):
                                mm = te.matmul(ps_o[:, nh * 512:(nh + 1) * 512],
                                               pt_sb[:, kc, :],
                                               v_tm[:, kc, nh * 512:(nh + 1) * 512],
                                               start=(kc == 0), stop=(kc == NT - 1))
                            ctx.inc(mm, "pe")
                        ev_pv = ctx.n["pe"]
                        ctx.w(s, "pe", ev_pv)
                        ctx.inc(s.activation(junkB[:], ps_o[:, 0:512], AF.Square,
                                             accum_out=i1[:]), "se")
                        ctx.w(s, "se")
                        ctx.inc(s.activation(junkB[:], ps_o[:, 512:1024], AF.Square,
                                             accum_out=i2[:]), "se")
                        ev_den = ctx.n["se"]
                        ctx.w(v, "se", ev_den)
                        ctx.w(v, "ve", ev_recip)
                        ctx.inc(v.tensor_tensor(i1[:], i1[:], i2[:], ALU.add), "ve")
                        ctx.w(v, "ve")
                        ctx.inc(v.tensor_tensor(i1[:], i1[:], rlcol[:], ALU.mult), "ve")
                        ctx.w(v, "ve")
                        ctx.inc(v.tensor_tensor(i1[:], i1[:], rlcol[:], ALU.mult), "ve")
                        ctx.w(v, "ve")
                        i_rl2 = ctx.n["ve"]
                        ctx.w(s, "ve", i_rl2)
                        i_abs = ctx.inc(s.activation(i1[:], i1[:], AF.Abs), "se")
                        ctx.w(v, "se", i_abs)
                        ctx.inc(v.tensor_scalar(i1[:], i1[:], 1e-8, None, ALU.max), "ve")
                        i_cl = ctx.n["ve"]
                        ctx.w(s, "ve", i_cl)
                        i_sq2 = ctx.inc(s.activation(rdcol[:], i1[:], AF.Sqrt), "se")
                        ctx.w(v, "se", i_sq2)
                        ctx.inc(v.reciprocal(rdcol[:], rdcol[:]), "ve")
                        ctx.w(v, "ve")
                        ctx.inc(v.tensor_tensor(rdcol[:], rdcol[:], rlcol[:], ALU.mult), "ve")
                        ctx.w(v, "ve")
                        if ev_stt:
                            ctx.w(sy, "ve", ev_stt)
                        ctx.dinc(sy.dma_start(xold[:, 0, :],
                                              xin[qt * 128:(qt + 1) * 128, :]), "xo0")
                        ctx.w(v, "xo0")
                        if ev_so_slot[sl]:
                            ctx.w(v, f"so{sl}", ev_so_slot[sl])
                        for nh in range(2):
                            ins = v.scalar_tensor_tensor(
                                onew[:, sl, nh * 512:(nh + 1) * 512],
                                ps_o[:, nh * 512:(nh + 1) * 512], rdcol[:],
                                xold[:, 0, nh * 512:(nh + 1) * 512], ALU.mult, ALU.add)
                        ctx.inc(ins, "ve")
                        ev_stt = ctx.n["ve"]
                        ctx.w(sy, "ve", ev_stt)
                        ctx.dinc(sy.dma_start(xout[qt * 128:(qt + 1) * 128, :],
                                              onew[:, sl, :]), f"so{sl}")
                        ev_so_slot[sl] = ctx.n[f"so{sl}"]
    ctx.barrier()





def emit_precast_l1w(ctx, W):
    if getattr(ctx, "l1w", None) is not None:
        return ctx.l1w
    nc = ctx.nc
    g, sy = nc.gpsimd, nc.sync
    w1b = nc.dram_tensor("w1bsp", [8, 128, NDC, 4096], BF16, kind="Internal").ap()
    w2b = nc.dram_tensor("w2bsp", [8, 128, 32, 1024], BF16, kind="Internal").ap()
    with ExitStack() as es:
        st1 = [es.enter_context(nc.sbuf_tensor(f"pc_st1_{i}", [128, NDC, 2048], BF16))
               for i in range(2)]
        with nc.Block() as blk:
            @blk.gpsimd
            def _(gp):
                ev_store = [0, 0]
                for e in range(8):
                    for hh in range(2):
                        sl = (e * 2 + hh) % 2
                        if ev_store[sl]:
                            ctx.w(g, f"pcs{sl}", ev_store[sl])
                        ctx.dinc(g.dma_start(
                            st1[sl][:],
                            W["l1f_w1"][e, :, hh * 2048:(hh + 1) * 2048]
                            .rearrange("(c p) f -> p c f", p=128)), f"pcl{sl}")
                        ctx.w(sy, f"pcl{sl}")
                        ctx.dinc(sy.dma_start(w1b[e, :, :, hh * 2048:(hh + 1) * 2048],
                                              st1[sl][:]), f"pcs{sl}")
                        ev_store[sl] = ctx.n[f"pcs{sl}"]
                for e in range(8):
                    for hh in range(2):
                        sl = (e * 2 + hh) % 2
                        ctx.w(g, f"pcs{sl}", ev_store[sl])
                        ctx.dinc(g.dma_start(
                            st1[sl][:].rearrange("p c f -> p (c f)")[:, 0:16384]
                            .rearrange("p (c d) -> p c d", c=16),
                            W["l1f_w2"][e, hh * 2048:(hh + 1) * 2048, :]
                            .rearrange("(c p) d -> p c d", p=128)), f"pcl{sl}")
                        ctx.w(sy, f"pcl{sl}")
                        ctx.dinc(sy.dma_start(w2b[e, :, hh * 16:(hh + 1) * 16, :],
                                              st1[sl][:].rearrange("p c f -> p (c f)")[:, 0:16384]
                                              .rearrange("p (c d) -> p c d", c=16)), f"pcs{sl}")
                        ev_store[sl] = ctx.n[f"pcs{sl}"]
    ctx.barrier()
    ctx.l1w = (w1b, w2b)
    return ctx.l1w


def emit_l1ffn(ctx, xin, xout, W):
    nc = ctx.nc
    te, v, s, g, sy = nc.tensor, nc.vector, nc.scalar, nc.gpsimd, nc.sync
    w1b, w2b = emit_precast_l1w(ctx, W)
    uid = ctx.uid()
    pfx = f"m{uid}_"
    xm = nc.dram_tensor(f"xmid{uid}", [T, D], F32, kind="Internal").ap()
    TS = 512
    NSL = T // TS
    TPS = TS // 128
    with ExitStack() as eso:
        def sbo(name, shape, dt):
            return es.enter_context(nc.sbuf_tensor(pfx + name, shape, dt))
        es = eso
        pool = es.enter_context(nc.psum_tensor(pfx + "pool", [128, 8, 512], F32))
        ps_t = pool[:, 5, 0:128]
        ones16 = ctx.consts["ones16"][:]
        ones32 = ctx.consts["ones32"][:]
        id32 = ctx.consts["ident32"][:]

        def bcast_row(row_ap, dst, n_done):
            for nh in range(2):
                ctx.w(te, "ve", n_done[0])
                i_m = ctx.inc(te.matmul(pool[:, 7, :],
                                        ones32, row_ap[0:1, nh * 512:(nh + 1) * 512],
                                        start=True, stop=True), "pe")
                ctx.w(v, "pe", i_m)
                ctx.inc(v.tensor_copy(dst[:, nh * 512:(nh + 1) * 512], pool[:, 7, :]), "ve")
                n_done[0] = ctx.n["ve"]

        with ExitStack() as esM:
            es = esM
            g1b = sbo("g1b", [128, D], F32)
            be1b = sbo("be1b", [128, D], F32)
            rows = sbo("rows", [1, 2, D], F32)
            mixw = sbo("mixw", [128, NDC, D], BF16)
            mixb = sbo("mixb", [1, D], BF16)
            xs = sbo("xs", [128, 2, D], F32)
            z32 = sbo("z32", [128, 2, D], F32)
            t132 = sbo("t132", [128, 2, D], F32)
            nbuf = sbo("nbuf", [128, 2, D], F32)
            stile = sbo("stile", [128, 2, D], F32)
            n16 = sbo("n16", [128, 2, D], BF16)
            nT = sbo("nT", [128, NDC, 128], BF16)
            junk = sbo("junk", [128, D], BF16)
            stats = alloc_stats(nc, es, pfx, 2)
            onew = sbo("onew", [128, 2, D], F32)
            with nc.Block() as blk:
                @blk.gpsimd
                def _(gp):
                    ctx.dinc(g.dma_start(rows[:, 0, :], W["l1f_g1r"]), "dg")
                    ctx.dinc(g.dma_start(rows[:, 1, :], W["l1f_be1r"]), "dg")
                    ctx.dinc(g.dma_start(mixw[:], W["l1f_mixw"].rearrange("(c p) f -> p c f", p=128)), "dg")
                    ctx.dinc(g.dma_start(mixb[:], W["l1f_mixb"]), "dg")
                    w_ready = ctx.n["dg"]
                    ctx.w(te, "dg", w_ready)
                    ctx.w(te, "po", ctx.n["po"])
                    nd = [0]
                    bcast_row(rows[:, 0, :], g1b[:], nd)
                    bcast_row(rows[:, 1, :], be1b[:], nd)
                    ev_bc = ctx.n["ve"]
                    ev_norm = [0] * NT
                    ev_tp = 0
                    ev_mm = 0
                    ev_stt = 0
                    ev_stile = [0, 0]
                    ev_tp_slot = [0, 0]
                    ev_stt_slot = [0, 0]
                    ev_so_slot = [0, 0]
                    for i in range(NT):
                        sl = i % 2
                        if ev_stile[sl]:
                            ctx.w(sy, "ve", ev_stile[sl])
                        ctx.dinc(sy.dma_start(xs[:, sl, :],
                                              xin[i * 128:(i + 1) * 128, :]), f"sx{sl}")
                        ctx.w(v, f"sx{sl}")
                        ctx.w(s, f"sx{sl}")
                        if i == 0:
                            ctx.w(v, "ve", ev_bc)
                        ev_z = emit_ln_tile(ctx, xs[:, sl, :], z32[:, sl, :],
                                            junk[:], stats[sl])
                        ctx.w(v, "ve", ev_z)
                        ctx.w(s, "ve", ev_z)
                        ev_norm[i] = emit_ln_tile(ctx, z32[:, sl, :], t132[:, sl, :],
                                                  junk[:], stats[sl])
                        ctx.w(v, "ve", ev_norm[i])
                        ctx.inc(v.tensor_tensor(nbuf[:, sl, :], t132[:, sl, :], g1b[:],
                                                ALU.mult), "ve")
                        ctx.w(v, "ve")
                        ctx.inc(v.tensor_tensor(nbuf[:, sl, :], nbuf[:, sl, :], be1b[:],
                                                ALU.add), "ve")
                        ctx.w(v, "ve")
                        ctx.inc(v.tensor_tensor(stile[:, sl, :], z32[:, sl, :],
                                                nbuf[:, sl, :], ALU.add), "ve")
                        ev_stile[sl] = ctx.n["ve"]
                        ctx.w(s, "ve", ev_stile[sl])
                        if ev_tp_slot[sl]:
                            ctx.w(s, "tpm", ev_tp_slot[sl])
                        i_c = ctx.inc(s.copy(n16[:, sl, :], nbuf[:, sl, :]), "se")
                        ctx.w(sy, "se", i_c)
                        if ev_mm:
                            ctx.w(sy, "pe", ev_mm)
                        ctx.dinc(sy.dma_start(nT[:], n16[:, sl, :], transpose=True), "tpm")
                        ev_tp = ctx.n["tpm"]
                        ev_tp_slot[sl] = ev_tp
                        ctx.w(te, "tpm", ev_tp)
                        if ev_stt:
                            ctx.w(te, "ve", ev_stt)
                        for nh in range(2):
                            mm = te.matmul(pool[:, nh, :], ones16,
                                           mixb[0:1, nh * 512:(nh + 1) * 512],
                                           start=True, stop=False)
                            for dc in range(NDC):
                                mm = te.matmul(pool[:, nh, :], nT[:, dc, :],
                                               mixw[:, dc, nh * 512:(nh + 1) * 512],
                                               start=False, stop=(dc == NDC - 1))
                        i_mm = ctx.inc(mm, "pe")
                        ev_mm = ctx.n["pe"]
                        ctx.w(v, "pe", i_mm)
                        if ev_so_slot[sl]:
                            ctx.w(v, f"so{sl}", ev_so_slot[sl])
                        ins = v.scalar_tensor_tensor(
                            onew[:, sl, :],
                            pool[:, 0:2, :].rearrange("p a b -> p (a b)"),
                            1.0, stile[:, sl, :], ALU.mult, ALU.add)
                        ctx.inc(ins, "ve")
                        ev_stt = ctx.n["ve"]
                        ev_stt_slot[sl] = ev_stt
                        ctx.w(sy, "ve", ev_stt)
                        ctx.dinc(sy.dma_start(xm[i * 128:(i + 1) * 128, :],
                                              onew[:, sl, :]), f"so{sl}")
                        ev_so_slot[sl] = ctx.n[f"so{sl}"]

        with ExitStack() as esE:
            es = esE
            g2b = sbo("g2b", [128, D], F32)
            be2b = sbo("be2b", [128, D], F32)
            rows2 = sbo("rows2", [1, 2, D], F32)
            routw = sbo("routw", [128, NDC, 8], F32)
            w1h = [sbo(f"w1h{i}", [128, NDC, 2048], BF16) for i in range(2)]
            w2ch = sbo("w2ch", [128, 2, 4, D], BF16)
            b1cm = sbo("b1cm", [128, 32], F32)
            b2row = sbo("b2row", [1, D], BF16)
            h_sb = sbo("h", [128, 32, TS], BF16)
            n2T = sbo("n2T", [128, NDC, TS], BF16)
            n2T32 = sbo("n2T32", [128, NDC, 128], F32)
            acc = sbo("acc", [128, TPS, D], F32)
            xs2 = sbo("xs2", [128, 2, D], F32)
            xr2 = sbo("xr2", [128, 2, D], F32)
            t232 = sbo("t232", [128, 2, D], F32)
            n2buf = sbo("n2buf", [128, 2, D], F32)
            n216 = sbo("n216", [128, 2, D], BF16)
            junk2 = sbo("junk2", [128, D], BF16)
            stats2 = alloc_stats(nc, es, pfx + "2", 2)
            tkB = alloc_topk(nc, es, pfx)
            lg = sbo("lg", [128, 8], F32)
            m1c = sbo("m1c", [128, 1], F32)
            m2c = sbo("m2c", [128, 1], F32)
            mask1 = sbo("mask1", [128, 8], F32)
            mask2 = sbo("mask2", [128, 8], F32)
            comb = sbo("comb", [128, TPS, 8], F32)
            with nc.Block() as blk:
                @blk.gpsimd
                def _(gp):
                    ctx.w(sy, "so0")
                    ctx.w(sy, "so1")
                    ctx.w(g, "so0")
                    ctx.w(g, "so1")
                    ctx.w(g, "ve")
                    ctx.w(g, "se")
                    ctx.dinc(g.dma_start(rows2[:, 0, :], W["l1f_g2r"]), "dg")
                    ctx.dinc(g.dma_start(rows2[:, 1, :], W["l1f_be2r"]), "dg")
                    ctx.dinc(g.dma_start(routw[:], W["l1f_routw"].rearrange("(c p) e -> p c e", p=128)), "dg")
                    w_ready = ctx.n["dg"]
                    ctx.w(te, "dg", w_ready)
                    ctx.w(te, "ve")
                    nd = [0]
                    bcast_row(rows2[:, 0, :], g2b[:], nd)
                    bcast_row(rows2[:, 1, :], be2b[:], nd)
                    ev_bc = ctx.n["ve"]
                    ev_gmm = 0
                    ev_tbank = 0
                    ev_lg_free = 0
                    ev_hband = [0, 0]
                    ev_w2evac = 0
                    ev_hmm = 0
                    ev_acc_store = 0
                    ev_w1ld = [0, 0]
                    ev_w2sl = [0, 0]
                    ev_norm2 = {}
                    ev_tp2 = 0
                    ev_cast2 = [0, 0]
                    ev_hmm = 0
                    ev_w1mm = [0, 0]
                    ev_w2mmsl = [0, 0]
                    ev_probs_se = 0
                    ev_h_all_last = 0
                    ev_b2_used = 0
                    ev_tp2_slot = [0, 0]
                    for sli in range(NSL):
                        for tt in range(TPS):
                            i = sli * TPS + tt
                            sl = i % 2
                            if i >= 2:
                                ctx.w(sy, "ve", ev_norm2.get(i - 2, 0))
                            ctx.dinc(sy.dma_start(xs2[:, sl, :],
                                                  xm[i * 128:(i + 1) * 128, :]), f"sm{sl}")
                            ctx.dinc(sy.dma_start(xr2[:, sl, :],
                                                  xin[i * 128:(i + 1) * 128, :]), f"sr{sl}")
                            ctx.w(v, f"sm{sl}")
                            ctx.w(v, f"sr{sl}")
                            ctx.w(s, f"sm{sl}")
                            if sli == 0 and tt == 0:
                                ctx.w(v, "ve", ev_bc)
                            if i >= 2:
                                ctx.w(v, "se", ev_cast2[sl])
                                if ev_gmm:
                                    ctx.w(v, "pe", ev_gmm)
                            env = emit_ln_tile(ctx, xs2[:, sl, :], t232[:, sl, :],
                                               junk2[:], stats2[sl])
                            ctx.w(v, "ve", env)
                            ctx.inc(v.tensor_tensor(n2buf[:, sl, :], t232[:, sl, :],
                                                    g2b[:], ALU.mult), "ve")
                            ctx.w(v, "ve")
                            ctx.inc(v.tensor_tensor(n2buf[:, sl, :], n2buf[:, sl, :],
                                                    be2b[:], ALU.add), "ve")
                            ctx.w(v, "ve")
                            ctx.w(v, "so_acc", ev_acc_store)
                            ctx.inc(v.tensor_tensor(acc[:, tt, :], xr2[:, sl, :],
                                                    xs2[:, sl, :], ALU.add), "ve")
                            ev_norm2[i] = ctx.n["ve"]
                            ctx.w(s, "ve", ev_norm2[i])
                            if ev_tp2_slot[sl]:
                                ctx.w(s, f"tpm{sl}", ev_tp2_slot[sl])
                            ev_cast2[sl] = ctx.inc(s.copy(n216[:, sl, :], n2buf[:, sl, :]), "se")
                            ctx.w(sy, "se", ev_cast2[sl])
                            if sli >= 1 and tt == 0:
                                ctx.w(sy, "pe", ev_hmm)
                            ctx.dinc(sy.dma_start(n2T[:, :, tt * 128:(tt + 1) * 128],
                                                  n216[:, sl, :], transpose=True), f"tpm{sl}")
                            ev_tp2_slot[sl] = ctx.n[f"tpm{sl}"]
                            ctx.w(te, "ve", ev_norm2[i])
                            for dc in range(NDC):
                                if ev_tbank:
                                    ctx.w(te, "ve", ev_tbank)
                                i_t = ctx.inc(te.transpose(ps_t, n2buf[:, sl, dc * 128:(dc + 1) * 128],
                                                           id32), "pe")
                                ctx.w(v, "pe", i_t)
                                if ev_gmm:
                                    ctx.w(v, "pe", ev_gmm)
                                ctx.inc(v.tensor_copy(n2T32[:, dc, :], ps_t), "ve")
                                ev_tbank = ctx.n["ve"]
                            i_t32 = ctx.n["ve"]
                            ctx.w(te, "ve", i_t32)
                            if ev_lg_free:
                                ctx.w(te, "ve", ev_lg_free)
                            for dc in range(NDC):
                                mm = te.matmul(pool[:, 6, 0:8], n2T32[:, dc, :],
                                               routw[:, dc, :], start=(dc == 0),
                                               stop=(dc == NDC - 1))
                            i_g = ctx.inc(mm, "pe")
                            ev_gmm = i_g
                            ctx.w(v, "pe", i_g)
                            if ev_lg_free:
                                ctx.w(v, "se", ev_probs_se)
                            ctx.inc(v.tensor_copy(lg[:], pool[:, 6, 0:8]), "ve")
                            ev_lg_free = ctx.n["ve"]
                            ctx.w(v, "ve")
                            emit_probs(ctx, lg[:], tkB)
                            ev_probs_se = ctx.n["se"]
                            emit_argmax_mask(ctx, tkB["pr"], mask1[:], tkB, save_max=m1c[:])
                            ctx.w(v, "ve")
                            ctx.inc(v.scalar_tensor_tensor(tkB["lg2"], mask1[:], -1e30,
                                                           tkB["pr"], ALU.mult, ALU.add), "ve")
                            ctx.w(v, "ve")
                            emit_argmax_mask(ctx, tkB["lg2"], mask2[:], tkB, save_max=m2c[:])
                            ctx.w(v, "ve")
                            ctx.inc(v.tensor_scalar(mask1[:], mask1[:], m1c[:], None,
                                                    ALU.mult), "ve")
                            ctx.w(v, "ve")
                            ctx.inc(v.tensor_scalar(mask2[:], mask2[:], m2c[:], None,
                                                    ALU.mult), "ve")
                            ctx.w(v, "ve")
                            ctx.inc(v.tensor_tensor(comb[:, tt, :], mask1[:], mask2[:],
                                                    ALU.add), "ve")
                            ctx.w(v, "ve")
                        ev_prep_done = ctx.n["ve"]
                        for e in range(8):
                            for hh in range(2):
                                whsl = hh
                                if ev_w1mm[whsl]:
                                    ctx.w(sy, "pe", ev_w1mm[whsl])
                                ctx.dinc(sy.dma_start(w1h[whsl][:],
                                                      w1b[e, :, :, hh * 2048:(hh + 1) * 2048]),
                                         f"wl{whsl}")
                                ev_w1ld[whsl] = ctx.n[f"wl{whsl}"]
                            if ev_h_all_last:
                                ctx.w(g, "se", ev_h_all_last)
                                ctx.w(g, "pe", ev_b2_used)
                            ctx.dinc(g.dma_start(b1cm[:], W["l1f_b1cm"][e]), "dgb")
                            ctx.dinc(g.dma_start(b2row[:], W["l1f_b2"][e:e + 1, :]), "dgb")
                            ev_b = ctx.n["dgb"]
                            ctx.w(te, "tpm0", ev_tp2_slot[0])
                            ctx.w(te, "tpm1", ev_tp2_slot[1])
                            ctx.w(te, "ve", ev_prep_done)
                            if ev_w2evac:
                                ctx.w(te, "ve", ev_w2evac)
                            for fc in range(32):
                                hh = fc // 16
                                bank = fc % 2
                                ctx.w(te, f"wl{hh}", ev_w1ld[hh])
                                if ev_hband[bank]:
                                    ctx.w(te, "se", ev_hband[bank])
                                for dc in range(NDC):
                                    mm = te.matmul(pool[:, bank, :],
                                                   w1h[hh][:, dc, (fc % 16) * 128:(fc % 16 + 1) * 128],
                                                   n2T[:, dc, :], start=(dc == 0),
                                                   stop=(dc == NDC - 1))
                                i_h = ctx.inc(mm, "pe")
                                if fc == 15:
                                    ev_w1mm_half0 = ctx.n["pe"]
                                ctx.w(s, "pe", i_h)
                                if fc == 0:
                                    ctx.w(s, "dgb", ev_b)
                                ctx.inc(s.activation(h_sb[:, fc, :], pool[:, bank, :],
                                                     AF.Silu, bias=b1cm[:, fc:fc + 1]), "se")
                                ev_hband[bank] = ctx.n["se"]
                            ev_w1mm = [ev_w1mm_half0, ctx.n["pe"]]
                            ev_h_all = ctx.n["se"]
                            ev_h_all_last = ev_h_all
                            def w2_load(ck):
                                w2sl = ck % 2
                                if ev_w2mmsl[w2sl]:
                                    ctx.w(s, "pe", ev_w2mmsl[w2sl])
                                ctx.dinc(s.dma_start(w2ch[:, w2sl, :, :],
                                                     w2b[e, :, ck * 4:(ck + 1) * 4, :]),
                                         f"w2l{w2sl}")
                                ev_w2sl[w2sl] = ctx.n[f"w2l{w2sl}"]
                            w2_load(0)
                            w2_load(1)
                            for fc in range(32):
                                ck = fc // 4
                                w2sl = ck % 2
                                if fc == 0:
                                    ctx.w(te, "se", ev_h_all)
                                if fc % 4 == 0:
                                    ctx.w(te, f"w2l{w2sl}", ev_w2sl[w2sl])
                                for tt in range(TPS):
                                    for nh in range(2):
                                        pb = pool[:, tt * 2 + nh, :]
                                        if fc == 0:
                                            mm = te.matmul(pb, ones16,
                                                           b2row[0:1, nh * 512:(nh + 1) * 512],
                                                           start=True, stop=False)
                                        mm = te.matmul(pb, h_sb[:, fc, tt * 128:(tt + 1) * 128],
                                                       w2ch[:, w2sl, fc % 4, nh * 512:(nh + 1) * 512],
                                                       start=False, stop=(fc == 31))
                                i_w2 = ctx.inc(mm, "pe")
                                if fc % 4 == 3:
                                    ev_w2mmsl[w2sl] = ctx.n["pe"]
                                    if ck + 2 < 8:
                                        w2_load(ck + 2)
                            i_w2_all = ctx.n["pe"]
                            ev_b2_used = i_w2_all
                            ctx.w(v, "pe", i_w2_all)
                            ctx.w(v, "ve", ev_prep_done)
                            for tt in range(TPS):
                                ins = v.scalar_tensor_tensor(
                                    acc[:, tt, :],
                                    pool[:, tt * 2:(tt + 1) * 2, :].rearrange("p a b -> p (a b)"),
                                    comb[:, tt, e:e + 1], acc[:, tt, :],
                                    ALU.mult, ALU.add)
                                ctx.inc(ins, "ve")
                                ctx.w(v, "ve")
                            ev_w2evac = ctx.n["ve"]
                        ev_hmm = ctx.n["pe"]
                        for tt in range(TPS):
                            i = sli * TPS + tt
                            ctx.w(sy, "ve", ev_w2evac)
                            ctx.dinc(sy.dma_start(xout[i * 128:(i + 1) * 128, :],
                                                  acc[:, tt, :]), "so_acc")
                        ev_acc_store = ctx.n["so_acc"]
    ctx.barrier()





def emit_l1b(ctx, xin, xout, W):
    nc = ctx.nc
    te, v, s, g, sy = nc.tensor, nc.vector, nc.scalar, nc.gpsimd, nc.sync
    uid = ctx.uid()
    pfx = f"rb{uid}_"
    qsp = nc.dram_tensor(f"rqsp{uid}", [D, T], BF16, kind="Internal").ap()
    ksp = nc.dram_tensor(f"rksp{uid}", [D, T], BF16, kind="Internal").ap()
    vsp = nc.dram_tensor(f"rvsp{uid}", [T, D], BF16, kind="Internal").ap()
    with ExitStack() as eso:
        def sbo(name, shape, dt):
            return es.enter_context(nc.sbuf_tensor(pfx + name, shape, dt))
        es = eso
        o_full = sbo("ofull", [128, NT, D], BF16)
        wo_w = sbo("wo", [128, NDC, D], BF16)
        borow = sbo("bo", [1, D], BF16)
        masks = sbo("masks", [128, 4, 512], F32)
        ones16 = ctx.consts["ones16"][:]

        with nc.Block() as blk:
            @blk.gpsimd
            def _(gp):
                ctx.dinc(g.dma_start(wo_w[:], W["l1b_woT"].rearrange("(c p) f -> p c f", p=128)), "dg")
                ctx.dinc(g.dma_start(borow[:], W["l1b_ob"]), "dg")
                for m in range(4):
                    ctx.inc(g.memset(masks[:, m, :], 0.0), "po")
                    ctx.w(g, "po")
                    ctx.inc(g.affine_select(masks[:, m, :], masks[:, m, :], [[-1, 512]],
                                            ALU.is_ge, -1e30, base=128 * m,
                                            channel_multiplier=1), "po")
                    ctx.w(g, "po")

        with ExitStack() as esA:
            es = esA
            qkvw = sbo("qkvw", [128, NDC, 3072], BF16)
            qkvb = sbo("qkvb", [1, 3072], BF16)
            cosT = sbo("cosT", [128, NT, 32], F32)
            sinT = sbo("sinT", [128, NT, 32], F32)
            xs = sbo("xs", [128, 2, D], F32)
            xln = sbo("xln", [128, 2, D], BF16)
            xlnT = sbo("xlnT", [128, NDC, 128], BF16)
            junk = sbo("junk", [128, D], BF16)
            stats = alloc_stats(nc, es, pfx, 2)
            qf = sbo("qf", [128, D], F32)
            kf = sbo("kf", [128, D], F32)
            t1 = sbo("t1", [128, 512], F32)
            t2 = sbo("t2", [128, 512], F32)
            rq = sbo("rq", [128, 2, D], BF16)
            rk = sbo("rk", [128, 2, D], BF16)
            vstage = sbo("vst", [128, 2, D], BF16)
            stq = sbo("stq", [128, NDC, 128], BF16)
            ps_p = [es.enter_context(nc.psum_tensor(f"{pfx}psp{i}", [128, 512], F32))
                    for i in range(2)]
            with nc.Block() as blk:
                @blk.gpsimd
                def _(gp):
                    ctx.dinc(g.dma_start(qkvw[:], W["l1b_qkvT"].rearrange("(c p) f -> p c f", p=128)), "dg")
                    ctx.dinc(g.dma_start(qkvb[:], W["l1b_qkvb"]), "dg")
                    ctx.dinc(g.dma_start(cosT[:], W["l1b_cos"][0:T, :].rearrange("(t p) i -> p t i", p=128)), "dg")
                    ctx.dinc(g.dma_start(sinT[:], W["l1b_sin"][0:T, :].rearrange("(t p) i -> p t i", p=128)), "dg")
                    w_ready = ctx.n["dg"]
                    ev_norm = [0] * NT
                    ev_tp = [0, 0]
                    ev_pbank = [0, 0]
                    ev_mm_tile = 0
                    ev_rot = [0, 0]
                    ev_rot_last = 0
                    ev_tpq = [0, 0]
                    ev_tpk = [0, 0]
                    ev_stq = 0
                    ev_sv = [0, 0]
                    ev_evac_q = 0
                    ev_evac_v = [0, 0]
                    first = True
                    for i in range(NT):
                        sl = i % 2
                        if i >= 2:
                            ctx.w(sy, "ve", ev_norm[i - 2])
                        ctx.dinc(sy.dma_start(xs[:, sl, :],
                                              xin[i * 128:(i + 1) * 128, :]), f"sx{sl}")
                        ctx.w(v, f"sx{sl}")
                        ctx.w(s, f"sx{sl}")
                        if i >= 2:
                            ctx.w(v, f"tp{sl}", ev_tp[sl])
                        ev_norm[i] = emit_ln_tile(ctx, xs[:, sl, :], xln[:, sl, :],
                                                  junk[:], stats[sl])
                        ctx.w(sy, "ve", ev_norm[i])
                        if ev_mm_tile:
                            ctx.w(sy, "pe", ev_mm_tile)
                        ctx.dinc(sy.dma_start(xlnT[:], xln[:, sl, :], transpose=True),
                                 f"tp{sl}")
                        ev_tp[sl] = ctx.n[f"tp{sl}"]
                        ctx.w(te, f"tp{sl}", ev_tp[sl])
                        if first:
                            ctx.w(te, "dg", w_ready)
                            first = False
                        for fc6 in range(6):
                            bank = fc6 % 2
                            if ev_pbank[bank]:
                                ctx.w(te, "se" if fc6 < 4 else "se", ev_pbank[bank])
                            mm = te.matmul(ps_p[bank][:], ones16,
                                           qkvb[0:1, fc6 * 512:(fc6 + 1) * 512],
                                           start=True, stop=False)
                            for dc in range(NDC):
                                mm = te.matmul(ps_p[bank][:], xlnT[:, dc, :],
                                               qkvw[:, dc, fc6 * 512:(fc6 + 1) * 512],
                                               start=False, stop=(dc == NDC - 1))
                            i_p = ctx.inc(mm, "pe")
                            ctx.w(s, "pe", i_p)
                            if fc6 < 2:
                                if fc6 == 0 and ev_rot_last:
                                    ctx.w(s, "ve", ev_rot_last)
                                ctx.inc(s.activation(qf[:, fc6 * 512:(fc6 + 1) * 512],
                                                     ps_p[bank][:], AF.Copy, scale=0.125),
                                        "se")
                            elif fc6 < 4:
                                ctx.inc(s.activation(kf[:, (fc6 - 2) * 512:(fc6 - 1) * 512],
                                                     ps_p[bank][:], AF.Copy), "se")
                            else:
                                if ev_sv[sl] and fc6 == 4:
                                    ctx.w(s, f"sv{sl}", ev_sv[sl])
                                ctx.inc(s.copy(vstage[:, sl, (fc6 - 4) * 512:(fc6 - 3) * 512],
                                               ps_p[bank][:]), "se")
                            ev_pbank[bank] = ctx.n["se"]
                        ev_mm_tile = ctx.n["pe"]
                        ev_evac_q = ctx.n["se"]
                        ctx.w(sy, "se", ev_evac_q)
                        ctx.dinc(sy.dma_start(
                            vsp[i * 128:(i + 1) * 128, :], vstage[:, sl, :]), f"sv{sl}")
                        ev_sv[sl] = ctx.n[f"sv{sl}"]
                        ctx.w(v, "se", ev_evac_q)
                        if i >= 2:
                            ctx.w(v, f"tpq{sl}", ev_tpq[sl])
                            ctx.w(v, f"tpk{sl}", ev_tpk[sl])
                        cosb = cosT[:, i, :].rearrange("p (o i) -> p o i", o=1)\
                                            .broadcast_to([128, 16, 32])
                        sinb = sinT[:, i, :].rearrange("p (o i) -> p o i", o=1)\
                                            .broadcast_to([128, 16, 32])
                        for src_f, dst in ((qf, rq), (kf, rk)):
                            sv = src_f[:].rearrange("p (h j) -> p h j", h=16)
                            dv = dst[:, sl, :].rearrange("p (h j) -> p h j", h=16)
                            ctx.inc(v.tensor_tensor(t1[:].rearrange("p (h j) -> p h j", h=16),
                                                    sv[:, :, 0:32], cosb, ALU.mult), "ve")
                            ctx.inc(v.tensor_tensor(t2[:].rearrange("p (h j) -> p h j", h=16),
                                                    sv[:, :, 32:64], sinb, ALU.mult), "ve")
                            ctx.w(v, "ve")
                            ctx.inc(v.tensor_tensor(dv[:, :, 0:32],
                                                    t1[:].rearrange("p (h j) -> p h j", h=16),
                                                    t2[:].rearrange("p (h j) -> p h j", h=16),
                                                    ALU.subtract), "ve")
                            ctx.w(v, "ve")
                            ctx.inc(v.tensor_tensor(t1[:].rearrange("p (h j) -> p h j", h=16),
                                                    sv[:, :, 0:32], sinb, ALU.mult), "ve")
                            ctx.inc(v.tensor_tensor(t2[:].rearrange("p (h j) -> p h j", h=16),
                                                    sv[:, :, 32:64], cosb, ALU.mult), "ve")
                            ctx.w(v, "ve")
                            ctx.inc(v.tensor_tensor(dv[:, :, 32:64],
                                                    t1[:].rearrange("p (h j) -> p h j", h=16),
                                                    t2[:].rearrange("p (h j) -> p h j", h=16),
                                                    ALU.add), "ve")
                            ctx.w(v, "ve")
                        ev_rot[sl] = ctx.n["ve"]
                        ev_rot_last = ev_rot[sl]
                        ctx.w(sy, "ve", ev_rot[sl])
                        if ev_stq:
                            ctx.w(sy, "sq", ev_stq)
                        ctx.dinc(sy.dma_start(stq[:], rq[:, sl, :], transpose=True), f"tpq{sl}")
                        ev_tpq[sl] = ctx.n[f"tpq{sl}"]
                        ctx.w(sy, f"tpq{sl}", ev_tpq[sl])
                        ctx.dinc(sy.dma_start(
                            qsp[:, i * 128:(i + 1) * 128].rearrange("(c p) t -> p c t", p=128),
                            stq[:]), "sq")
                        ev_stq = ctx.n["sq"]
                        ctx.w(sy, "sq", ev_stq)
                        ctx.dinc(sy.dma_start(stq[:], rk[:, sl, :], transpose=True), f"tpk{sl}")
                        ev_tpk[sl] = ctx.n[f"tpk{sl}"]
                        ctx.w(sy, f"tpk{sl}", ev_tpk[sl])
                        ctx.dinc(sy.dma_start(
                            ksp[:, i * 128:(i + 1) * 128].rearrange("(c p) t -> p c t", p=128),
                            stq[:]), "sq")
                        ev_stq = ctx.n["sq"]

        with ExitStack() as esB:
            es = esB
            s_sb = sbo("s_sb", [128, 2, T], F32)
            p_sb = sbo("p_sb", [128, 2, T], BF16)
            pt_sb = sbo("pt", [128, 2, NT, 128], BF16)
            k_h = sbo("k_h", [64, 2, T], BF16)
            v_h = sbo("v_h", [128, 2, NT, 64], BF16)
            q_st = sbo("q_st", [64, 2, 128], BF16)
            mcol = sbo("mcol", [128, 2], F32)
            negm = sbo("negm", [128, 2], F32)
            lcol = sbo("lcol", [128, 2], F32)
            rlcol = sbo("rlcol", [128, 2], F32)
            ps_s = [es.enter_context(nc.psum_tensor(f"{pfx}pss{i}", [128, 512], F32))
                    for i in range(2)]
            ps_pv = [es.enter_context(nc.psum_tensor(f"{pfx}pspv{i}", [128, 64], F32))
                     for i in range(2)]
            with nc.Block() as blk:
                @blk.gpsimd
                def _(gp):
                    ctx.w(te, "se")
                    ctx.w(te, "ve")
                    ctx.w(sy, "sq")
                    ctx.w(sy, "sv0")
                    ctx.w(sy, "sv1")
                    ev_ssbank = [0, 0]
                    ev_exp_slot = [0, 0]
                    ev_recip_slot = [0, 0]
                    ev_pt_slot = [0, 0]
                    ev_pv_slot = [0, 0]
                    ev_pvbank = [0, 0]
                    ev_kv = [0, 0]
                    ev_scores_q = [0, 0]
                    ev_mask_slot = [0, 0]

                    def emit_front(h, qt, hsl):
                        qsl = qt % 2
                        nv = qt // 4 + 1
                        if ev_scores_q[qsl]:
                            ctx.w(sy, "pe", ev_scores_q[qsl])
                        ctx.dinc(sy.dma_start(q_st[:, qsl, :],
                                              qsp[h * 64:(h + 1) * 64,
                                                  qt * 128:(qt + 1) * 128]), f"qh{qsl}")
                        ctx.w(te, f"qh{qsl}")
                        for kt in range(nv):
                            bank = kt % 2
                            if ev_ssbank[bank]:
                                ctx.w(te, "se", ev_ssbank[bank])
                            mm = te.matmul(ps_s[bank][:], q_st[:, qsl, :],
                                           k_h[:, hsl, kt * 512:(kt + 1) * 512],
                                           start=True, stop=True)
                            i_s = ctx.inc(mm, "pe")
                            ctx.w(s, "pe", i_s)
                            if kt == 0:
                                if ev_exp_slot[qsl]:
                                    ctx.w(s, "se", ev_exp_slot[qsl])
                                if ev_mask_slot[qsl]:
                                    ctx.w(s, "ve", ev_mask_slot[qsl])
                            ctx.inc(s.activation(s_sb[:, qsl, kt * 512:(kt + 1) * 512],
                                                 ps_s[bank][:], AF.Copy), "se")
                            ev_ssbank[bank] = ctx.n["se"]
                        ev_scores_q[qsl] = ctx.n["pe"]
                        ev_sevac = ctx.n["se"]
                        ctx.w(v, "se", ev_sevac)
                        ctx.inc(v.tensor_tensor(
                            s_sb[:, qsl, (nv - 1) * 512:nv * 512],
                            s_sb[:, qsl, (nv - 1) * 512:nv * 512],
                            masks[:, qt % 4, :], ALU.add), "ve")
                        ev_mask_slot[qsl] = ctx.n["ve"]
                        ctx.w(v, "ve")
                        ctx.inc(v.reduce_max(mcol[:, qsl:qsl + 1],
                                             s_sb[:, qsl, 0:nv * 512], axis=AX.X), "ve")
                        ctx.w(v, "ve")
                        if ev_exp_slot[qsl]:
                            ctx.w(v, "se", ev_exp_slot[qsl])
                        ctx.inc(v.tensor_scalar(negm[:, qsl:qsl + 1], mcol[:, qsl:qsl + 1],
                                                -1.0, None, ALU.mult), "ve")
                        ev_rmax = ctx.n["ve"]
                        ctx.w(s, "ve", ev_rmax)
                        if ev_pt_slot[qsl]:
                            ctx.w(s, f"pt{qsl}", ev_pt_slot[qsl])
                        if ev_recip_slot[qsl]:
                            ctx.w(s, "ve", ev_recip_slot[qsl])
                        ev_exp_slot[qsl] = ctx.inc(
                            s.activation(p_sb[:, qsl, 0:nv * 512],
                                         s_sb[:, qsl, 0:nv * 512], AF.Exp,
                                         bias=negm[:, qsl:qsl + 1],
                                         accum_out=lcol[:, qsl:qsl + 1]), "se")
                        ctx.w(v, "se", ev_exp_slot[qsl])
                        ctx.inc(v.reciprocal(rlcol[:, qsl:qsl + 1], lcol[:, qsl:qsl + 1]),
                                "ve")
                        ev_recip_slot[qsl] = ctx.n["ve"]
                        ctx.w(sy, "se", ev_exp_slot[qsl])
                        if ev_pv_slot[qsl]:
                            ctx.w(sy, "pe", ev_pv_slot[qsl])
                        ctx.dinc(sy.dma_start(pt_sb[:, qsl, 0:nv * 4, :],
                                              p_sb[:, qsl, 0:nv * 512], transpose=True),
                                 f"pt{qsl}")
                        ev_pt_slot[qsl] = ctx.n[f"pt{qsl}"]

                    def emit_back(h, qt, hsl):
                        qsl = qt % 2
                        nkc = qt + 1
                        ctx.w(te, f"pt{qsl}", ev_pt_slot[qsl])
                        bank = qsl
                        if ev_pvbank[bank]:
                            ctx.w(te, "ve", ev_pvbank[bank])
                        for kc in range(nkc):
                            mm = te.matmul(ps_pv[bank][:], pt_sb[:, qsl, kc, :],
                                           v_h[:, hsl, kc, :],
                                           start=(kc == 0), stop=(kc == nkc - 1))
                        i_pv = ctx.inc(mm, "pe")
                        ev_pv_slot[qsl] = ctx.n["pe"]
                        ctx.w(v, "pe", i_pv)
                        ctx.w(v, "ve", ev_recip_slot[qsl])
                        ctx.inc(v.tensor_scalar(o_full[:, qt, h * 64:(h + 1) * 64],
                                                ps_pv[bank][:], rlcol[:, qsl:qsl + 1],
                                                None, ALU.mult), "ve")
                        ev_pvbank[bank] = ctx.n["ve"]

                    for h in range(16):
                        hsl = h % 2
                        if h >= 2:
                            ctx.w(sy, "pe", ev_kv[hsl])
                        ctx.dinc(sy.dma_start(k_h[:, hsl, :],
                                              ksp[h * 64:(h + 1) * 64, :]), f"kh{hsl}")
                        ctx.dinc(sy.dma_start(
                            v_h[:, hsl, :, :],
                            vsp[:, h * 64:(h + 1) * 64].rearrange("(i p) d -> p i d", p=128)),
                            f"kh{hsl}")
                        ctx.w(te, f"kh{hsl}")
                        for qt in range(NT):
                            emit_front(h, qt, hsl)
                            if qt >= 1:
                                emit_back(h, qt - 1, hsl)
                        emit_back(h, NT - 1, hsl)
                        ev_kv[hsl] = ctx.n["pe"]

        with ExitStack() as esC:
            es = esC
            oT = sbo("oT", [128, NDC, 128], BF16)
            xold = sbo("xold", [128, 1, D], F32)
            onew = sbo("onew", [128, 2, D], F32)
            ps_c = es.enter_context(nc.psum_tensor(pfx + "psc", [128, 1024], F32))
            with nc.Block() as blk:
                @blk.gpsimd
                def _(gp):
                    ctx.w(te, "ve")
                    ctx.w(sy, "ve")
                    ev_stt = 0
                    ev_so_slot = [0, 0]
                    ev_cmm = 0
                    for qt in range(NT):
                        sl = qt % 2
                        if ev_cmm:
                            ctx.w(sy, "pe", ev_cmm)
                        ctx.dinc(sy.dma_start(oT[:], o_full[:, qt, :], transpose=True),
                                 "pt0")
                        ctx.w(te, "pt0")
                        if ev_stt:
                            ctx.w(te, "ve", ev_stt)
                        for nh in range(2):
                            mm = te.matmul(ps_c[:, nh * 512:(nh + 1) * 512], ones16,
                                           borow[0:1, nh * 512:(nh + 1) * 512],
                                           start=True, stop=False)
                            for dc in range(NDC):
                                mm = te.matmul(ps_c[:, nh * 512:(nh + 1) * 512],
                                               oT[:, dc, :],
                                               wo_w[:, dc, nh * 512:(nh + 1) * 512],
                                               start=False, stop=(dc == NDC - 1))
                        i_c = ctx.inc(mm, "pe")
                        ev_cmm = ctx.n["pe"]
                        if ev_stt:
                            ctx.w(sy, "ve", ev_stt)
                        ctx.dinc(sy.dma_start(xold[:, 0, :],
                                              xin[qt * 128:(qt + 1) * 128, :]), "xo0")
                        ctx.w(v, "pe", i_c)
                        ctx.w(v, "xo0")
                        if ev_so_slot[sl]:
                            ctx.w(v, f"so{sl}", ev_so_slot[sl])
                        ins = v.scalar_tensor_tensor(onew[:, sl, :], ps_c[:], 1.0,
                                                     xold[:, 0, :], ALU.mult, ALU.add)
                        ctx.inc(ins, "ve")
                        ev_stt = ctx.n["ve"]
                        ctx.w(sy, "ve", ev_stt)
                        ctx.dinc(sy.dma_start(xout[qt * 128:(qt + 1) * 128, :],
                                              onew[:, sl, :]), f"so{sl}")
                        ev_so_slot[sl] = ctx.n[f"so{sl}"]
    ctx.barrier()



def _prep_weights(inp):
    f32 = lambda a: np.ascontiguousarray(np.asarray(a), dtype=np.float32)
    W = {}
    W["l0a_QT"] = f32(inp["L0_attn_a_Q"].T)
    W["l0a_KT"] = f32(inp["L0_attn_a_K"].T)
    W["l0a_VO"] = f32(inp["L0_attn_a_VO"])
    W["l0f_wiT"] = f32(inp["L0_ffn_wi_W"].T)
    W["l0f_woT"] = f32(inp["L0_ffn_wo_W"].T)
    W["l0f_gateT"] = f32(inp["L0_ffn_gate_W"].T)
    W["l0f_gate_b"] = f32(inp["L0_ffn_gate_b"]).reshape(1, 8)
    A = np.asarray(inp["L0_ffn_lora_A"])
    W["l0f_AallT"] = f32(A.transpose(2, 0, 1).reshape(1024, 32))
    Bm = np.asarray(inp["L0_ffn_lora_B"])
    W["l0f_BallT"] = f32(Bm.transpose(0, 2, 1).reshape(32, 4096))
    W["l0f_wib_cm"] = f32(np.asarray(inp["L0_ffn_wi_b"]).reshape(32, 128).T)
    W["l0f_wo_b"] = f32(inp["L0_ffn_wo_b"]).reshape(1, 1024)
    W["l0b_qkW"] = f32(np.concatenate([np.asarray(inp["L0_attn_b_Wq"]),
                                       np.asarray(inp["L0_attn_b_Wk"])], axis=1))
    W["l0b_wv"] = f32(inp["L0_attn_b_Wv"])
    W["l0b_wo"] = f32(inp["L0_attn_b_Wo"])
    bqk = np.concatenate([np.asarray(inp["L0_attn_b_bq"]),
                          np.asarray(inp["L0_attn_b_bk"])])
    W["l0b_bqk_cm"] = f32(bqk.reshape(16, 128).T)
    W["l0b_bv"] = f32(inp["L0_attn_b_bv"]).reshape(1, 1024)
    W["l0b_bo"] = f32(inp["L0_attn_b_bo"]).reshape(1, 1024)
    W["l1a_rot"] = f32(inp["L1_attn_a_rot_w"])
    W["l1a_scb"] = f32(np.stack([np.asarray(inp["L1_attn_a_scale"]).reshape(()),
                                 np.asarray(inp["L1_attn_a_bias"]).reshape(())])).reshape(1, 2)
    W["l1f_g1r"] = f32(inp["L1_ffn_g1"]).reshape(1, 1024)
    W["l1f_be1r"] = f32(inp["L1_ffn_be1"]).reshape(1, 1024)
    W["l1f_mixw"] = f32(inp["L1_ffn_mix_w"])
    W["l1f_mixb"] = f32(inp["L1_ffn_mix_b"]).reshape(1, 1024)
    W["l1f_g2r"] = f32(inp["L1_ffn_g2"]).reshape(1, 1024)
    W["l1f_be2r"] = f32(inp["L1_ffn_be2"]).reshape(1, 1024)
    W["l1f_routw"] = f32(inp["L1_ffn_router_w"])
    W["l1f_w1"] = f32(inp["L1_ffn_w1"])
    W["l1f_w2"] = f32(inp["L1_ffn_w2"])
    b1 = np.asarray(inp["L1_ffn_b1"])
    W["l1f_b1cm"] = f32(np.stack([b1[e].reshape(32, 128).T for e in range(8)]))
    W["l1f_b2"] = f32(inp["L1_ffn_b2"])
    W["l1b_qkvT"] = f32(np.asarray(inp["L1_attn_b_qkv_w"]).T)
    W["l1b_qkvb"] = f32(inp["L1_attn_b_qkv_b"]).reshape(1, 3072)
    W["l1b_woT"] = f32(np.asarray(inp["L1_attn_b_out_w"]).T)
    W["l1b_ob"] = f32(inp["L1_attn_b_out_b"]).reshape(1, 1024)
    half = 32
    inv_freq = 1.0 / 10000.0 ** (np.arange(half, dtype=np.float64) / half)
    pos = np.arange(4096, dtype=np.float64)
    ang = np.outer(pos, inv_freq)
    W["l1b_cos"] = np.cos(ang).astype(np.float32)
    W["l1b_sin"] = np.sin(ang).astype(np.float32)
    return W


SUBLAYERS = [emit_l0a, emit_l0ffn, emit_l0b, emit_l0ffn, emit_l1a, emit_l1ffn, emit_l1b, emit_l1ffn]


def build_program(W_shapes, sublayers=None):
    sublayers = sublayers or SUBLAYERS
    nc = bass.Bass("TRN2", target_bir_lowering=False)
    xin = nc.dram_tensor("x_in", [T, D], F32, kind="ExternalInput").ap()
    xout = nc.dram_tensor("x_out", [T, D], F32, kind="ExternalOutput").ap()
    W = {name: nc.dram_tensor(name, list(shape), F32, kind="ExternalInput").ap()
         for name, shape in W_shapes.items()}
    with ExitStack() as es:
        ctx = Ctx(nc, es)
        streams = [xin]
        for i in range(len(sublayers) - 1):
            streams.append(nc.dram_tensor(f"x_mid{i}", [T, D], F32, kind="Internal").ap())
        streams.append(xout)
        emit_consts(ctx)
        for i, emit in enumerate(sublayers):
            emit(ctx, streams[i], streams[i + 1], W)
    return nc


def kernel(**inputs):
    x = np.ascontiguousarray(inputs["L0_attn_a_x"], dtype=np.float32)
    W = _prep_weights(inputs)
    nc = build_program({k: v.shape for k, v in W.items()})
    in_maps = []
    for c in range(8):
        m = {"x_in": np.ascontiguousarray(x[c])}
        m.update(W)
        in_maps.append(m)
    res = run_bass_kernel_spmd(nc, in_maps, core_ids=list(range(8)))
    return np.stack([r["x_out"] for r in res.results], axis=0)


# version 05
# speedup vs baseline: 1.0211x; 1.0142x over previous
import numpy as np
from contextlib import ExitStack

import concourse.bass as bass
import concourse.mybir as mybir
from concourse.bass_utils import run_bass_kernel_spmd

F32 = mybir.dt.float32
BF16 = mybir.dt.bfloat16
AF = mybir.ActivationFunctionType
ALU = mybir.AluOpType
AX = mybir.AxisListType

T = 4096
D = 1024
NT = T // 128
NDC = D // 128
NKT = T // 512


class Ctx:

    def __init__(self, nc, es):
        self.nc = nc
        self.es = es
        self._sems = {}
        self.n = {}

    def uid(self):
        self._uid = getattr(self, "_uid", 0) + 1
        return self._uid

    def sem(self, name):
        if name not in self._sems:
            self._sems[name] = self.es.enter_context(self.nc.semaphore(name))
            self.n[name] = 0
        return self._sems[name]

    def inc(self, instr, name, step=1):
        instr.then_inc(self.sem(name), step)
        self.n[name] += step
        return self.n[name]

    def dinc(self, instr, name):
        return self.inc(instr, name, 16)

    def w(self, eng, name, val=None):
        v = self.n.setdefault(name, 0) if val is None else val
        self.sem(name)
        if v > 0:
            eng.wait_ge(self._sems[name], v)

    def barrier(self):
        nc = self.nc
        with nc.Block() as blk:
            @blk.gpsimd
            def _(gp):
                snap = dict(self.n)
                for eng in [nc.tensor, nc.vector, nc.scalar, nc.gpsimd, nc.sync]:
                    for name, val in snap.items():
                        if val > 0:
                            eng.wait_ge(self._sems[name], val)



def emit_ln_tile(ctx, xs_ap, xln_ap, junk_ap, stat):
    nc = ctx.nc
    v, s = nc.vector, nc.scalar
    i_sq = ctx.inc(s.activation(junk_ap, xs_ap, AF.Square, accum_out=stat["Q"]), "se")
    ctx.inc(v.reduce_sum(stat["S"], xs_ap, axis=AX.X), "ve")
    ctx.w(v, "ve")
    ctx.inc(v.tensor_scalar(stat["negmean"], stat["S"], -1.0 / D, None, ALU.mult), "ve")
    ctx.w(v, "ve")
    ctx.w(v, "se", i_sq)
    ctx.inc(v.tensor_scalar(stat["varr"], stat["Q"], 1.0 / D, 1e-5, ALU.mult, ALU.add), "ve")
    ctx.w(v, "ve")
    ctx.inc(v.tensor_tensor(stat["mm"], stat["negmean"], stat["negmean"], ALU.mult), "ve")
    ctx.w(v, "ve")
    ctx.inc(v.tensor_tensor(stat["varr"], stat["varr"], stat["mm"], ALU.subtract), "ve")
    i_var = ctx.n["ve"]
    ctx.w(s, "ve", i_var)
    i_sqrt = ctx.inc(s.activation(stat["rstd"], stat["varr"], AF.Sqrt), "se")
    ctx.w(v, "se", i_sqrt)
    ctx.inc(v.reciprocal(stat["rstd"], stat["rstd"]), "ve")
    ctx.w(v, "ve")
    ctx.inc(v.tensor_scalar(xln_ap, xs_ap, stat["negmean"], stat["rstd"],
                            ALU.add, ALU.mult), "ve")
    return ctx.n["ve"]


def alloc_stats(nc, es, prefix, nslots=2):
    out = []
    for sl in range(nslots):
        d = {}
        for nm in ["S", "Q", "negmean", "varr", "mm", "rstd"]:
            t = es.enter_context(nc.sbuf_tensor(f"{prefix}_{nm}{sl}", [128, 1], F32))
            d[nm] = t[:]
        out.append(d)
    return out



def emit_l0a(ctx, xin, xout, W):
    nc = ctx.nc
    te, v, s, g, sy = nc.tensor, nc.vector, nc.scalar, nc.gpsimd, nc.sync
    pfx = f"a{ctx.uid()}_"
    with ExitStack() as es:
        def sb(name, shape, dt):
            return es.enter_context(nc.sbuf_tensor(pfx + name, shape, dt))

        qtw = sb("qtw", [128, NDC, 64], BF16)
        ktw = sb("ktw", [128, NDC, 64], BF16)
        vow = sb("vow", [128, NDC, D], BF16)
        xqT = sb("xqT", [64, T], BF16)
        xkT = sb("xkT", [64, T], BF16)
        xv = sb("xv", [128, NT, D], BF16)
        xs = sb("xs", [128, 2, D], F32)
        xln = sb("xln", [128, 2, D], BF16)
        xlnT = sb("xlnT", [128, NDC, 512], BF16)
        junk = sb("junk", [128, D], BF16)
        stats = alloc_stats(nc, es, pfx, 2)
        s_sb = sb("s_sb", [128, T], F32)
        p_sb = sb("p_sb", [128, T], BF16)
        pt_sb = sb("pt", [128, NT, 128], BF16)
        mcol = sb("mcol", [128, 1], F32)
        negm = sb("negm", [128, 1], F32)
        lcol = sb("lcol", [128, 1], F32)
        rlcol = sb("rlcol", [128, 1], F32)
        xold = sb("xold", [128, 2, D], F32)
        onew = sb("onew", [128, 2, D], F32)
        ps_q = es.enter_context(nc.psum_tensor(pfx + "psq", [64, 512], F32))
        ps_k = es.enter_context(nc.psum_tensor(pfx + "psk", [64, 512], F32))
        ps_v = [es.enter_context(nc.psum_tensor(f"{pfx}psv{i}", [128, 512], F32))
                for i in range(2)]
        ps_s = [es.enter_context(nc.psum_tensor(f"{pfx}pss{i}", [128, 512], F32))
                for i in range(2)]
        ps_o = es.enter_context(nc.psum_tensor(pfx + "pso", [128, 1024], F32))

        with nc.Block() as blk:
            @blk.gpsimd
            def _(gp):
                ctx.dinc(g.dma_start(qtw[:], W["l0a_QT"].rearrange("(c p) r -> p c r", p=128)), "dg")
                ctx.dinc(g.dma_start(ktw[:], W["l0a_KT"].rearrange("(c p) r -> p c r", p=128)), "dg")
                ctx.dinc(g.dma_start(vow[:], W["l0a_VO"].rearrange("(c p) n -> p c n", p=128)), "dg")
                w_ready = ctx.n["dg"]

                ev_norm = [0] * NT
                ev_tp = [0] * NT
                ev_gproj = [0] * (NT // 4)
                ev_vevac = [0] * NT
                ev_vbank = [0, 0]
                ev_qkevac = 0
                vuse = 0

                for gi in range(NT // 4):
                    for ii in range(4):
                        i = gi * 4 + ii
                        sl = i % 2
                        if i >= 2:
                            ctx.w(sy, "ve", ev_norm[i - 2])
                        ctx.dinc(sy.dma_start(xs[:, sl, :],
                                              xin[i * 128:(i + 1) * 128, :]), f"sx{sl}")
                        ctx.w(v, f"sx{sl}")
                        ctx.w(s, f"sx{sl}")
                        if i >= 2:
                            ctx.w(v, f"tp{sl}", ev_tp[i - 2])
                        ev_norm[i] = emit_ln_tile(ctx, xs[:, sl, :], xln[:, sl, :],
                                                  junk[:], stats[sl])
                        ctx.w(sy, "ve", ev_norm[i])
                        if gi >= 1 and ii == 0:
                            ctx.w(sy, "pe", ev_gproj[gi - 1])
                        ctx.dinc(sy.dma_start(xlnT[:, :, ii * 128:(ii + 1) * 128],
                                              xln[:, sl, :], transpose=True), f"tp{sl}")
                        ev_tp[i] = ctx.n[f"tp{sl}"]

                    ctx.w(te, "tp0", ev_tp[gi * 4 + 2])
                    ctx.w(te, "tp1", ev_tp[gi * 4 + 3])
                    if gi == 0:
                        ctx.w(te, "dg", w_ready)
                    if gi >= 1:
                        ctx.w(te, "se", ev_qkevac)
                    for dc in range(NDC):
                        mm = te.matmul(ps_q[:], qtw[:, dc, :], xlnT[:, dc, :],
                                       start=(dc == 0), stop=(dc == NDC - 1))
                    i_q = ctx.inc(mm, "pe")
                    for dc in range(NDC):
                        mm = te.matmul(ps_k[:], ktw[:, dc, :], xlnT[:, dc, :],
                                       start=(dc == 0), stop=(dc == NDC - 1))
                    i_k = ctx.inc(mm, "pe")
                    for ii in range(4):
                        i = gi * 4 + ii
                        for nh in range(2):
                            bank = vuse % 2
                            vuse += 1
                            if ev_vbank[bank]:
                                ctx.w(te, "ve", ev_vbank[bank])
                            for dc in range(NDC):
                                mm = te.matmul(ps_v[bank][:],
                                               xlnT[:, dc, ii * 128:(ii + 1) * 128],
                                               vow[:, dc, nh * 512:(nh + 1) * 512],
                                               start=(dc == 0), stop=(dc == NDC - 1))
                            i_v = ctx.inc(mm, "pe")
                            ctx.w(v, "pe", i_v)
                            ctx.inc(v.tensor_copy(xv[:, i, nh * 512:(nh + 1) * 512],
                                                  ps_v[bank][:]), "ve")
                            ev_vbank[bank] = ctx.n["ve"]
                            ev_vevac[i] = ctx.n["ve"]
                    ev_gproj[gi] = ctx.n["pe"]
                    ctx.w(s, "pe", i_q)
                    ctx.inc(s.copy(xqT[:, gi * 512:(gi + 1) * 512], ps_q[:]), "se")
                    ctx.w(s, "pe", i_k)
                    ctx.inc(s.copy(xkT[:, gi * 512:(gi + 1) * 512], ps_k[:]), "se")
                    ev_qkevac = ctx.n["se"]

                ev_rmax = 0
                ev_exp = 0
                ev_pt = 0
                ev_stt = 0
                ev_recip = 0
                ev_pv = 0
                ev_ssbank = [0, 0]
                ev_stt_slot = [0, 0]
                ev_so_slot = [0, 0]
                for qt in range(NT):
                    sl = qt % 2
                    for kt in range(NKT):
                        bank = kt % 2
                        if ev_ssbank[bank]:
                            ctx.w(te, "se", ev_ssbank[bank])
                        if qt == 0 and kt == 0:
                            ctx.w(te, "se", ev_qkevac)
                        mm = te.matmul(ps_s[bank][:], xqT[:, qt * 128:(qt + 1) * 128],
                                       xkT[:, kt * 512:(kt + 1) * 512],
                                       start=True, stop=True)
                        i_s = ctx.inc(mm, "pe")
                        ctx.w(s, "pe", i_s)
                        if qt >= 1 and kt == 0:
                            ctx.w(s, "ve", ev_rmax)
                        ctx.inc(s.activation(s_sb[:, kt * 512:(kt + 1) * 512],
                                             ps_s[bank][:], AF.Copy, scale=1.0 / 32.0),
                                "se")
                        ev_ssbank[bank] = ctx.n["se"]
                    ev_sevac = ctx.n["se"]
                    ctx.w(v, "se", ev_sevac)
                    ctx.inc(v.reduce_max(mcol[:], s_sb[:], axis=AX.X), "ve")
                    ctx.w(v, "ve")
                    if qt >= 1:
                        ctx.w(v, "se", ev_exp)
                    ctx.inc(v.tensor_scalar(negm[:], mcol[:], -1.0, None, ALU.mult), "ve")
                    ev_rmax = ctx.n["ve"]
                    ctx.w(s, "ve", ev_rmax)
                    if qt >= 1:
                        ctx.w(s, "pt0", ev_pt)
                        ctx.w(s, "ve", ev_recip)
                    ev_exp = ctx.inc(s.activation(p_sb[:], s_sb[:], AF.Exp, bias=negm[:],
                                                  accum_out=lcol[:]), "se")
                    ctx.w(v, "se", ev_exp)
                    ctx.inc(v.reciprocal(rlcol[:], lcol[:]), "ve")
                    ev_recip = ctx.n["ve"]
                    ctx.w(sy, "se", ev_exp)
                    if qt >= 1:
                        ctx.w(sy, "pe", ev_pv)
                    ctx.dinc(sy.dma_start(pt_sb[:], p_sb[:], transpose=True), "pt0")
                    ev_pt = ctx.n["pt0"]
                    ctx.w(te, "pt0", ev_pt)
                    if qt >= 1:
                        ctx.w(te, "ve", ev_stt)
                    if qt == 0:
                        ctx.w(te, "ve", ev_vevac[NT - 1])
                    for nh in range(2):
                        for kc in range(NT):
                            mm = te.matmul(ps_o[:, nh * 512:(nh + 1) * 512],
                                           pt_sb[:, kc, :],
                                           xv[:, kc, nh * 512:(nh + 1) * 512],
                                           start=(kc == 0), stop=(kc == NT - 1))
                        ctx.inc(mm, "pe")
                    ev_pv = ctx.n["pe"]
                    if qt >= 2:
                        ctx.w(sy, "ve", ev_stt_slot[sl])
                    ctx.dinc(sy.dma_start(xold[:, sl, :],
                                          xin[qt * 128:(qt + 1) * 128, :]), f"xo{sl}")
                    ctx.w(v, "pe", ev_pv)
                    ctx.w(v, f"xo{sl}")
                    ctx.w(v, "ve", ev_recip)
                    if qt >= 2:
                        ctx.w(v, f"so{sl}", ev_so_slot[sl])
                    for nh in range(2):
                        ins = v.scalar_tensor_tensor(
                            onew[:, sl, nh * 512:(nh + 1) * 512],
                            ps_o[:, nh * 512:(nh + 1) * 512], rlcol[:],
                            xold[:, sl, nh * 512:(nh + 1) * 512], ALU.mult, ALU.add)
                    ctx.inc(ins, "ve")
                    ev_stt = ctx.n["ve"]
                    ev_stt_slot[sl] = ev_stt
                    ctx.w(sy, "ve", ev_stt)
                    ctx.dinc(sy.dma_start(xout[qt * 128:(qt + 1) * 128, :],
                                          onew[:, sl, :]), f"so{sl}")
                    ev_so_slot[sl] = ctx.n[f"so{sl}"]
    ctx.barrier()




def emit_consts(ctx):
    nc = ctx.nc
    es = ctx.es
    C = {}
    C["ones16"] = es.enter_context(nc.sbuf_tensor("c_ones16", [1, 128], BF16))
    C["ones32"] = es.enter_context(nc.sbuf_tensor("c_ones32", [1, 128], F32))
    C["ident32"] = es.enter_context(nc.sbuf_tensor("c_id32", [128, 128], F32))
    C["ident16"] = es.enter_context(nc.sbuf_tensor("c_id16", [128, 128], BF16))
    C["w8"] = es.enter_context(nc.sbuf_tensor("c_w8", [128, 8], F32))
    ctx.consts = C
    g = nc.gpsimd
    with nc.Block() as blk:
        @blk.gpsimd
        def _(gp):
            ctx.inc(g.memset(C["ones16"][:], 1.0), "po")
            ctx.inc(g.memset(C["ones32"][:], 1.0), "po")
            ctx.inc(g.memset(C["ident32"][:], 0.0), "po")
            ctx.w(g, "po")
            ctx.inc(g.affine_select(C["ident32"][:], C["ident32"][:], [[-1, 128]],
                                    ALU.not_equal, 1.0, base=0, channel_multiplier=1), "po")
            ctx.inc(g.memset(C["ident16"][:], 0.0), "po")
            ctx.w(g, "po")
            ctx.inc(g.affine_select(C["ident16"][:], C["ident16"][:], [[-1, 128]],
                                    ALU.not_equal, 1.0, base=0, channel_multiplier=1), "po")
            ctx.inc(g.iota(C["w8"][:], [[-1, 8]], base=8, channel_multiplier=0,
                           allow_small_or_imprecise_dtypes=True), "po")
    ctx.barrier()


def alloc_topk(nc, es, prefix):
    B = {}
    for nm, shape in [("m", [128, 1]), ("negm", [128, 1]), ("l", [128, 1]),
                      ("rl", [128, 1]), ("s8", [128, 1]), ("m1", [128, 1]),
                      ("m2", [128, 1]), ("e8", [128, 8]), ("pr", [128, 8]),
                      ("eq", [128, 8]), ("t8", [128, 8]), ("raw", [128, 8]),
                      ("lg2", [128, 8])]:
        t = es.enter_context(nc.sbuf_tensor(f"{prefix}_tk_{nm}", shape, F32))
        B[nm] = t[:]
    return B


def emit_probs(ctx, lg_ap, B):
    nc = ctx.nc
    v, s = nc.vector, nc.scalar
    ctx.w(v, "se")
    ctx.w(s, "ve")
    ctx.inc(v.reduce_max(B["m"], lg_ap, axis=AX.X), "ve")
    ctx.w(v, "ve")
    ctx.inc(v.tensor_scalar(B["negm"], B["m"], -1.0, None, ALU.mult), "ve")
    ctx.w(s, "ve")
    ctx.inc(s.activation(B["e8"], lg_ap, AF.Exp, bias=B["negm"], accum_out=B["l"]), "se")
    ctx.w(v, "se")
    ctx.inc(v.reciprocal(B["rl"], B["l"]), "ve")
    ctx.w(v, "ve")
    ctx.inc(v.tensor_scalar(B["pr"], B["e8"], B["rl"], None, ALU.mult), "ve")
    ctx.w(v, "ve")
    return ctx.n["ve"]


def emit_argmax_mask(ctx, src_ap, mask_ap, B, save_max=None):
    nc = ctx.nc
    v, s = nc.vector, nc.scalar
    w8 = ctx.consts["w8"][:]
    ctx.inc(v.reduce_max(B["m"], src_ap, axis=AX.X), "ve")
    ctx.w(v, "ve")
    if save_max is not None:
        ctx.inc(v.tensor_copy(save_max, B["m"]), "ve")
    ctx.inc(v.tensor_scalar(B["negm"], B["m"], -1.0, None, ALU.mult), "ve")
    ctx.w(s, "ve")
    ctx.inc(s.activation(B["eq"], src_ap, AF.Sign, bias=B["negm"]), "se")
    ctx.w(v, "se")
    ctx.inc(v.tensor_scalar(B["eq"], B["eq"], 1.0, None, ALU.add), "ve")
    ctx.w(v, "ve")
    ctx.inc(v.tensor_tensor(B["t8"], B["eq"], w8, ALU.mult), "ve")
    ctx.w(v, "ve")
    ctx.inc(v.reduce_max(B["s8"], B["t8"], axis=AX.X), "ve")
    ctx.w(v, "ve")
    ctx.inc(v.tensor_scalar(B["s8"], B["s8"], -1.0, None, ALU.mult), "ve")
    ctx.w(s, "ve")
    ctx.inc(s.activation(B["raw"], B["t8"], AF.Sign, bias=B["s8"]), "se")
    ctx.w(v, "se")
    ctx.inc(v.tensor_scalar(mask_ap, B["raw"], 1.0, None, ALU.add), "ve")
    ctx.w(v, "ve")
    return ctx.n["ve"]



GRP = 256


def emit_l0ffn(ctx, xin, xout, W, dbg=None):
    nc = ctx.nc
    te, v, s, g, sy = nc.tensor, nc.vector, nc.scalar, nc.gpsimd, nc.sync
    pfx = f"f{ctx.uid()}_"
    NG = T // GRP
    TPG = GRP // 128
    with ExitStack() as es:
        def sb(name, shape, dt):
            return es.enter_context(nc.sbuf_tensor(pfx + name, shape, dt))

        wiw = sb("wiw", [128, NDC, 4096], BF16)
        wow = sb("wow", [128, 32, 1024], BF16)
        gtw = sb("gtw", [128, NDC, 8], F32)
        gbrow = sb("gbrow", [1, 8], F32)
        atw = sb("atw", [128, NDC, 32], BF16)
        btw = sb("btw", [32, 4096], BF16)
        wib = sb("wib", [128, 32], F32)
        wob = sb("wob", [1, 1024], BF16)
        xs = sb("xs", [128, 2, D], F32)
        xln32 = sb("xln32", [128, 2, D], F32)
        xln16 = sb("xln16", [128, 2, D], BF16)
        xlnT = sb("xlnT", [128, NDC, GRP], BF16)
        xlnT32 = sb("xlnT32", [128, NDC, 128], F32)
        junk = sb("junk", [128, D], BF16)
        stats = alloc_stats(nc, es, pfx, 2)
        tkB = alloc_topk(nc, es, pfx)
        lg = sb("lg", [128, 8], F32)
        mask = sb("mask", [128, 2, 8], F32)
        u_m = sb("u_m", [128, 2, 32], BF16)
        utT = sb("utT", [32, GRP], BF16)
        h_sb = sb("h", [128, 32, GRP], BF16)
        xold = sb("xold", [128, 1, D], F32)
        onew = sb("onew", [128, 2, D], F32)
        ps_g = es.enter_context(nc.psum_tensor(pfx + "psg", [128, 8], F32))
        ps_u = es.enter_context(nc.psum_tensor(pfx + "psu", [128, 32], F32))
        ps_t = [es.enter_context(nc.psum_tensor(pfx + "pst0", [128, 128], F32))]
        ps_ut = es.enter_context(nc.psum_tensor(pfx + "psut", [32, 128], BF16))
        ps_h = [es.enter_context(nc.psum_tensor(f"{pfx}psh{i}", [128, GRP], F32))
                for i in range(2)]
        ps_o = es.enter_context(nc.psum_tensor(pfx + "pso", [128, 1024], F32))

        ones16 = ctx.consts["ones16"][:]
        ones32 = ctx.consts["ones32"][:]
        id32 = ctx.consts["ident32"][:]
        id16 = ctx.consts["ident16"][:]

        with nc.Block() as blk:
            @blk.gpsimd
            def _(gp):
                ctx.dinc(g.dma_start(wiw[:], W["l0f_wiT"].rearrange("(c p) f -> p c f", p=128)), "dg")
                ctx.dinc(g.dma_start(wow[:], W["l0f_woT"].rearrange("(c p) f -> p c f", p=128)), "dg")
                ctx.dinc(g.dma_start(gtw[:], W["l0f_gateT"].rearrange("(c p) e -> p c e", p=128)), "dg")
                ctx.dinc(g.dma_start(gbrow[:], W["l0f_gate_b"]), "dg")
                ctx.dinc(g.dma_start(atw[:], W["l0f_AallT"].rearrange("(c p) r -> p c r", p=128)), "dg")
                ctx.dinc(g.dma_start(btw[:], W["l0f_BallT"]), "dg")
                ctx.dinc(g.dma_start(wib[:], W["l0f_wib_cm"]), "dg")
                ctx.dinc(g.dma_start(wob[:], W["l0f_wo_b"]), "dg")
                w_ready = ctx.n["dg"]

                ev_norm = [0] * NT
                ev_tp = [0] * NT
                ev_t32 = [0] * NT
                ev_cast = [0, 0]
                ev_gmm = 0
                ev_lg_free = 0
                ev_wo = 0
                ev_gbase = 0
                ev_g32 = 0
                ev_hbank = [0, 0]
                ev_tbank = [0, 0]
                ev_stt_slot = [0, 0]
                ev_so_slot = [0, 0]
                ev_stt = 0
                ev_g_evac = 0
                ev_um = [0, 0]
                ev_ut = 0
                ev_mask = [0, 0]
                first = True

                for gi in range(NG):
                    for ii in range(TPG):
                        i = gi * TPG + ii
                        sl = i % 2
                        if i >= 2:
                            ctx.w(sy, "ve", ev_norm[i - 2])
                        ctx.dinc(sy.dma_start(xs[:, sl, :],
                                              xin[i * 128:(i + 1) * 128, :]), f"sx{sl}")
                        ctx.w(v, f"sx{sl}")
                        ctx.w(s, f"sx{sl}")
                        if i >= 2:
                            ctx.w(v, "se", ev_cast[sl])
                            ctx.w(v, "ve", ev_t32[i - 2])
                            ctx.w(v, f"tp{sl}", ev_tp[i - 2])
                        ev_norm[i] = emit_ln_tile(ctx, xs[:, sl, :], xln32[:, sl, :],
                                                  junk[:], stats[sl])
                        ctx.w(s, "ve", ev_norm[i])
                        ev_cast[sl] = ctx.inc(s.copy(xln16[:, sl, :], xln32[:, sl, :]), "se")
                        ctx.w(sy, "se", ev_cast[sl])
                        if gi >= 1 and ii == 0:
                            ctx.w(sy, "pe", ev_gbase)
                        ctx.dinc(sy.dma_start(xlnT[:, :, ii * 128:(ii + 1) * 128],
                                              xln16[:, sl, :], transpose=True), f"tp{sl}")
                        ev_tp[i] = ctx.n[f"tp{sl}"]
                        if first:
                            ctx.w(te, "dg", w_ready)
                            ctx.w(te, "po", ctx.n["po"])
                            first = False
                        ctx.w(te, "ve", ev_norm[i])
                        for dc in range(NDC):
                            bank = 0
                            if ev_tbank[bank]:
                                ctx.w(te, "ve", ev_tbank[bank])
                            i_t = ctx.inc(te.transpose(ps_t[bank][:],
                                                       xln32[:, sl, dc * 128:(dc + 1) * 128],
                                                       id32), "pe")
                            ctx.w(v, "pe", i_t)
                            if ev_gmm:
                                ctx.w(v, "pe", ev_gmm)
                            ctx.inc(v.tensor_copy(xlnT32[:, dc, :],
                                                  ps_t[bank][:]), "ve")
                            ev_tbank[bank] = ctx.n["ve"]
                        ev_t32[i] = ctx.n["ve"]
                        if dbg is not None and len(dbg) > 3:
                            ctx.w(sy, "ve", ev_t32[i])
                            ctx.dinc(sy.dma_start(dbg[3][i * 128:(i + 1) * 128, :],
                                                  xlnT32[:, 0, :]), "sdbg")
                            ctx.w(sy, "pe", ctx.n["pe"])

                        ctx.w(te, "ve", ev_t32[i])
                        if ev_g_evac:
                            ctx.w(te, "ve", ev_g_evac)
                        mm = te.matmul(ps_g[:], ones32, gbrow[:], start=True, stop=False)
                        for dc in range(NDC):
                            mm = te.matmul(ps_g[:], xlnT32[:, dc, :],
                                           gtw[:, dc, :], start=False, stop=(dc == NDC - 1))
                        i_g = ctx.inc(mm, "pe")
                        ev_gmm = i_g
                        ctx.w(v, "pe", i_g)
                        if ev_lg_free:
                            ctx.w(v, "se", ev_lg_free)
                        ctx.inc(v.tensor_copy(lg[:], ps_g[:]), "ve")
                        ev_g_evac = ctx.n["ve"]
                        ctx.w(v, "ve")
                        emit_probs(ctx, lg[:], tkB)
                        ev_lg_free = ctx.n["se"]
                        if dbg is not None:
                            ctx.w(sy, "ve", ctx.n["ve"])
                            ctx.dinc(sy.dma_start(dbg[0][i * 128:(i + 1) * 128, :], lg[:]), "sdbg")
                            ctx.dinc(sy.dma_start(dbg[1][i * 128:(i + 1) * 128, :], tkB["pr"]), "sdbg")
                        ev_mask[sl] = emit_argmax_mask(ctx, tkB["pr"], mask[:, sl, :], tkB)
                        if dbg is not None:
                            ctx.w(sy, "ve", ev_mask[sl])
                            ctx.dinc(sy.dma_start(dbg[2][i * 128:(i + 1) * 128, :], mask[:, sl, :]), "sdbg")

                        ctx.w(te, f"tp{sl}", ev_tp[i])
                        if max(ev_um):
                            ctx.w(te, "ve", max(ev_um))
                        for dc in range(NDC):
                            mm = te.matmul(ps_u[:], xlnT[:, dc, ii * 128:(ii + 1) * 128],
                                           atw[:, dc, :], start=(dc == 0), stop=(dc == NDC - 1))
                        i_u = ctx.inc(mm, "pe")
                        ctx.w(v, "pe", i_u)
                        ctx.w(v, "ve", ev_mask[sl])
                        mask_b4 = mask[:, sl, :].rearrange("p (e one) -> p e one", one=1)                                                .broadcast_to([128, 8, 4])
                        ctx.inc(v.tensor_tensor(u_m[:, sl, :].rearrange("p (e r) -> p e r", e=8),
                                                ps_u[:].rearrange("p (e r) -> p e r", e=8),
                                                mask_b4, ALU.mult), "ve")
                        ev_um[sl] = ctx.n["ve"]
                        ctx.w(te, "ve", ev_um[sl])
                        if ev_ut:
                            ctx.w(te, "ve", ev_ut)
                        i_ut = ctx.inc(te.transpose(ps_ut[:], u_m[:, sl, :], id16), "pe")
                        ctx.w(v, "pe", i_ut)
                        ctx.inc(v.tensor_copy(utT[:, ii * 128:(ii + 1) * 128], ps_ut[:]), "ve")
                        ev_ut = ctx.n["ve"]

                    ctx.w(te, f"tp0", ev_tp[gi * TPG + (TPG - 2)])
                    ctx.w(te, f"tp1", ev_tp[gi * TPG + (TPG - 1)])
                    ctx.w(te, "ve", ev_ut)
                    for fc in range(32):
                        bank = fc % 2
                        if ev_hbank[bank]:
                            ctx.w(te, "se", ev_hbank[bank])
                        if fc == 0 and ev_wo:
                            ctx.w(s, "pe", ev_wo)
                        for dc in range(NDC):
                            mm = te.matmul(ps_h[bank][:], wiw[:, dc, fc * 128:(fc + 1) * 128],
                                           xlnT[:, dc, :], start=(dc == 0), stop=False)
                        mm = te.matmul(ps_h[bank][:], btw[:, fc * 128:(fc + 1) * 128],
                                       utT[:], start=False, stop=True)
                        i_h = ctx.inc(mm, "pe")
                        ctx.w(s, "pe", i_h)
                        ctx.inc(s.activation(h_sb[:, fc, :], ps_h[bank][:], AF.Relu,
                                             bias=wib[:, fc:fc + 1]), "se")
                        ev_hbank[bank] = ctx.n["se"]
                    ev_gbase = ctx.n["pe"]
                    ev_h_all = ctx.n["se"]
                    ev_g32 = ctx.n["pe"]

                    for tt in range(TPG):
                        i = gi * TPG + tt
                        sl = i % 2
                        ctx.w(te, "se", ev_h_all)
                        if ev_stt:
                            ctx.w(te, "ve", ev_stt)
                        for nh in range(2):
                            mm = te.matmul(ps_o[:, nh * 512:(nh + 1) * 512], ones16,
                                           wob[0:1, nh * 512:(nh + 1) * 512],
                                           start=True, stop=False)
                            for fc in range(32):
                                mm = te.matmul(ps_o[:, nh * 512:(nh + 1) * 512],
                                               h_sb[:, fc, tt * 128:(tt + 1) * 128],
                                               wow[:, fc, nh * 512:(nh + 1) * 512],
                                               start=False, stop=(fc == 31))
                        i_o = ctx.inc(mm, "pe")
                        if ev_stt:
                            ctx.w(sy, "ve", ev_stt)
                        ctx.dinc(sy.dma_start(xold[:, 0, :],
                                              xin[i * 128:(i + 1) * 128, :]), "xo0")
                        ctx.w(v, "pe", i_o)
                        ctx.w(v, "xo0")
                        if ev_so_slot[sl]:
                            ctx.w(v, f"so{sl}", ev_so_slot[sl])
                        ins = v.scalar_tensor_tensor(onew[:, sl, :], ps_o[:], 1.0,
                                                     xold[:, 0, :], ALU.mult, ALU.add)
                        ctx.inc(ins, "ve")
                        ev_stt = ctx.n["ve"]
                        ev_stt_slot[sl] = ev_stt
                        ctx.w(sy, "ve", ev_stt)
                        ctx.dinc(sy.dma_start(xout[i * 128:(i + 1) * 128, :],
                                              onew[:, sl, :]), f"so{sl}")
                        ev_so_slot[sl] = ctx.n[f"so{sl}"]
                    ev_wo = ctx.n["pe"]
    ctx.barrier()





def emit_l0b(ctx, xin, xout, W):
    nc = ctx.nc
    te, v, s, g, sy = nc.tensor, nc.vector, nc.scalar, nc.gpsimd, nc.sync
    uid = ctx.uid()
    pfx = f"b{uid}_"
    qsp = nc.dram_tensor(f"qsp{uid}", [D, T], BF16, kind="Internal").ap()
    ksp = nc.dram_tensor(f"ksp{uid}", [D, T], BF16, kind="Internal").ap()
    vsp = nc.dram_tensor(f"vsp{uid}", [T, D], BF16, kind="Internal").ap()
    XX = (2.0 ** 8) ** (1.0 / 16)
    slopes = [1.0 / XX ** (i + 1) for i in range(16)]
    with ExitStack() as eso:
        def sbo(name, shape, dt):
            return es.enter_context(nc.sbuf_tensor(pfx + name, shape, dt))
        es = eso
        o_full = sbo("ofull", [128, NT, D], BF16)
        rel0 = sbo("rel0", [128, T], F32)
        wo_w = sbo("wo", [128, NDC, D], BF16)
        borow = sbo("bo", [1, D], BF16)
        ones16 = ctx.consts["ones16"][:]

        with nc.Block() as blk:
            @blk.gpsimd
            def _(gp):
                ctx.dinc(g.dma_start(wo_w[:], W["l0b_wo"].rearrange("(c p) f -> p c f", p=128)), "dg")
                ctx.dinc(g.dma_start(borow[:], W["l0b_bo"]), "dg")
                ctx.inc(g.iota(rel0[:], [[1, T]], base=0, channel_multiplier=-1,
                               allow_small_or_imprecise_dtypes=True), "po")

        with ExitStack() as esA:
            es = esA
            qkw = sbo("qkw", [128, NDC, 2048], BF16)
            wv = sbo("wv", [128, NDC, D], BF16)
            bqk = sbo("bqk", [128, 16], F32)
            bvrow = sbo("bv", [1, D], BF16)
            xs = sbo("xs", [128, 2, D], F32)
            xln = sbo("xln", [128, 2, D], BF16)
            xlnT = sbo("xlnT", [128, NDC, 512], BF16)
            junk = sbo("junk", [128, D], BF16)
            stats = alloc_stats(nc, es, pfx + "s", 2)
            stqk = sbo("stqk", [128, 16, 512], BF16)
            stv = sbo("stv", [128, 4, D], BF16)
            ps_p = [es.enter_context(nc.psum_tensor(f"{pfx}psp{i}", [128, 512], F32))
                    for i in range(2)]
            ps_v = [es.enter_context(nc.psum_tensor(f"{pfx}psv{i}", [128, 512], F32))
                    for i in range(2)]
            with nc.Block() as blk:
                @blk.gpsimd
                def _(gp):
                    ctx.dinc(g.dma_start(qkw[:], W["l0b_qkW"].rearrange("(c p) f -> p c f", p=128)), "dg")
                    ctx.dinc(g.dma_start(wv[:], W["l0b_wv"].rearrange("(c p) f -> p c f", p=128)), "dg")
                    ctx.dinc(g.dma_start(bqk[:], W["l0b_bqk_cm"]), "dg")
                    ctx.dinc(g.dma_start(bvrow[:], W["l0b_bv"]), "dg")
                    w_ready = ctx.n["dg"]
                    ctx.w(v, "dg", w_ready)
                    ctx.inc(v.tensor_scalar(bqk[:, 0:8], bqk[:, 0:8], 0.125, None,
                                            ALU.mult), "ve")
                    ctx.w(v, "ve")
                    bqk_ready = ctx.n["ve"]

                    ev_norm = [0] * NT
                    ev_tp = [0] * NT
                    ev_pbank = [0, 0]
                    ev_vbank = [0, 0]
                    ev_gdone = 0
                    ev_st = {}
                    first = True
                    for gi in range(NKT):
                        for ii in range(4):
                            i = gi * 4 + ii
                            sl = i % 2
                            if i >= 2:
                                ctx.w(sy, "ve", ev_norm[i - 2])
                            ctx.dinc(sy.dma_start(xs[:, sl, :],
                                                  xin[i * 128:(i + 1) * 128, :]), f"sx{sl}")
                            ctx.w(v, f"sx{sl}")
                            ctx.w(s, f"sx{sl}")
                            if i >= 2:
                                ctx.w(v, f"tp{sl}", ev_tp[i - 2])
                            ev_norm[i] = emit_ln_tile(ctx, xs[:, sl, :], xln[:, sl, :],
                                                      junk[:], stats[sl])
                            ctx.w(sy, "ve", ev_norm[i])
                            if gi >= 1 and ii == 0:
                                ctx.w(sy, "pe", ev_gdone)
                            ctx.dinc(sy.dma_start(xlnT[:, :, ii * 128:(ii + 1) * 128],
                                                  xln[:, sl, :], transpose=True), f"tp{sl}")
                            ev_tp[i] = ctx.n[f"tp{sl}"]
                        ctx.w(te, "tp0", ev_tp[gi * 4 + 2])
                        ctx.w(te, "tp1", ev_tp[gi * 4 + 3])
                        if first:
                            ctx.w(te, "dg", w_ready)
                            first = False
                        for fc in range(16):
                            bank = fc % 2
                            if ev_pbank[bank]:
                                ctx.w(te, "se", ev_pbank[bank])
                            for dc in range(NDC):
                                mm = te.matmul(ps_p[bank][:], qkw[:, dc, fc * 128:(fc + 1) * 128],
                                               xlnT[:, dc, :], start=(dc == 0),
                                               stop=(dc == NDC - 1))
                            i_p = ctx.inc(mm, "pe")
                            ctx.w(s, "pe", i_p)
                            if fc == 0:
                                ctx.w(s, "ve", bqk_ready)
                                if gi >= 1:
                                    ctx.w(s, f"sq", ev_st["q"])
                            ctx.inc(s.activation(stqk[:, fc, :], ps_p[bank][:], AF.Identity,
                                                 scale=(0.125 if fc < 8 else 1.0),
                                                 bias=bqk[:, fc:fc + 1]), "se")
                            ev_pbank[bank] = ctx.n["se"]
                        ev_qk_st = ctx.n["se"]
                        for ii in range(4):
                            for nh in range(2):
                                bank = (ii * 2 + nh) % 2
                                if ev_vbank[bank]:
                                    ctx.w(te, "ve", ev_vbank[bank])
                                if ii == 0 and nh == 0 and gi >= 1:
                                    ctx.w(te, f"sv", ev_st["v"])
                                mm = te.matmul(ps_v[bank][:], ones16,
                                               bvrow[0:1, nh * 512:(nh + 1) * 512],
                                               start=True, stop=False)
                                for dc in range(NDC):
                                    mm = te.matmul(ps_v[bank][:],
                                                   xlnT[:, dc, ii * 128:(ii + 1) * 128],
                                                   wv[:, dc, nh * 512:(nh + 1) * 512],
                                                   start=False, stop=(dc == NDC - 1))
                                i_v = ctx.inc(mm, "pe")
                                ctx.w(v, "pe", i_v)
                                ctx.inc(v.tensor_copy(stv[:, ii, nh * 512:(nh + 1) * 512],
                                                      ps_v[bank][:]), "ve")
                                ev_vbank[bank] = ctx.n["ve"]
                        ev_v_st = ctx.n["ve"]
                        ev_gdone = ctx.n["pe"]
                        ctx.w(sy, "se", ev_qk_st)
                        ctx.dinc(sy.dma_start(
                            qsp[:, gi * 512:(gi + 1) * 512].rearrange("(c p) t -> p c t", p=128),
                            stqk[:, 0:8, :]), "sq")
                        ctx.dinc(sy.dma_start(
                            ksp[:, gi * 512:(gi + 1) * 512].rearrange("(c p) t -> p c t", p=128),
                            stqk[:, 8:16, :]), "sq")
                        ev_st["q"] = ctx.n["sq"]
                        ctx.w(sy, "ve", ev_v_st)
                        ctx.dinc(sy.dma_start(
                            vsp[gi * 512:(gi + 1) * 512, :].rearrange("(i p) d -> p i d", p=128),
                            stv[:]), "sv")
                        ev_st["v"] = ctx.n["sv"]

        with ExitStack() as esB:
            es = esB
            s_sb = sbo("s_sb", [128, 2, T], F32)
            p_sb = sbo("p_sb", [128, 2, T], BF16)
            pt_sb = sbo("pt", [128, 2, NT, 128], BF16)
            k_h = sbo("k_h", [64, 2, T], BF16)
            v_h = sbo("v_h", [128, 2, NT, 64], BF16)
            q_st = sbo("q_st", [64, 2, 128], BF16)
            mcol = sbo("mcol", [128, 2], F32)
            negm = sbo("negm", [128, 2], F32)
            lcol = sbo("lcol", [128, 2], F32)
            rlcol = sbo("rlcol", [128, 2], F32)
            ps_s = [es.enter_context(nc.psum_tensor(f"{pfx}pss{i}", [128, 512], F32))
                    for i in range(2)]
            ps_pv = [es.enter_context(nc.psum_tensor(f"{pfx}pspv{i}", [128, 64], F32))
                     for i in range(2)]
            with nc.Block() as blk:
                @blk.gpsimd
                def _(gp):
                    ctx.w(te, "se")
                    ctx.w(te, "ve")
                    ctx.w(sy, "sq")
                    ctx.w(sy, "sv")
                    ev_ssbank = [0, 0]
                    ev_exp_slot = [0, 0]
                    ev_recip_slot = [0, 0]
                    ev_pt_slot = [0, 0]
                    ev_pv_slot = [0, 0]
                    ev_pvbank = [0, 0]
                    ev_kv = [0, 0]
                    ev_scores_q = [0, 0]
                    ev_exp_last = 0

                    def emit_front(h, qt, hsl):
                        qsl = qt % 2
                        if ev_scores_q[qsl]:
                            ctx.w(sy, "pe", ev_scores_q[qsl])
                        ctx.dinc(sy.dma_start(q_st[:, qsl, :],
                                              qsp[h * 64:(h + 1) * 64,
                                                  qt * 128:(qt + 1) * 128]), f"qh{qsl}")
                        ctx.w(te, f"qh{qsl}")
                        for kt in range(NKT):
                            bank = kt % 2
                            if ev_ssbank[bank]:
                                ctx.w(te, "ve", ev_ssbank[bank])
                            mm = te.matmul(ps_s[bank][:], q_st[:, qsl, :],
                                           k_h[:, hsl, kt * 512:(kt + 1) * 512],
                                           start=True, stop=True)
                            i_s = ctx.inc(mm, "pe")
                            ctx.w(v, "pe", i_s)
                            if kt == 0 and ev_exp_slot[qsl]:
                                ctx.w(v, "se", ev_exp_slot[qsl])
                            ctx.inc(v.scalar_tensor_tensor(
                                s_sb[:, qsl, kt * 512:(kt + 1) * 512],
                                rel0[:, kt * 512:(kt + 1) * 512], slopes[h],
                                ps_s[bank][:], ALU.mult, ALU.add), "ve")
                            ev_ssbank[bank] = ctx.n["ve"]
                        ev_scores_q[qsl] = ctx.n["pe"]
                        ctx.w(v, "ve")
                        ctx.inc(v.reduce_max(mcol[:, qsl:qsl + 1], s_sb[:, qsl, :],
                                             axis=AX.X), "ve")
                        ctx.w(v, "ve")
                        if ev_exp_slot[qsl]:
                            ctx.w(v, "se", ev_exp_slot[qsl])
                        ctx.inc(v.tensor_scalar(negm[:, qsl:qsl + 1], mcol[:, qsl:qsl + 1],
                                                -1.0, None, ALU.mult), "ve")
                        ev_rmax = ctx.n["ve"]
                        ctx.w(s, "ve", ev_rmax)
                        if ev_pt_slot[qsl]:
                            ctx.w(s, f"pt{qsl}", ev_pt_slot[qsl])
                        if ev_recip_slot[qsl]:
                            ctx.w(s, "ve", ev_recip_slot[qsl])
                        ev_exp_slot[qsl] = ctx.inc(
                            s.activation(p_sb[:, qsl, :], s_sb[:, qsl, :], AF.Exp,
                                         bias=negm[:, qsl:qsl + 1],
                                         accum_out=lcol[:, qsl:qsl + 1]), "se")
                        ctx.w(v, "se", ev_exp_slot[qsl])
                        ctx.inc(v.reciprocal(rlcol[:, qsl:qsl + 1], lcol[:, qsl:qsl + 1]),
                                "ve")
                        ev_recip_slot[qsl] = ctx.n["ve"]
                        ctx.w(sy, "se", ev_exp_slot[qsl])
                        if ev_pv_slot[qsl]:
                            ctx.w(sy, "pe", ev_pv_slot[qsl])
                        ctx.dinc(sy.dma_start(pt_sb[:, qsl, :, :], p_sb[:, qsl, :],
                                              transpose=True), f"pt{qsl}")
                        ev_pt_slot[qsl] = ctx.n[f"pt{qsl}"]

                    def emit_back(h, qt, hsl):
                        qsl = qt % 2
                        ctx.w(te, f"pt{qsl}", ev_pt_slot[qsl])
                        bank = qsl
                        if ev_pvbank[bank]:
                            ctx.w(te, "ve", ev_pvbank[bank])
                        for kc in range(NT):
                            mm = te.matmul(ps_pv[bank][:], pt_sb[:, qsl, kc, :],
                                           v_h[:, hsl, kc, :],
                                           start=(kc == 0), stop=(kc == NT - 1))
                        i_pv = ctx.inc(mm, "pe")
                        ev_pv_slot[qsl] = ctx.n["pe"]
                        ctx.w(v, "pe", i_pv)
                        ctx.w(v, "ve", ev_recip_slot[qsl])
                        ctx.inc(v.tensor_scalar(o_full[:, qt, h * 64:(h + 1) * 64],
                                                ps_pv[bank][:], rlcol[:, qsl:qsl + 1],
                                                None, ALU.mult), "ve")
                        ev_pvbank[bank] = ctx.n["ve"]

                    for h in range(16):
                        hsl = h % 2
                        if h >= 2:
                            ctx.w(sy, "pe", ev_kv[hsl])
                        ctx.dinc(sy.dma_start(k_h[:, hsl, :],
                                              ksp[h * 64:(h + 1) * 64, :]), f"kh{hsl}")
                        ctx.dinc(sy.dma_start(
                            v_h[:, hsl, :, :],
                            vsp[:, h * 64:(h + 1) * 64].rearrange("(i p) d -> p i d", p=128)),
                            f"kh{hsl}")
                        if h == 0:
                            ctx.w(v, "po", ctx.n["po"])
                        ctx.w(te, f"kh{hsl}")
                        for qt in range(NT):
                            emit_front(h, qt, hsl)
                            if qt >= 1:
                                emit_back(h, qt - 1, hsl)
                        emit_back(h, NT - 1, hsl)
                        ev_kv[hsl] = ctx.n["pe"]

        with ExitStack() as esC:
            es = esC
            oT = sbo("oT", [128, NDC, 128], BF16)
            xold = sbo("xold", [128, 1, D], F32)
            onew = sbo("onew", [128, 2, D], F32)
            ps_c = es.enter_context(nc.psum_tensor(pfx + "psc", [128, 1024], F32))
            with nc.Block() as blk:
                @blk.gpsimd
                def _(gp):
                    ctx.w(te, "ve")
                    ctx.w(sy, "ve")
                    ev_stt = 0
                    ev_so_slot = [0, 0]
                    ev_cmm = 0
                    for qt in range(NT):
                        sl = qt % 2
                        if ev_cmm:
                            ctx.w(sy, "pe", ev_cmm)
                        ctx.dinc(sy.dma_start(oT[:], o_full[:, qt, :], transpose=True),
                                 "pt0")
                        ctx.w(te, "pt0")
                        if ev_stt:
                            ctx.w(te, "ve", ev_stt)
                        for nh in range(2):
                            mm = te.matmul(ps_c[:, nh * 512:(nh + 1) * 512], ones16,
                                           borow[0:1, nh * 512:(nh + 1) * 512],
                                           start=True, stop=False)
                            for dc in range(NDC):
                                mm = te.matmul(ps_c[:, nh * 512:(nh + 1) * 512],
                                               oT[:, dc, :],
                                               wo_w[:, dc, nh * 512:(nh + 1) * 512],
                                               start=False, stop=(dc == NDC - 1))
                        i_c = ctx.inc(mm, "pe")
                        ev_cmm = ctx.n["pe"]
                        if ev_stt:
                            ctx.w(sy, "ve", ev_stt)
                        ctx.dinc(sy.dma_start(xold[:, 0, :],
                                              xin[qt * 128:(qt + 1) * 128, :]), "xo0")
                        ctx.w(v, "pe", i_c)
                        ctx.w(v, "xo0")
                        if ev_so_slot[sl]:
                            ctx.w(v, f"so{sl}", ev_so_slot[sl])
                        ins = v.scalar_tensor_tensor(onew[:, sl, :], ps_c[:], 1.0,
                                                     xold[:, 0, :], ALU.mult, ALU.add)
                        ctx.inc(ins, "ve")
                        ev_stt = ctx.n["ve"]
                        ctx.w(sy, "ve", ev_stt)
                        ctx.dinc(sy.dma_start(xout[qt * 128:(qt + 1) * 128, :],
                                              onew[:, sl, :]), f"so{sl}")
                        ev_so_slot[sl] = ctx.n[f"so{sl}"]
    ctx.barrier()





def emit_l1a(ctx, xin, xout, W):
    nc = ctx.nc
    te, v, s, g, sy = nc.tensor, nc.vector, nc.scalar, nc.gpsimd, nc.sync
    uid = ctx.uid()
    pfx = f"ga{uid}_"
    qsp = nc.dram_tensor(f"gqsp{uid}", [D, T], BF16, kind="Internal").ap()
    with ExitStack() as eso:
        def sbo(name, shape, dt):
            return es.enter_context(nc.sbuf_tensor(pfx + name, shape, dt))
        es = eso
        kT = sbo("kT", [128, NDC, T], BF16)
        v_tm = sbo("v", [128, NT, D], BF16)
        c1col = sbo("c1c", [128, 1], F32)
        c2col = sbo("c2c", [128, 1], F32)
        ones32 = ctx.consts["ones32"][:]
        ones16 = ctx.consts["ones16"][:]

        with ExitStack() as esS:
            es = esS
            rotw = sbo("rotw", [1, 3, D], F32)
            scb = sbo("scb", [1, 2], F32)
            tA = sbo("tA", [1, 3, 512], F32)
            tB = sbo("tB", [1, 3, 512], F32)
            crow = sbo("crow", [1, 3, 512], F32)
            srow = sbo("srow", [1, 3, 512], F32)
            c12row = sbo("c12", [1, 2], F32)
            cb = sbo("cb", [128, 3, 512], F32)
            sb_ = sbo("sb", [128, 3, 512], F32)
            ps_b = es.enter_context(nc.psum_tensor(pfx + "psb", [128, 512], F32))
            with nc.Block() as blk:
                @blk.gpsimd
                def _(gp):
                    ctx.dinc(g.dma_start(rotw[:], W["l1a_rot"].rearrange("(one r) d -> one r d", one=1)), "dg")
                    ctx.dinc(g.dma_start(scb[:], W["l1a_scb"]), "dg")
                    ctx.w(v, "dg")
                    re = rotw[:, :, :].rearrange("o r (i two) -> o r i two", two=2)
                    ctx.inc(v.tensor_tensor(tA[:], re[:, :, :, 0], re[:, :, :, 0], ALU.mult), "ve")
                    ctx.inc(v.tensor_tensor(tB[:], re[:, :, :, 1], re[:, :, :, 1], ALU.mult), "ve")
                    ctx.w(v, "ve")
                    ctx.inc(v.tensor_tensor(tA[:], tA[:], tB[:], ALU.add), "ve")
                    ctx.w(s, "ve")
                    ctx.inc(s.activation(tA[:], tA[:], AF.Sqrt), "se")
                    ctx.w(v, "se")
                    ctx.inc(v.reciprocal(tA[:], tA[:]), "ve")
                    ctx.w(v, "ve")
                    ctx.inc(v.tensor_tensor(crow[:], re[:, :, :, 0], tA[:], ALU.mult), "ve")
                    ctx.inc(v.tensor_tensor(srow[:], re[:, :, :, 1], tA[:], ALU.mult), "ve")
                    ctx.w(v, "ve")
                    ctx.inc(v.reciprocal(c12row[:, 0:1], scb[:, 0:1]), "ve")
                    ctx.w(v, "ve")
                    ctx.inc(v.tensor_scalar(c12row[:, 0:1], c12row[:, 0:1], 2.0, None, ALU.mult), "ve")
                    ctx.w(v, "ve")
                    ctx.inc(v.tensor_tensor(c12row[:, 1:2], c12row[:, 0:1], scb[:, 1:2], ALU.add), "ve")
                    ctx.w(v, "ve")
                    st_rows = ctx.n["ve"]
                    ctx.w(te, "ve", st_rows)
                    for r in range(3):
                        i_m = ctx.inc(te.matmul(ps_b[:], ones32, crow[:, r, :],
                                                start=True, stop=True), "pe")
                        ctx.w(v, "pe", i_m)
                        ctx.inc(v.tensor_copy(cb[:, r, :], ps_b[:]), "ve")
                        ctx.w(te, "ve")
                        i_m = ctx.inc(te.matmul(ps_b[:], ones32, srow[:, r, :],
                                                start=True, stop=True), "pe")
                        ctx.w(v, "pe", i_m)
                        ctx.inc(v.tensor_copy(sb_[:, r, :], ps_b[:]), "ve")
                        ctx.w(te, "ve")
                    i_m = ctx.inc(te.matmul(ps_b[:, 0:2], ones32, c12row[:],
                                            start=True, stop=True), "pe")
                    ctx.w(v, "pe", i_m)
                    ctx.inc(v.tensor_copy(c1col[:], ps_b[:, 0:1]), "ve")
                    ctx.inc(v.tensor_copy(c2col[:], ps_b[:, 1:2]), "ve")
                    ctx.w(v, "ve")

            with ExitStack() as esA:
                es = esA
                xs = sbo("xs", [128, 2, D], F32)
                xln = sbo("xln", [128, 2, D], BF16)
                junk = sbo("junk", [128, D], BF16)
                stats = alloc_stats(nc, es, pfx, 2)
                t1 = sbo("t1", [128, 512], F32)
                t2 = sbo("t2", [128, 512], F32)
                rq = sbo("rq", [128, 2, D], BF16)
                rk = sbo("rk", [128, 2, D], BF16)
                stq = sbo("stq", [128, NDC, 128], BF16)
                with nc.Block() as blk:
                    @blk.gpsimd
                    def _(gp):
                        ev_norm = [0] * NT
                        ev_rot = [0, 0]
                        ev_tpq = [0, 0]
                        ev_tpk = [0, 0]
                        ev_stq = 0
                        for i in range(NT):
                            sl = i % 2
                            if i >= 2:
                                ctx.w(sy, "ve", ev_norm[i - 2])
                            ctx.dinc(sy.dma_start(xs[:, sl, :],
                                                  xin[i * 128:(i + 1) * 128, :]), f"sx{sl}")
                            ctx.w(v, f"sx{sl}")
                            ctx.w(s, f"sx{sl}")
                            if i >= 2:
                                ctx.w(v, f"tpq{sl}", ev_tpq[sl])
                                ctx.w(v, f"tpk{sl}", ev_tpk[sl])
                            ev_norm[i] = emit_ln_tile(ctx, xs[:, sl, :], xln[:, sl, :],
                                                      junk[:], stats[sl])
                            ctx.w(v, "ve", ev_norm[i])
                            xe = xln[:, sl, :].rearrange("p (i two) -> p i two", two=2)
                            for r, dst in ((0, rq[:, sl, :]), (1, rk[:, sl, :]),
                                           (2, v_tm[:, i, :])):
                                de = dst.rearrange("p (i two) -> p i two", two=2)
                                ctx.inc(v.tensor_tensor(t1[:], xe[:, :, 0], cb[:, r, :], ALU.mult), "ve")
                                ctx.inc(v.tensor_tensor(t2[:], xe[:, :, 1], sb_[:, r, :], ALU.mult), "ve")
                                ctx.w(v, "ve")
                                ctx.inc(v.tensor_tensor(de[:, :, 0], t1[:], t2[:], ALU.subtract), "ve")
                                ctx.w(v, "ve")
                                ctx.inc(v.tensor_tensor(t1[:], xe[:, :, 0], sb_[:, r, :], ALU.mult), "ve")
                                ctx.inc(v.tensor_tensor(t2[:], xe[:, :, 1], cb[:, r, :], ALU.mult), "ve")
                                ctx.w(v, "ve")
                                ctx.inc(v.tensor_tensor(de[:, :, 1], t1[:], t2[:], ALU.add), "ve")
                                ctx.w(v, "ve")
                            ev_rot[sl] = ctx.n["ve"]
                            ctx.w(sy, "ve", ev_rot[sl])
                            if ev_stq:
                                ctx.w(sy, f"sq", ev_stq)
                            ctx.dinc(sy.dma_start(stq[:], rq[:, sl, :], transpose=True), f"tpq{sl}")
                            ev_tpq[sl] = ctx.n[f"tpq{sl}"]
                            ctx.w(sy, f"tpq{sl}", ev_tpq[sl])
                            ctx.dinc(sy.dma_start(
                                qsp[:, i * 128:(i + 1) * 128].rearrange("(c p) t -> p c t", p=128),
                                stq[:]), "sq")
                            ev_stq = ctx.n["sq"]
                            ctx.dinc(sy.dma_start(kT[:, :, i * 128:(i + 1) * 128],
                                                  rk[:, sl, :], transpose=True), f"tpk{sl}")
                            ev_tpk[sl] = ctx.n[f"tpk{sl}"]


        with ExitStack() as esB:
            es = esB
            s_sb = sbo("s_sb", [128, T], F32)
            p_sb = sbo("p_sb", [128, T], BF16)
            pt_sb = sbo("pt", [128, NT, 128], BF16)
            q_st = sbo("q_st", [128, 2, NDC, 128], BF16)
            junkB = sbo("junkB", [128, 512], BF16)
            mcol = sbo("mcol", [128, 1], F32)
            negm = sbo("negm", [128, 1], F32)
            lcol = sbo("lcol", [128, 1], F32)
            rlcol = sbo("rlcol", [128, 1], F32)
            i1 = sbo("i1", [128, 1], F32)
            i2 = sbo("i2", [128, 1], F32)
            rdcol = sbo("rdcol", [128, 1], F32)
            xold = sbo("xold", [128, 1, D], F32)
            onew = sbo("onew", [128, 2, D], F32)
            ps_s = [es.enter_context(nc.psum_tensor(f"{pfx}pss{i}", [128, 512], F32))
                    for i in range(2)]
            ps_o = es.enter_context(nc.psum_tensor(pfx + "pso", [128, 1024], F32))
            with nc.Block() as blk:
                @blk.gpsimd
                def _(gp):
                    ctx.w(te, "ve")
                    ctx.w(te, "se")
                    ctx.w(te, "tpk0")
                    ctx.w(te, "tpk1")
                    ctx.w(sy, "sq")
                    ev_ssbank = [0, 0]
                    ev_rmax = 0
                    ev_exp = 0
                    ev_recip = 0
                    ev_pt = 0
                    ev_pv = 0
                    ev_stt = 0
                    ev_sq = [0, 0]
                    ev_so_slot = [0, 0]
                    ev_den = 0
                    for qt in range(NT):
                        sl = qt % 2
                        if ev_sq[sl]:
                            ctx.w(sy, "pe", ev_sq[sl])
                        ctx.dinc(sy.dma_start(
                            q_st[:, sl, :, :],
                            qsp[:, qt * 128:(qt + 1) * 128].rearrange("(c p) t -> p c t", p=128)),
                            f"qh{sl}")
                        ctx.w(te, f"qh{sl}")
                        for kt in range(NKT):
                            bank = kt % 2
                            if ev_ssbank[bank]:
                                ctx.w(te, "ve", ev_ssbank[bank])
                            for dc in range(NDC):
                                mm = te.matmul(ps_s[bank][:], q_st[:, sl, dc, :],
                                               kT[:, dc, kt * 512:(kt + 1) * 512],
                                               start=(dc == 0), stop=(dc == NDC - 1))
                            i_s = ctx.inc(mm, "pe")
                            ctx.w(v, "pe", i_s)
                            if qt >= 1 and kt == 0:
                                ctx.w(v, "se", ev_exp)
                            ctx.inc(v.tensor_scalar(s_sb[:, kt * 512:(kt + 1) * 512],
                                                    ps_s[bank][:], c1col[:], c2col[:],
                                                    ALU.mult, ALU.add), "ve")
                            ev_ssbank[bank] = ctx.n["ve"]
                        ev_sq[sl] = ctx.n["pe"]
                        ctx.w(v, "ve")
                        ctx.inc(v.reduce_max(mcol[:], s_sb[:], axis=AX.X), "ve")
                        ctx.w(v, "ve")
                        if ev_exp:
                            ctx.w(v, "se", ev_exp)
                        ctx.inc(v.tensor_scalar(negm[:], mcol[:], -1.0, None, ALU.mult), "ve")
                        ev_rmax = ctx.n["ve"]
                        ctx.w(s, "ve", ev_rmax)
                        if ev_pt:
                            ctx.w(s, "pt0", ev_pt)
                        if ev_recip:
                            ctx.w(s, "ve", ev_recip)
                        ev_exp = ctx.inc(s.activation(p_sb[:], s_sb[:], AF.Exp,
                                                      bias=negm[:], accum_out=lcol[:]), "se")
                        ctx.w(v, "se", ev_exp)
                        ctx.inc(v.reciprocal(rlcol[:], lcol[:]), "ve")
                        ev_recip = ctx.n["ve"]
                        ctx.w(sy, "se", ev_exp)
                        if ev_pv:
                            ctx.w(sy, "pe", ev_pv)
                        ctx.dinc(sy.dma_start(pt_sb[:], p_sb[:], transpose=True), "pt0")
                        ev_pt = ctx.n["pt0"]
                        ctx.w(te, "pt0", ev_pt)
                        if ev_stt:
                            ctx.w(te, "ve", ev_stt)
                            ctx.w(te, "se", ev_den)
                        for nh in range(2):
                            for kc in range(NT):
                                mm = te.matmul(ps_o[:, nh * 512:(nh + 1) * 512],
                                               pt_sb[:, kc, :],
                                               v_tm[:, kc, nh * 512:(nh + 1) * 512],
                                               start=(kc == 0), stop=(kc == NT - 1))
                            ctx.inc(mm, "pe")
                        ev_pv = ctx.n["pe"]
                        ctx.w(s, "pe", ev_pv)
                        ctx.inc(s.activation(junkB[:], ps_o[:, 0:512], AF.Square,
                                             accum_out=i1[:]), "se")
                        ctx.w(s, "se")
                        ctx.inc(s.activation(junkB[:], ps_o[:, 512:1024], AF.Square,
                                             accum_out=i2[:]), "se")
                        ev_den = ctx.n["se"]
                        ctx.w(v, "se", ev_den)
                        ctx.w(v, "ve", ev_recip)
                        ctx.inc(v.tensor_tensor(i1[:], i1[:], i2[:], ALU.add), "ve")
                        ctx.w(v, "ve")
                        ctx.inc(v.tensor_tensor(i1[:], i1[:], rlcol[:], ALU.mult), "ve")
                        ctx.w(v, "ve")
                        ctx.inc(v.tensor_tensor(i1[:], i1[:], rlcol[:], ALU.mult), "ve")
                        ctx.w(v, "ve")
                        i_rl2 = ctx.n["ve"]
                        ctx.w(s, "ve", i_rl2)
                        i_abs = ctx.inc(s.activation(i1[:], i1[:], AF.Abs), "se")
                        ctx.w(v, "se", i_abs)
                        ctx.inc(v.tensor_scalar(i1[:], i1[:], 1e-8, None, ALU.max), "ve")
                        i_cl = ctx.n["ve"]
                        ctx.w(s, "ve", i_cl)
                        i_sq2 = ctx.inc(s.activation(rdcol[:], i1[:], AF.Sqrt), "se")
                        ctx.w(v, "se", i_sq2)
                        ctx.inc(v.reciprocal(rdcol[:], rdcol[:]), "ve")
                        ctx.w(v, "ve")
                        ctx.inc(v.tensor_tensor(rdcol[:], rdcol[:], rlcol[:], ALU.mult), "ve")
                        ctx.w(v, "ve")
                        if ev_stt:
                            ctx.w(sy, "ve", ev_stt)
                        ctx.dinc(sy.dma_start(xold[:, 0, :],
                                              xin[qt * 128:(qt + 1) * 128, :]), "xo0")
                        ctx.w(v, "xo0")
                        if ev_so_slot[sl]:
                            ctx.w(v, f"so{sl}", ev_so_slot[sl])
                        for nh in range(2):
                            ins = v.scalar_tensor_tensor(
                                onew[:, sl, nh * 512:(nh + 1) * 512],
                                ps_o[:, nh * 512:(nh + 1) * 512], rdcol[:],
                                xold[:, 0, nh * 512:(nh + 1) * 512], ALU.mult, ALU.add)
                        ctx.inc(ins, "ve")
                        ev_stt = ctx.n["ve"]
                        ctx.w(sy, "ve", ev_stt)
                        ctx.dinc(sy.dma_start(xout[qt * 128:(qt + 1) * 128, :],
                                              onew[:, sl, :]), f"so{sl}")
                        ev_so_slot[sl] = ctx.n[f"so{sl}"]
    ctx.barrier()





def emit_precast_l1w(ctx, W):
    if getattr(ctx, "l1w", None) is not None:
        return ctx.l1w
    nc = ctx.nc
    g, sy = nc.gpsimd, nc.sync
    w1b = nc.dram_tensor("w1bsp", [8, 128, NDC, 4096], BF16, kind="Internal").ap()
    w2b = nc.dram_tensor("w2bsp", [8, 128, 32, 1024], BF16, kind="Internal").ap()
    with ExitStack() as es:
        st1 = [es.enter_context(nc.sbuf_tensor(f"pc_st1_{i}", [128, NDC, 2048], BF16))
               for i in range(2)]
        with nc.Block() as blk:
            @blk.gpsimd
            def _(gp):
                ev_store = [0, 0]
                for e in range(8):
                    for hh in range(2):
                        sl = (e * 2 + hh) % 2
                        if ev_store[sl]:
                            ctx.w(g, f"pcs{sl}", ev_store[sl])
                        ctx.dinc(g.dma_start(
                            st1[sl][:],
                            W["l1f_w1"][e, :, hh * 2048:(hh + 1) * 2048]
                            .rearrange("(c p) f -> p c f", p=128)), f"pcl{sl}")
                        ctx.w(sy, f"pcl{sl}")
                        ctx.dinc(sy.dma_start(w1b[e, :, :, hh * 2048:(hh + 1) * 2048],
                                              st1[sl][:]), f"pcs{sl}")
                        ev_store[sl] = ctx.n[f"pcs{sl}"]
                for e in range(8):
                    for hh in range(2):
                        sl = (e * 2 + hh) % 2
                        ctx.w(g, f"pcs{sl}", ev_store[sl])
                        ctx.dinc(g.dma_start(
                            st1[sl][:].rearrange("p c f -> p (c f)")[:, 0:16384]
                            .rearrange("p (c d) -> p c d", c=16),
                            W["l1f_w2"][e, hh * 2048:(hh + 1) * 2048, :]
                            .rearrange("(c p) d -> p c d", p=128)), f"pcl{sl}")
                        ctx.w(sy, f"pcl{sl}")
                        ctx.dinc(sy.dma_start(w2b[e, :, hh * 16:(hh + 1) * 16, :],
                                              st1[sl][:].rearrange("p c f -> p (c f)")[:, 0:16384]
                                              .rearrange("p (c d) -> p c d", c=16)), f"pcs{sl}")
                        ev_store[sl] = ctx.n[f"pcs{sl}"]
    ctx.barrier()
    ctx.l1w = (w1b, w2b)
    return ctx.l1w


def emit_l1ffn(ctx, xin, xout, W):
    nc = ctx.nc
    te, v, s, g, sy = nc.tensor, nc.vector, nc.scalar, nc.gpsimd, nc.sync
    w1b, w2b = emit_precast_l1w(ctx, W)
    uid = ctx.uid()
    pfx = f"m{uid}_"
    xm = nc.dram_tensor(f"xmid{uid}", [T, D], F32, kind="Internal").ap()
    TS = 512
    NSL = T // TS
    TPS = TS // 128
    with ExitStack() as eso:
        def sbo(name, shape, dt):
            return es.enter_context(nc.sbuf_tensor(pfx + name, shape, dt))
        es = eso
        pool = es.enter_context(nc.psum_tensor(pfx + "pool", [128, 8, 512], F32))
        ps_t = pool[:, 5, 0:128]
        ones16 = ctx.consts["ones16"][:]
        ones32 = ctx.consts["ones32"][:]
        id32 = ctx.consts["ident32"][:]

        def bcast_row(row_ap, dst, n_done):
            for nh in range(2):
                ctx.w(te, "ve", n_done[0])
                i_m = ctx.inc(te.matmul(pool[:, 7, :],
                                        ones32, row_ap[0:1, nh * 512:(nh + 1) * 512],
                                        start=True, stop=True), "pe")
                ctx.w(v, "pe", i_m)
                ctx.inc(v.tensor_copy(dst[:, nh * 512:(nh + 1) * 512], pool[:, 7, :]), "ve")
                n_done[0] = ctx.n["ve"]

        with ExitStack() as esM:
            es = esM
            g1b = sbo("g1b", [128, D], F32)
            be1b = sbo("be1b", [128, D], F32)
            rows = sbo("rows", [1, 2, D], F32)
            mixw = sbo("mixw", [128, NDC, D], BF16)
            mixb = sbo("mixb", [1, D], BF16)
            xs = sbo("xs", [128, 2, D], F32)
            z32 = sbo("z32", [128, 2, D], F32)
            t132 = sbo("t132", [128, 2, D], F32)
            nbuf = sbo("nbuf", [128, 2, D], F32)
            stile = sbo("stile", [128, 2, D], F32)
            n16 = sbo("n16", [128, 2, D], BF16)
            nT = sbo("nT", [128, NDC, 128], BF16)
            junk = sbo("junk", [128, D], BF16)
            stats = alloc_stats(nc, es, pfx, 2)
            onew = sbo("onew", [128, 2, D], F32)
            with nc.Block() as blk:
                @blk.gpsimd
                def _(gp):
                    ctx.dinc(g.dma_start(rows[:, 0, :], W["l1f_g1r"]), "dg")
                    ctx.dinc(g.dma_start(rows[:, 1, :], W["l1f_be1r"]), "dg")
                    ctx.dinc(g.dma_start(mixw[:], W["l1f_mixw"].rearrange("(c p) f -> p c f", p=128)), "dg")
                    ctx.dinc(g.dma_start(mixb[:], W["l1f_mixb"]), "dg")
                    w_ready = ctx.n["dg"]
                    ctx.w(te, "dg", w_ready)
                    ctx.w(te, "po", ctx.n["po"])
                    nd = [0]
                    bcast_row(rows[:, 0, :], g1b[:], nd)
                    bcast_row(rows[:, 1, :], be1b[:], nd)
                    ev_bc = ctx.n["ve"]
                    ev_norm = [0] * NT
                    ev_tp = 0
                    ev_mm = 0
                    ev_stt = 0
                    ev_stile = [0, 0]
                    ev_tp_slot = [0, 0]
                    ev_stt_slot = [0, 0]
                    ev_so_slot = [0, 0]
                    for i in range(NT):
                        sl = i % 2
                        if ev_stile[sl]:
                            ctx.w(sy, "ve", ev_stile[sl])
                        ctx.dinc(sy.dma_start(xs[:, sl, :],
                                              xin[i * 128:(i + 1) * 128, :]), f"sx{sl}")
                        ctx.w(v, f"sx{sl}")
                        ctx.w(s, f"sx{sl}")
                        if i == 0:
                            ctx.w(v, "ve", ev_bc)
                        ev_z = emit_ln_tile(ctx, xs[:, sl, :], z32[:, sl, :],
                                            junk[:], stats[sl])
                        ctx.w(v, "ve", ev_z)
                        ctx.w(s, "ve", ev_z)
                        ev_norm[i] = emit_ln_tile(ctx, z32[:, sl, :], t132[:, sl, :],
                                                  junk[:], stats[sl])
                        ctx.w(v, "ve", ev_norm[i])
                        ctx.inc(v.tensor_tensor(nbuf[:, sl, :], t132[:, sl, :], g1b[:],
                                                ALU.mult), "ve")
                        ctx.w(v, "ve")
                        ctx.inc(v.tensor_tensor(nbuf[:, sl, :], nbuf[:, sl, :], be1b[:],
                                                ALU.add), "ve")
                        ctx.w(v, "ve")
                        ctx.inc(v.tensor_tensor(stile[:, sl, :], z32[:, sl, :],
                                                nbuf[:, sl, :], ALU.add), "ve")
                        ev_stile[sl] = ctx.n["ve"]
                        ctx.w(s, "ve", ev_stile[sl])
                        if ev_tp_slot[sl]:
                            ctx.w(s, "tpm", ev_tp_slot[sl])
                        i_c = ctx.inc(s.copy(n16[:, sl, :], nbuf[:, sl, :]), "se")
                        ctx.w(sy, "se", i_c)
                        if ev_mm:
                            ctx.w(sy, "pe", ev_mm)
                        ctx.dinc(sy.dma_start(nT[:], n16[:, sl, :], transpose=True), "tpm")
                        ev_tp = ctx.n["tpm"]
                        ev_tp_slot[sl] = ev_tp
                        ctx.w(te, "tpm", ev_tp)
                        if ev_stt:
                            ctx.w(te, "ve", ev_stt)
                        for nh in range(2):
                            mm = te.matmul(pool[:, nh, :], ones16,
                                           mixb[0:1, nh * 512:(nh + 1) * 512],
                                           start=True, stop=False)
                            for dc in range(NDC):
                                mm = te.matmul(pool[:, nh, :], nT[:, dc, :],
                                               mixw[:, dc, nh * 512:(nh + 1) * 512],
                                               start=False, stop=(dc == NDC - 1))
                        i_mm = ctx.inc(mm, "pe")
                        ev_mm = ctx.n["pe"]
                        ctx.w(v, "pe", i_mm)
                        if ev_so_slot[sl]:
                            ctx.w(v, f"so{sl}", ev_so_slot[sl])
                        ins = v.scalar_tensor_tensor(
                            onew[:, sl, :],
                            pool[:, 0:2, :].rearrange("p a b -> p (a b)"),
                            1.0, stile[:, sl, :], ALU.mult, ALU.add)
                        ctx.inc(ins, "ve")
                        ev_stt = ctx.n["ve"]
                        ev_stt_slot[sl] = ev_stt
                        ctx.w(sy, "ve", ev_stt)
                        ctx.dinc(sy.dma_start(xm[i * 128:(i + 1) * 128, :],
                                              onew[:, sl, :]), f"so{sl}")
                        ev_so_slot[sl] = ctx.n[f"so{sl}"]

        with ExitStack() as esE:
            es = esE
            g2b = sbo("g2b", [128, D], F32)
            be2b = sbo("be2b", [128, D], F32)
            rows2 = sbo("rows2", [1, 2, D], F32)
            routw = sbo("routw", [128, NDC, 8], F32)
            w1h = [sbo(f"w1h{i}", [128, NDC, 2048], BF16) for i in range(2)]
            w2ch = sbo("w2ch", [128, 2, 4, D], BF16)
            b1cm = sbo("b1cm", [128, 32], F32)
            b2row = sbo("b2row", [1, D], BF16)
            h_sb = sbo("h", [128, 32, TS], BF16)
            n2T = sbo("n2T", [128, NDC, TS], BF16)
            n2T32 = sbo("n2T32", [128, NDC, 128], F32)
            acc = sbo("acc", [128, TPS, D], F32)
            xs2 = sbo("xs2", [128, 2, D], F32)
            xr2 = sbo("xr2", [128, 2, D], F32)
            t232 = sbo("t232", [128, 2, D], F32)
            n2buf = sbo("n2buf", [128, 2, D], F32)
            n216 = sbo("n216", [128, 2, D], BF16)
            junk2 = sbo("junk2", [128, D], BF16)
            stats2 = alloc_stats(nc, es, pfx + "2", 2)
            tkB = alloc_topk(nc, es, pfx)
            lg = sbo("lg", [128, 8], F32)
            m1c = sbo("m1c", [128, 1], F32)
            m2c = sbo("m2c", [128, 1], F32)
            mask1 = sbo("mask1", [128, 8], F32)
            mask2 = sbo("mask2", [128, 8], F32)
            comb = sbo("comb", [128, TPS, 8], F32)
            with nc.Block() as blk:
                @blk.gpsimd
                def _(gp):
                    ctx.w(sy, "so0")
                    ctx.w(sy, "so1")
                    ctx.w(g, "so0")
                    ctx.w(g, "so1")
                    ctx.w(g, "ve")
                    ctx.w(g, "se")
                    ctx.dinc(g.dma_start(rows2[:, 0, :], W["l1f_g2r"]), "dg")
                    ctx.dinc(g.dma_start(rows2[:, 1, :], W["l1f_be2r"]), "dg")
                    ctx.dinc(g.dma_start(routw[:], W["l1f_routw"].rearrange("(c p) e -> p c e", p=128)), "dg")
                    w_ready = ctx.n["dg"]
                    ctx.w(te, "dg", w_ready)
                    ctx.w(te, "ve")
                    nd = [0]
                    bcast_row(rows2[:, 0, :], g2b[:], nd)
                    bcast_row(rows2[:, 1, :], be2b[:], nd)
                    ev_bc = ctx.n["ve"]
                    ev_gmm = 0
                    ev_tbank = 0
                    ev_lg_free = 0
                    ev_hband = [0, 0]
                    ev_w2evac = 0
                    ev_hmm = 0
                    ev_acc_store = 0
                    ev_w1ld = [0, 0]
                    ev_w2sl = [0, 0]
                    ev_norm2 = {}
                    ev_tp2 = 0
                    ev_cast2 = [0, 0]
                    ev_hmm = 0
                    ev_w1mm = [0, 0]
                    ev_w2mmsl = [0, 0]
                    ev_probs_se = 0
                    ev_h_all_last = 0
                    ev_b2_used = 0
                    ev_tp2_slot = [0, 0]
                    for sli in range(NSL):
                        for tt in range(TPS):
                            i = sli * TPS + tt
                            sl = i % 2
                            if i >= 2:
                                ctx.w(sy, "ve", ev_norm2.get(i - 2, 0))
                            ctx.dinc(sy.dma_start(xs2[:, sl, :],
                                                  xm[i * 128:(i + 1) * 128, :]), f"sm{sl}")
                            ctx.dinc(sy.dma_start(xr2[:, sl, :],
                                                  xin[i * 128:(i + 1) * 128, :]), f"sr{sl}")
                            ctx.w(v, f"sm{sl}")
                            ctx.w(v, f"sr{sl}")
                            ctx.w(s, f"sm{sl}")
                            if sli == 0 and tt == 0:
                                ctx.w(v, "ve", ev_bc)
                            if i >= 2:
                                ctx.w(v, "se", ev_cast2[sl])
                                if ev_gmm:
                                    ctx.w(v, "pe", ev_gmm)
                            env = emit_ln_tile(ctx, xs2[:, sl, :], t232[:, sl, :],
                                               junk2[:], stats2[sl])
                            ctx.w(v, "ve", env)
                            ctx.inc(v.tensor_tensor(n2buf[:, sl, :], t232[:, sl, :],
                                                    g2b[:], ALU.mult), "ve")
                            ctx.w(v, "ve")
                            ctx.inc(v.tensor_tensor(n2buf[:, sl, :], n2buf[:, sl, :],
                                                    be2b[:], ALU.add), "ve")
                            ctx.w(v, "ve")
                            ctx.w(v, "so_acc", ev_acc_store)
                            ctx.inc(v.tensor_tensor(acc[:, tt, :], xr2[:, sl, :],
                                                    xs2[:, sl, :], ALU.add), "ve")
                            ev_norm2[i] = ctx.n["ve"]
                            ctx.w(s, "ve", ev_norm2[i])
                            if ev_tp2_slot[sl]:
                                ctx.w(s, f"tpm{sl}", ev_tp2_slot[sl])
                            ev_cast2[sl] = ctx.inc(s.copy(n216[:, sl, :], n2buf[:, sl, :]), "se")
                            ctx.w(sy, "se", ev_cast2[sl])
                            if sli >= 1 and tt == 0:
                                ctx.w(sy, "pe", ev_hmm)
                            ctx.dinc(sy.dma_start(n2T[:, :, tt * 128:(tt + 1) * 128],
                                                  n216[:, sl, :], transpose=True), f"tpm{sl}")
                            ev_tp2_slot[sl] = ctx.n[f"tpm{sl}"]
                            ctx.w(te, "ve", ev_norm2[i])
                            for dc in range(NDC):
                                if ev_tbank:
                                    ctx.w(te, "ve", ev_tbank)
                                i_t = ctx.inc(te.transpose(ps_t, n2buf[:, sl, dc * 128:(dc + 1) * 128],
                                                           id32), "pe")
                                ctx.w(v, "pe", i_t)
                                if ev_gmm:
                                    ctx.w(v, "pe", ev_gmm)
                                ctx.inc(v.tensor_copy(n2T32[:, dc, :], ps_t), "ve")
                                ev_tbank = ctx.n["ve"]
                            i_t32 = ctx.n["ve"]
                            ctx.w(te, "ve", i_t32)
                            if ev_lg_free:
                                ctx.w(te, "ve", ev_lg_free)
                            for dc in range(NDC):
                                mm = te.matmul(pool[:, 6, 0:8], n2T32[:, dc, :],
                                               routw[:, dc, :], start=(dc == 0),
                                               stop=(dc == NDC - 1))
                            i_g = ctx.inc(mm, "pe")
                            ev_gmm = i_g
                            ctx.w(v, "pe", i_g)
                            if ev_lg_free:
                                ctx.w(v, "se", ev_probs_se)
                            ctx.inc(v.tensor_copy(lg[:], pool[:, 6, 0:8]), "ve")
                            ev_lg_free = ctx.n["ve"]
                            ctx.w(v, "ve")
                            emit_probs(ctx, lg[:], tkB)
                            ev_probs_se = ctx.n["se"]
                            emit_argmax_mask(ctx, tkB["pr"], mask1[:], tkB, save_max=m1c[:])
                            ctx.w(v, "ve")
                            ctx.inc(v.scalar_tensor_tensor(tkB["lg2"], mask1[:], -1e30,
                                                           tkB["pr"], ALU.mult, ALU.add), "ve")
                            ctx.w(v, "ve")
                            emit_argmax_mask(ctx, tkB["lg2"], mask2[:], tkB, save_max=m2c[:])
                            ctx.w(v, "ve")
                            ctx.inc(v.tensor_scalar(mask1[:], mask1[:], m1c[:], None,
                                                    ALU.mult), "ve")
                            ctx.w(v, "ve")
                            ctx.inc(v.tensor_scalar(mask2[:], mask2[:], m2c[:], None,
                                                    ALU.mult), "ve")
                            ctx.w(v, "ve")
                            ctx.inc(v.tensor_tensor(comb[:, tt, :], mask1[:], mask2[:],
                                                    ALU.add), "ve")
                            ctx.w(v, "ve")
                        ev_prep_done = ctx.n["ve"]
                        for e in range(8):
                            for hh in range(2):
                                whsl = hh
                                if ev_w1mm[whsl]:
                                    ctx.w(s, "pe", ev_w1mm[whsl])
                                ctx.dinc(s.dma_start(w1h[whsl][:],
                                                     w1b[e, :, :, hh * 2048:(hh + 1) * 2048]),
                                         f"wl{whsl}")
                                ev_w1ld[whsl] = ctx.n[f"wl{whsl}"]
                            if ev_h_all_last:
                                ctx.w(g, "se", ev_h_all_last)
                                ctx.w(g, "pe", ev_b2_used)
                            ctx.dinc(g.dma_start(b1cm[:], W["l1f_b1cm"][e]), "dgb")
                            ctx.dinc(g.dma_start(b2row[:], W["l1f_b2"][e:e + 1, :]), "dgb")
                            ev_b = ctx.n["dgb"]
                            ctx.w(te, "tpm0", ev_tp2_slot[0])
                            ctx.w(te, "tpm1", ev_tp2_slot[1])
                            ctx.w(te, "ve", ev_prep_done)
                            if ev_w2evac:
                                ctx.w(te, "ve", ev_w2evac)
                            for fc in range(32):
                                hh = fc // 16
                                bank = fc % 2
                                ctx.w(te, f"wl{hh}", ev_w1ld[hh])
                                if ev_hband[bank]:
                                    ctx.w(te, "se", ev_hband[bank])
                                for dc in range(NDC):
                                    mm = te.matmul(pool[:, bank, :],
                                                   w1h[hh][:, dc, (fc % 16) * 128:(fc % 16 + 1) * 128],
                                                   n2T[:, dc, :], start=(dc == 0),
                                                   stop=(dc == NDC - 1))
                                i_h = ctx.inc(mm, "pe")
                                if fc == 15:
                                    ev_w1mm_half0 = ctx.n["pe"]
                                ctx.w(s, "pe", i_h)
                                if fc == 0:
                                    ctx.w(s, "dgb", ev_b)
                                ctx.inc(s.activation(h_sb[:, fc, :], pool[:, bank, :],
                                                     AF.Silu, bias=b1cm[:, fc:fc + 1]), "se")
                                ev_hband[bank] = ctx.n["se"]
                            ev_w1mm = [ev_w1mm_half0, ctx.n["pe"]]
                            ev_h_all = ctx.n["se"]
                            ev_h_all_last = ev_h_all
                            def w2_load(ck):
                                w2sl = ck % 2
                                if ev_w2mmsl[w2sl]:
                                    ctx.w(s, "pe", ev_w2mmsl[w2sl])
                                ctx.dinc(s.dma_start(w2ch[:, w2sl, :, :],
                                                     w2b[e, :, ck * 4:(ck + 1) * 4, :]),
                                         f"w2l{w2sl}")
                                ev_w2sl[w2sl] = ctx.n[f"w2l{w2sl}"]
                            w2_load(0)
                            w2_load(1)
                            for fc in range(32):
                                ck = fc // 4
                                w2sl = ck % 2
                                if fc == 0:
                                    ctx.w(te, "se", ev_h_all)
                                if fc % 4 == 0:
                                    ctx.w(te, f"w2l{w2sl}", ev_w2sl[w2sl])
                                for tt in range(TPS):
                                    for nh in range(2):
                                        pb = pool[:, tt * 2 + nh, :]
                                        if fc == 0:
                                            mm = te.matmul(pb, ones16,
                                                           b2row[0:1, nh * 512:(nh + 1) * 512],
                                                           start=True, stop=False)
                                        mm = te.matmul(pb, h_sb[:, fc, tt * 128:(tt + 1) * 128],
                                                       w2ch[:, w2sl, fc % 4, nh * 512:(nh + 1) * 512],
                                                       start=False, stop=(fc == 31))
                                i_w2 = ctx.inc(mm, "pe")
                                if fc % 4 == 3:
                                    ev_w2mmsl[w2sl] = ctx.n["pe"]
                                    if ck + 2 < 8:
                                        w2_load(ck + 2)
                            i_w2_all = ctx.n["pe"]
                            ev_b2_used = i_w2_all
                            ctx.w(v, "pe", i_w2_all)
                            ctx.w(v, "ve", ev_prep_done)
                            for tt in range(TPS):
                                ins = v.scalar_tensor_tensor(
                                    acc[:, tt, :],
                                    pool[:, tt * 2:(tt + 1) * 2, :].rearrange("p a b -> p (a b)"),
                                    comb[:, tt, e:e + 1], acc[:, tt, :],
                                    ALU.mult, ALU.add)
                                ctx.inc(ins, "ve")
                                ctx.w(v, "ve")
                            ev_w2evac = ctx.n["ve"]
                        ev_hmm = ctx.n["pe"]
                        for tt in range(TPS):
                            i = sli * TPS + tt
                            ctx.w(sy, "ve", ev_w2evac)
                            ctx.dinc(sy.dma_start(xout[i * 128:(i + 1) * 128, :],
                                                  acc[:, tt, :]), "so_acc")
                        ev_acc_store = ctx.n["so_acc"]
    ctx.barrier()





def emit_l1b(ctx, xin, xout, W):
    nc = ctx.nc
    te, v, s, g, sy = nc.tensor, nc.vector, nc.scalar, nc.gpsimd, nc.sync
    uid = ctx.uid()
    pfx = f"rb{uid}_"
    qsp = nc.dram_tensor(f"rqsp{uid}", [D, T], BF16, kind="Internal").ap()
    ksp = nc.dram_tensor(f"rksp{uid}", [D, T], BF16, kind="Internal").ap()
    vsp = nc.dram_tensor(f"rvsp{uid}", [T, D], BF16, kind="Internal").ap()
    with ExitStack() as eso:
        def sbo(name, shape, dt):
            return es.enter_context(nc.sbuf_tensor(pfx + name, shape, dt))
        es = eso
        o_full = sbo("ofull", [128, NT, D], BF16)
        wo_w = sbo("wo", [128, NDC, D], BF16)
        borow = sbo("bo", [1, D], BF16)
        masks = sbo("masks", [128, 4, 512], F32)
        ones16 = ctx.consts["ones16"][:]

        with nc.Block() as blk:
            @blk.gpsimd
            def _(gp):
                ctx.dinc(g.dma_start(wo_w[:], W["l1b_woT"].rearrange("(c p) f -> p c f", p=128)), "dg")
                ctx.dinc(g.dma_start(borow[:], W["l1b_ob"]), "dg")
                for m in range(4):
                    ctx.inc(g.memset(masks[:, m, :], 0.0), "po")
                    ctx.w(g, "po")
                    ctx.inc(g.affine_select(masks[:, m, :], masks[:, m, :], [[-1, 512]],
                                            ALU.is_ge, -1e30, base=128 * m,
                                            channel_multiplier=1), "po")
                    ctx.w(g, "po")

        with ExitStack() as esA:
            es = esA
            qkvw = sbo("qkvw", [128, NDC, 3072], BF16)
            qkvb = sbo("qkvb", [1, 3072], BF16)
            cosT = sbo("cosT", [128, NT, 32], F32)
            sinT = sbo("sinT", [128, NT, 32], F32)
            xs = sbo("xs", [128, 2, D], F32)
            xln = sbo("xln", [128, 2, D], BF16)
            xlnT = sbo("xlnT", [128, NDC, 128], BF16)
            junk = sbo("junk", [128, D], BF16)
            stats = alloc_stats(nc, es, pfx, 2)
            qf = sbo("qf", [128, D], F32)
            kf = sbo("kf", [128, D], F32)
            t1 = sbo("t1", [128, 512], F32)
            t2 = sbo("t2", [128, 512], F32)
            rq = sbo("rq", [128, 2, D], BF16)
            rk = sbo("rk", [128, 2, D], BF16)
            vstage = sbo("vst", [128, 2, D], BF16)
            stq = sbo("stq", [128, NDC, 128], BF16)
            ps_p = [es.enter_context(nc.psum_tensor(f"{pfx}psp{i}", [128, 512], F32))
                    for i in range(2)]
            with nc.Block() as blk:
                @blk.gpsimd
                def _(gp):
                    ctx.dinc(g.dma_start(qkvw[:], W["l1b_qkvT"].rearrange("(c p) f -> p c f", p=128)), "dg")
                    ctx.dinc(g.dma_start(qkvb[:], W["l1b_qkvb"]), "dg")
                    ctx.dinc(g.dma_start(cosT[:], W["l1b_cos"][0:T, :].rearrange("(t p) i -> p t i", p=128)), "dg")
                    ctx.dinc(g.dma_start(sinT[:], W["l1b_sin"][0:T, :].rearrange("(t p) i -> p t i", p=128)), "dg")
                    w_ready = ctx.n["dg"]
                    ev_norm = [0] * NT
                    ev_tp = [0, 0]
                    ev_pbank = [0, 0]
                    ev_mm_tile = 0
                    ev_rot = [0, 0]
                    ev_rot_last = 0
                    ev_tpq = [0, 0]
                    ev_tpk = [0, 0]
                    ev_stq = 0
                    ev_sv = [0, 0]
                    ev_evac_q = 0
                    ev_evac_v = [0, 0]
                    first = True
                    for i in range(NT):
                        sl = i % 2
                        if i >= 2:
                            ctx.w(sy, "ve", ev_norm[i - 2])
                        ctx.dinc(sy.dma_start(xs[:, sl, :],
                                              xin[i * 128:(i + 1) * 128, :]), f"sx{sl}")
                        ctx.w(v, f"sx{sl}")
                        ctx.w(s, f"sx{sl}")
                        if i >= 2:
                            ctx.w(v, f"tp{sl}", ev_tp[sl])
                        ev_norm[i] = emit_ln_tile(ctx, xs[:, sl, :], xln[:, sl, :],
                                                  junk[:], stats[sl])
                        ctx.w(sy, "ve", ev_norm[i])
                        if ev_mm_tile:
                            ctx.w(sy, "pe", ev_mm_tile)
                        ctx.dinc(sy.dma_start(xlnT[:], xln[:, sl, :], transpose=True),
                                 f"tp{sl}")
                        ev_tp[sl] = ctx.n[f"tp{sl}"]
                        ctx.w(te, f"tp{sl}", ev_tp[sl])
                        if first:
                            ctx.w(te, "dg", w_ready)
                            first = False
                        for fc6 in range(6):
                            bank = fc6 % 2
                            if ev_pbank[bank]:
                                ctx.w(te, "se" if fc6 < 4 else "se", ev_pbank[bank])
                            mm = te.matmul(ps_p[bank][:], ones16,
                                           qkvb[0:1, fc6 * 512:(fc6 + 1) * 512],
                                           start=True, stop=False)
                            for dc in range(NDC):
                                mm = te.matmul(ps_p[bank][:], xlnT[:, dc, :],
                                               qkvw[:, dc, fc6 * 512:(fc6 + 1) * 512],
                                               start=False, stop=(dc == NDC - 1))
                            i_p = ctx.inc(mm, "pe")
                            ctx.w(s, "pe", i_p)
                            if fc6 < 2:
                                if fc6 == 0 and ev_rot_last:
                                    ctx.w(s, "ve", ev_rot_last)
                                ctx.inc(s.activation(qf[:, fc6 * 512:(fc6 + 1) * 512],
                                                     ps_p[bank][:], AF.Copy, scale=0.125),
                                        "se")
                            elif fc6 < 4:
                                ctx.inc(s.activation(kf[:, (fc6 - 2) * 512:(fc6 - 1) * 512],
                                                     ps_p[bank][:], AF.Copy), "se")
                            else:
                                if ev_sv[sl] and fc6 == 4:
                                    ctx.w(s, f"sv{sl}", ev_sv[sl])
                                ctx.inc(s.copy(vstage[:, sl, (fc6 - 4) * 512:(fc6 - 3) * 512],
                                               ps_p[bank][:]), "se")
                            ev_pbank[bank] = ctx.n["se"]
                        ev_mm_tile = ctx.n["pe"]
                        ev_evac_q = ctx.n["se"]
                        ctx.w(sy, "se", ev_evac_q)
                        ctx.dinc(sy.dma_start(
                            vsp[i * 128:(i + 1) * 128, :], vstage[:, sl, :]), f"sv{sl}")
                        ev_sv[sl] = ctx.n[f"sv{sl}"]
                        ctx.w(v, "se", ev_evac_q)
                        if i >= 2:
                            ctx.w(v, f"tpq{sl}", ev_tpq[sl])
                            ctx.w(v, f"tpk{sl}", ev_tpk[sl])
                        cosb = cosT[:, i, :].rearrange("p (o i) -> p o i", o=1)\
                                            .broadcast_to([128, 16, 32])
                        sinb = sinT[:, i, :].rearrange("p (o i) -> p o i", o=1)\
                                            .broadcast_to([128, 16, 32])
                        for src_f, dst in ((qf, rq), (kf, rk)):
                            sv = src_f[:].rearrange("p (h j) -> p h j", h=16)
                            dv = dst[:, sl, :].rearrange("p (h j) -> p h j", h=16)
                            ctx.inc(v.tensor_tensor(t1[:].rearrange("p (h j) -> p h j", h=16),
                                                    sv[:, :, 0:32], cosb, ALU.mult), "ve")
                            ctx.inc(v.tensor_tensor(t2[:].rearrange("p (h j) -> p h j", h=16),
                                                    sv[:, :, 32:64], sinb, ALU.mult), "ve")
                            ctx.w(v, "ve")
                            ctx.inc(v.tensor_tensor(dv[:, :, 0:32],
                                                    t1[:].rearrange("p (h j) -> p h j", h=16),
                                                    t2[:].rearrange("p (h j) -> p h j", h=16),
                                                    ALU.subtract), "ve")
                            ctx.w(v, "ve")
                            ctx.inc(v.tensor_tensor(t1[:].rearrange("p (h j) -> p h j", h=16),
                                                    sv[:, :, 0:32], sinb, ALU.mult), "ve")
                            ctx.inc(v.tensor_tensor(t2[:].rearrange("p (h j) -> p h j", h=16),
                                                    sv[:, :, 32:64], cosb, ALU.mult), "ve")
                            ctx.w(v, "ve")
                            ctx.inc(v.tensor_tensor(dv[:, :, 32:64],
                                                    t1[:].rearrange("p (h j) -> p h j", h=16),
                                                    t2[:].rearrange("p (h j) -> p h j", h=16),
                                                    ALU.add), "ve")
                            ctx.w(v, "ve")
                        ev_rot[sl] = ctx.n["ve"]
                        ev_rot_last = ev_rot[sl]
                        ctx.w(sy, "ve", ev_rot[sl])
                        if ev_stq:
                            ctx.w(sy, "sq", ev_stq)
                        ctx.dinc(sy.dma_start(stq[:], rq[:, sl, :], transpose=True), f"tpq{sl}")
                        ev_tpq[sl] = ctx.n[f"tpq{sl}"]
                        ctx.w(sy, f"tpq{sl}", ev_tpq[sl])
                        ctx.dinc(sy.dma_start(
                            qsp[:, i * 128:(i + 1) * 128].rearrange("(c p) t -> p c t", p=128),
                            stq[:]), "sq")
                        ev_stq = ctx.n["sq"]
                        ctx.w(sy, "sq", ev_stq)
                        ctx.dinc(sy.dma_start(stq[:], rk[:, sl, :], transpose=True), f"tpk{sl}")
                        ev_tpk[sl] = ctx.n[f"tpk{sl}"]
                        ctx.w(sy, f"tpk{sl}", ev_tpk[sl])
                        ctx.dinc(sy.dma_start(
                            ksp[:, i * 128:(i + 1) * 128].rearrange("(c p) t -> p c t", p=128),
                            stq[:]), "sq")
                        ev_stq = ctx.n["sq"]

        with ExitStack() as esB:
            es = esB
            s_sb = sbo("s_sb", [128, 2, T], F32)
            p_sb = sbo("p_sb", [128, 2, T], BF16)
            pt_sb = sbo("pt", [128, 2, NT, 128], BF16)
            k_h = sbo("k_h", [64, 2, T], BF16)
            v_h = sbo("v_h", [128, 2, NT, 64], BF16)
            q_st = sbo("q_st", [64, 2, 128], BF16)
            mcol = sbo("mcol", [128, 2], F32)
            negm = sbo("negm", [128, 2], F32)
            lcol = sbo("lcol", [128, 2], F32)
            rlcol = sbo("rlcol", [128, 2], F32)
            ps_s = [es.enter_context(nc.psum_tensor(f"{pfx}pss{i}", [128, 512], F32))
                    for i in range(2)]
            ps_pv = [es.enter_context(nc.psum_tensor(f"{pfx}pspv{i}", [128, 64], F32))
                     for i in range(2)]
            with nc.Block() as blk:
                @blk.gpsimd
                def _(gp):
                    ctx.w(te, "se")
                    ctx.w(te, "ve")
                    ctx.w(sy, "sq")
                    ctx.w(sy, "sv0")
                    ctx.w(sy, "sv1")
                    ev_ssbank = [0, 0]
                    ev_exp_slot = [0, 0]
                    ev_recip_slot = [0, 0]
                    ev_pt_slot = [0, 0]
                    ev_pv_slot = [0, 0]
                    ev_pvbank = [0, 0]
                    ev_kv = [0, 0]
                    ev_scores_q = [0, 0]
                    ev_mask_slot = [0, 0]

                    def emit_front(h, qt, hsl):
                        qsl = qt % 2
                        nv = qt // 4 + 1
                        if ev_scores_q[qsl]:
                            ctx.w(sy, "pe", ev_scores_q[qsl])
                        ctx.dinc(sy.dma_start(q_st[:, qsl, :],
                                              qsp[h * 64:(h + 1) * 64,
                                                  qt * 128:(qt + 1) * 128]), f"qh{qsl}")
                        ctx.w(te, f"qh{qsl}")
                        for kt in range(nv):
                            bank = kt % 2
                            if ev_ssbank[bank]:
                                ctx.w(te, "se", ev_ssbank[bank])
                            mm = te.matmul(ps_s[bank][:], q_st[:, qsl, :],
                                           k_h[:, hsl, kt * 512:(kt + 1) * 512],
                                           start=True, stop=True)
                            i_s = ctx.inc(mm, "pe")
                            ctx.w(s, "pe", i_s)
                            if kt == 0:
                                if ev_exp_slot[qsl]:
                                    ctx.w(s, "se", ev_exp_slot[qsl])
                                if ev_mask_slot[qsl]:
                                    ctx.w(s, "ve", ev_mask_slot[qsl])
                            ctx.inc(s.activation(s_sb[:, qsl, kt * 512:(kt + 1) * 512],
                                                 ps_s[bank][:], AF.Copy), "se")
                            ev_ssbank[bank] = ctx.n["se"]
                        ev_scores_q[qsl] = ctx.n["pe"]
                        ev_sevac = ctx.n["se"]
                        ctx.w(v, "se", ev_sevac)
                        ctx.inc(v.tensor_tensor(
                            s_sb[:, qsl, (nv - 1) * 512:nv * 512],
                            s_sb[:, qsl, (nv - 1) * 512:nv * 512],
                            masks[:, qt % 4, :], ALU.add), "ve")
                        ev_mask_slot[qsl] = ctx.n["ve"]
                        ctx.w(v, "ve")
                        ctx.inc(v.reduce_max(mcol[:, qsl:qsl + 1],
                                             s_sb[:, qsl, 0:nv * 512], axis=AX.X), "ve")
                        ctx.w(v, "ve")
                        if ev_exp_slot[qsl]:
                            ctx.w(v, "se", ev_exp_slot[qsl])
                        ctx.inc(v.tensor_scalar(negm[:, qsl:qsl + 1], mcol[:, qsl:qsl + 1],
                                                -1.0, None, ALU.mult), "ve")
                        ev_rmax = ctx.n["ve"]
                        ctx.w(s, "ve", ev_rmax)
                        if ev_pt_slot[qsl]:
                            ctx.w(s, f"pt{qsl}", ev_pt_slot[qsl])
                        if ev_recip_slot[qsl]:
                            ctx.w(s, "ve", ev_recip_slot[qsl])
                        ev_exp_slot[qsl] = ctx.inc(
                            s.activation(p_sb[:, qsl, 0:nv * 512],
                                         s_sb[:, qsl, 0:nv * 512], AF.Exp,
                                         bias=negm[:, qsl:qsl + 1],
                                         accum_out=lcol[:, qsl:qsl + 1]), "se")
                        ctx.w(v, "se", ev_exp_slot[qsl])
                        ctx.inc(v.reciprocal(rlcol[:, qsl:qsl + 1], lcol[:, qsl:qsl + 1]),
                                "ve")
                        ev_recip_slot[qsl] = ctx.n["ve"]
                        ctx.w(sy, "se", ev_exp_slot[qsl])
                        if ev_pv_slot[qsl]:
                            ctx.w(sy, "pe", ev_pv_slot[qsl])
                        ctx.dinc(sy.dma_start(pt_sb[:, qsl, 0:nv * 4, :],
                                              p_sb[:, qsl, 0:nv * 512], transpose=True),
                                 f"pt{qsl}")
                        ev_pt_slot[qsl] = ctx.n[f"pt{qsl}"]

                    def emit_back(h, qt, hsl):
                        qsl = qt % 2
                        nkc = qt + 1
                        ctx.w(te, f"pt{qsl}", ev_pt_slot[qsl])
                        bank = qsl
                        if ev_pvbank[bank]:
                            ctx.w(te, "ve", ev_pvbank[bank])
                        for kc in range(nkc):
                            mm = te.matmul(ps_pv[bank][:], pt_sb[:, qsl, kc, :],
                                           v_h[:, hsl, kc, :],
                                           start=(kc == 0), stop=(kc == nkc - 1))
                        i_pv = ctx.inc(mm, "pe")
                        ev_pv_slot[qsl] = ctx.n["pe"]
                        ctx.w(v, "pe", i_pv)
                        ctx.w(v, "ve", ev_recip_slot[qsl])
                        ctx.inc(v.tensor_scalar(o_full[:, qt, h * 64:(h + 1) * 64],
                                                ps_pv[bank][:], rlcol[:, qsl:qsl + 1],
                                                None, ALU.mult), "ve")
                        ev_pvbank[bank] = ctx.n["ve"]

                    for h in range(16):
                        hsl = h % 2
                        if h >= 2:
                            ctx.w(sy, "pe", ev_kv[hsl])
                        ctx.dinc(sy.dma_start(k_h[:, hsl, :],
                                              ksp[h * 64:(h + 1) * 64, :]), f"kh{hsl}")
                        ctx.dinc(sy.dma_start(
                            v_h[:, hsl, :, :],
                            vsp[:, h * 64:(h + 1) * 64].rearrange("(i p) d -> p i d", p=128)),
                            f"kh{hsl}")
                        ctx.w(te, f"kh{hsl}")
                        for qt in range(NT):
                            emit_front(h, qt, hsl)
                            if qt >= 1:
                                emit_back(h, qt - 1, hsl)
                        emit_back(h, NT - 1, hsl)
                        ev_kv[hsl] = ctx.n["pe"]

        with ExitStack() as esC:
            es = esC
            oT = sbo("oT", [128, NDC, 128], BF16)
            xold = sbo("xold", [128, 1, D], F32)
            onew = sbo("onew", [128, 2, D], F32)
            ps_c = es.enter_context(nc.psum_tensor(pfx + "psc", [128, 1024], F32))
            with nc.Block() as blk:
                @blk.gpsimd
                def _(gp):
                    ctx.w(te, "ve")
                    ctx.w(sy, "ve")
                    ev_stt = 0
                    ev_so_slot = [0, 0]
                    ev_cmm = 0
                    for qt in range(NT):
                        sl = qt % 2
                        if ev_cmm:
                            ctx.w(sy, "pe", ev_cmm)
                        ctx.dinc(sy.dma_start(oT[:], o_full[:, qt, :], transpose=True),
                                 "pt0")
                        ctx.w(te, "pt0")
                        if ev_stt:
                            ctx.w(te, "ve", ev_stt)
                        for nh in range(2):
                            mm = te.matmul(ps_c[:, nh * 512:(nh + 1) * 512], ones16,
                                           borow[0:1, nh * 512:(nh + 1) * 512],
                                           start=True, stop=False)
                            for dc in range(NDC):
                                mm = te.matmul(ps_c[:, nh * 512:(nh + 1) * 512],
                                               oT[:, dc, :],
                                               wo_w[:, dc, nh * 512:(nh + 1) * 512],
                                               start=False, stop=(dc == NDC - 1))
                        i_c = ctx.inc(mm, "pe")
                        ev_cmm = ctx.n["pe"]
                        if ev_stt:
                            ctx.w(sy, "ve", ev_stt)
                        ctx.dinc(sy.dma_start(xold[:, 0, :],
                                              xin[qt * 128:(qt + 1) * 128, :]), "xo0")
                        ctx.w(v, "pe", i_c)
                        ctx.w(v, "xo0")
                        if ev_so_slot[sl]:
                            ctx.w(v, f"so{sl}", ev_so_slot[sl])
                        ins = v.scalar_tensor_tensor(onew[:, sl, :], ps_c[:], 1.0,
                                                     xold[:, 0, :], ALU.mult, ALU.add)
                        ctx.inc(ins, "ve")
                        ev_stt = ctx.n["ve"]
                        ctx.w(sy, "ve", ev_stt)
                        ctx.dinc(sy.dma_start(xout[qt * 128:(qt + 1) * 128, :],
                                              onew[:, sl, :]), f"so{sl}")
                        ev_so_slot[sl] = ctx.n[f"so{sl}"]
    ctx.barrier()



def _prep_weights(inp):
    f32 = lambda a: np.ascontiguousarray(np.asarray(a), dtype=np.float32)
    W = {}
    W["l0a_QT"] = f32(inp["L0_attn_a_Q"].T)
    W["l0a_KT"] = f32(inp["L0_attn_a_K"].T)
    W["l0a_VO"] = f32(inp["L0_attn_a_VO"])
    W["l0f_wiT"] = f32(inp["L0_ffn_wi_W"].T)
    W["l0f_woT"] = f32(inp["L0_ffn_wo_W"].T)
    W["l0f_gateT"] = f32(inp["L0_ffn_gate_W"].T)
    W["l0f_gate_b"] = f32(inp["L0_ffn_gate_b"]).reshape(1, 8)
    A = np.asarray(inp["L0_ffn_lora_A"])
    W["l0f_AallT"] = f32(A.transpose(2, 0, 1).reshape(1024, 32))
    Bm = np.asarray(inp["L0_ffn_lora_B"])
    W["l0f_BallT"] = f32(Bm.transpose(0, 2, 1).reshape(32, 4096))
    W["l0f_wib_cm"] = f32(np.asarray(inp["L0_ffn_wi_b"]).reshape(32, 128).T)
    W["l0f_wo_b"] = f32(inp["L0_ffn_wo_b"]).reshape(1, 1024)
    W["l0b_qkW"] = f32(np.concatenate([np.asarray(inp["L0_attn_b_Wq"]),
                                       np.asarray(inp["L0_attn_b_Wk"])], axis=1))
    W["l0b_wv"] = f32(inp["L0_attn_b_Wv"])
    W["l0b_wo"] = f32(inp["L0_attn_b_Wo"])
    bqk = np.concatenate([np.asarray(inp["L0_attn_b_bq"]),
                          np.asarray(inp["L0_attn_b_bk"])])
    W["l0b_bqk_cm"] = f32(bqk.reshape(16, 128).T)
    W["l0b_bv"] = f32(inp["L0_attn_b_bv"]).reshape(1, 1024)
    W["l0b_bo"] = f32(inp["L0_attn_b_bo"]).reshape(1, 1024)
    W["l1a_rot"] = f32(inp["L1_attn_a_rot_w"])
    W["l1a_scb"] = f32(np.stack([np.asarray(inp["L1_attn_a_scale"]).reshape(()),
                                 np.asarray(inp["L1_attn_a_bias"]).reshape(())])).reshape(1, 2)
    W["l1f_g1r"] = f32(inp["L1_ffn_g1"]).reshape(1, 1024)
    W["l1f_be1r"] = f32(inp["L1_ffn_be1"]).reshape(1, 1024)
    W["l1f_mixw"] = f32(inp["L1_ffn_mix_w"])
    W["l1f_mixb"] = f32(inp["L1_ffn_mix_b"]).reshape(1, 1024)
    W["l1f_g2r"] = f32(inp["L1_ffn_g2"]).reshape(1, 1024)
    W["l1f_be2r"] = f32(inp["L1_ffn_be2"]).reshape(1, 1024)
    W["l1f_routw"] = f32(inp["L1_ffn_router_w"])
    W["l1f_w1"] = f32(inp["L1_ffn_w1"])
    W["l1f_w2"] = f32(inp["L1_ffn_w2"])
    b1 = np.asarray(inp["L1_ffn_b1"])
    W["l1f_b1cm"] = f32(np.stack([b1[e].reshape(32, 128).T for e in range(8)]))
    W["l1f_b2"] = f32(inp["L1_ffn_b2"])
    W["l1b_qkvT"] = f32(np.asarray(inp["L1_attn_b_qkv_w"]).T)
    W["l1b_qkvb"] = f32(inp["L1_attn_b_qkv_b"]).reshape(1, 3072)
    W["l1b_woT"] = f32(np.asarray(inp["L1_attn_b_out_w"]).T)
    W["l1b_ob"] = f32(inp["L1_attn_b_out_b"]).reshape(1, 1024)
    half = 32
    inv_freq = 1.0 / 10000.0 ** (np.arange(half, dtype=np.float64) / half)
    pos = np.arange(4096, dtype=np.float64)
    ang = np.outer(pos, inv_freq)
    W["l1b_cos"] = np.cos(ang).astype(np.float32)
    W["l1b_sin"] = np.sin(ang).astype(np.float32)
    return W


SUBLAYERS = [emit_l0a, emit_l0ffn, emit_l0b, emit_l0ffn, emit_l1a, emit_l1ffn, emit_l1b, emit_l1ffn]


def build_program(W_shapes, sublayers=None):
    sublayers = sublayers or SUBLAYERS
    nc = bass.Bass("TRN2", target_bir_lowering=False)
    xin = nc.dram_tensor("x_in", [T, D], F32, kind="ExternalInput").ap()
    xout = nc.dram_tensor("x_out", [T, D], F32, kind="ExternalOutput").ap()
    W = {name: nc.dram_tensor(name, list(shape), F32, kind="ExternalInput").ap()
         for name, shape in W_shapes.items()}
    with ExitStack() as es:
        ctx = Ctx(nc, es)
        streams = [xin]
        for i in range(len(sublayers) - 1):
            streams.append(nc.dram_tensor(f"x_mid{i}", [T, D], F32, kind="Internal").ap())
        streams.append(xout)
        emit_consts(ctx)
        for i, emit in enumerate(sublayers):
            emit(ctx, streams[i], streams[i + 1], W)
    return nc


def kernel(**inputs):
    x = np.ascontiguousarray(inputs["L0_attn_a_x"], dtype=np.float32)
    W = _prep_weights(inputs)
    nc = build_program({k: v.shape for k, v in W.items()})
    in_maps = []
    for c in range(8):
        m = {"x_in": np.ascontiguousarray(x[c])}
        m.update(W)
        in_maps.append(m)
    res = run_bass_kernel_spmd(nc, in_maps, core_ids=list(range(8)))
    return np.stack([r["x_out"] for r in res.results], axis=0)
